# version 07
# baseline (speedup 1.0000x reference)
import contextlib
import numpy as np
import ml_dtypes
import concourse.bass as bass
import concourse.mybir as mybir
from concourse.bass_utils import run_bass_kernel_spmd

F32 = mybir.dt.float32
BF16 = mybir.dt.bfloat16
AF = mybir.ActivationFunctionType
ALU = mybir.AluOpType
AX = mybir.AxisListType

TOK = 1024
CS = 128
NCH = 8
HD = 128
PH = 64
GR = 8
NS = 128
INTER = 8192
CONVD = 10240
FT_X = 64
FT_CONV = 80
FT_ALL = 81
DM = 4096
KD = 32
NEG = -1.0e30
BF = ml_dtypes.bfloat16

STAGE = 9
DEBUG = False


class Sem:
    def __init__(self, es, nc, name, serial=True):
        self.h = es.enter_context(nc.semaphore(name))
        self.n = 0
        self.serial = serial

    def inc(self, instr, k=1):
        instr.then_inc(self.h, k)
        self.n += k
        return self.n

    def dma(self, instr):
        return self.inc(instr, 16)


def dslot(eng, sem, out, in_, **kw):
    if sem.serial and sem.n:
        eng.wait_ge(sem.h, sem.n)
    sem.dma(eng.dma_start(out=out, in_=in_, **kw))
    return sem.n


class Ctx:
    pass


class BlkProxy:

    def __init__(self, blk):
        self._blk = blk

    @staticmethod
    def _wrap(f):
        def g(eng):
            eng.drain()
            return f(eng)
        return g

    def vector(self, body):
        return self._blk.vector(self._wrap(body))

    def scalar(self, body):
        return self._blk.scalar(self._wrap(body))

    def tensor(self, body):
        return self._blk.tensor(body)

    def sync(self, body):
        return self._blk.sync(body)

    def gpsimd(self, body):
        return self._blk.gpsimd(body)


def build_nc(stage=STAGE, debug=DEBUG):
    nc = bass.Bass("TRN2", target_bir_lowering=False, debug=False)
    c = Ctx()
    c.nc = nc
    c.stage = stage
    c.debug = debug

    def inp(name, shape, dt=F32):
        return nc.dram_tensor(name, shape, dt, kind="ExternalInput").ap()

    c.x_own = inp("x_own", [TOK, DM])
    c.x_first = inp("x_first", [TOK, DM])
    c.x_second = inp("x_second", [TOK, DM])
    c.flag_col = inp("flag_col", [128, 1])
    c.whbc = inp("whbc", [FT_ALL, 128, KD, 128])
    c.wgateT = inp("wgateT", [DM, INTER])
    c.woutT = inp("woutT", [INTER, DM])
    c.w1t = inp("w1t", [3, KD, 128, KD, 128])
    c.w2 = inp("w2", [3, DM, DM])
    c.b1r = inp("b1r", [1, 3 * DM])
    c.b2r = inp("b2r", [1, 3 * DM])
    c.gwm = inp("gwm", [128, KD, 3])
    c.gwr = inp("gwr", [1, 3])
    c.gbr = inp("gbr", [1, 3])
    c.convw = inp("convw", [128, FT_CONV, 4])
    c.convb = inp("convb", [128, FT_CONV])
    c.dtb = inp("dtb", [128, 1])
    c.alog = inp("alog", [128, 1])
    c.drow = inp("drow", [1, 128])
    c.nwcol = inp("nwcol", [128, FT_X])
    c.ident_b = inp("ident_b", [128, 128], BF16)
    c.ident_f = inp("ident_f", [128, 128])
    c.i64x2_b = inp("i64x2_b", [128, 64], BF16)
    c.ut_f = inp("ut_f", [128, 128])
    c.addmask = inp("addmask", [128, 128], BF16)
    c.ones_rb = inp("ones_rb", [1, 512], BF16)
    c.ones_rf = inp("ones_rf", [1, 512])
    c.ones_cb = inp("ones_cb", [128, 1], BF16)
    c.ones2_b = inp("ones2_b", [2, 128], BF16)
    c.vismask = inp("vismask", [128, 8])
    c.epscol = inp("epscol", [128, 1])

    c.out = nc.dram_tensor("out", [TOK, DM], F32, kind="ExternalOutput").ap()

    c.dbg = {}
    if debug:
        def dbg(name, shape, dt=F32):
            c.dbg[name] = nc.dram_tensor("dbg_" + name, shape, dt, kind="ExternalOutput").ap()
        dbg("stats", [128, 64])
        dbg("xnT", [128, 1024], BF16)
        dbg("dtA", [128, 2048])
        dbg("carry", [128, INTER])
        dbg("hbc0", [128, CONVD], BF16)
        dbg("xtok0", [128, INTER], BF16)
        dbg("scal", [1, 64])
        dbg("wcols", [128, 24])
        dbg("sg0", [128, INTER], BF16)
        dbg("g0", [128, DM])
        dbg("y0", [128, INTER], BF16)
        dbg("z0", [128, FT_X * 128], BF16)
        dbg("cum0", [128, 128])

    def scr(name, shape, dt):
        return nc.dram_tensor(name, shape, dt).ap()

    c.sc_hbc = scr("sc_hbc", [NCH, 128, FT_CONV, 128], BF16)
    c.sc_xtok = scr("sc_xtok", [NCH, 128, FT_X, 128], BF16)
    c.sc_sg = scr("sc_sg", [TOK, INTER], BF16)
    c.sc_g = scr("sc_g", [TOK, DM], F32)
    c.sc_moe = scr("sc_moe", [TOK, DM], F32)
    c.sc_z = scr("sc_z", [NCH, 128, FT_X * 128], BF16)
    c.sc_carry = scr("sc_carry", [128, INTER], F32)
    c.sc_rows = scr("sc_rows", [2, 128, 128], BF16)
    c.sc_srow = scr("sc_srow", [8, 4], F32)
    c.sc_wr = scr("sc_wr", [24, 128], BF16)

    es = contextlib.ExitStack()
    with es:
        sb = lambda name, shape, dt: es.enter_context(nc.sbuf_tensor(name, shape, dt))
        c.sb = sb
        c.k_identb = sb("k_identb", [128, 128], BF16)
        c.k_identf = sb("k_identf", [128, 128], F32)
        c.k_i64x2 = sb("k_i64x2", [128, 64], BF16)
        c.k_utf = sb("k_utf", [128, 128], F32)
        c.k_addm = sb("k_addm", [128, 128], BF16)
        c.k_onesrb = sb("k_onesrb", [1, 512], BF16)
        c.k_onesrf = sb("k_onesrf", [1, 512], F32)
        c.k_onescb = sb("k_onescb", [128, 1], BF16)
        c.k_ones2 = sb("k_ones2", [2, 128], BF16)
        c.k_convw = sb("k_convw", [128, FT_CONV * 4], F32)
        c.k_convb = sb("k_convb", [128, FT_CONV], F32)
        c.k_dtb = sb("k_dtb", [128, 1], F32)
        c.k_acol = sb("k_acol", [128, 1], F32)
        c.k_flag = sb("k_flag", [128, 1], F32)
        c.k_nw = sb("k_nw", [128, FT_X], F32)
        c.k_vism = sb("k_vism", [128, 8], F32)
        c.k_eps = sb("k_eps", [128, 1], F32)
        c.k_gwm = sb("k_gwm", [128, KD * 3], BF16)
        c.k_gwr = sb("k_gwr", [1, 3], F32)
        c.k_gbr = sb("k_gbr", [1, 3], F32)

        c.absA = sb("absA", [128, 8], F32)
        c.absB = sb("absB", [128, 8], F32)
        c.xn0 = sb("xn0", [128, KD], BF16)
        c.ratio = sb("ratio", [1, 1], F32)
        c.i0bc = sb("i0bc", [128, 3], F32)
        c.i1bc = sb("i1bc", [128, 3], F32)
        c.wcols = sb("wcols", [128, 8 * 3], F32)
        c.convtl = sb("convtl", [128, FT_CONV * 3], BF16)
        c.wrow_e = [sb(f"wrow{e}", [1, 8 * 128], BF16) for e in range(3)]

        S = lambda name: Sem(es, nc, name)
        c.mm = S("mm")
        c.ev = S("ev")
        c.sc = S("sc")
        c.sc2 = S("sc2")
        c.fr = S("fr")
        c.fr2 = S("fr2")
        c.fr3 = S("fr3")
        c.pb = S("pb")
        c.dw = S("dw")
        c.dsems = {}

        def DS(name, serial=True):
            if name not in c.dsems:
                c.dsems[name] = Sem(es, nc, name, serial=serial)
            return c.dsems[name]
        c.DS = DS

        c.blk = BlkProxy(es.enter_context(nc.Block()))
        _build_program(c)

    return nc, c


def barrier(c):
    blk, pb = c.blk, c.pb
    snap = [(sem.h, sem.n) for sem in c.dsems.values() if sem.n]

    def syncq(s):
        for h, n in snap:
            s.wait_ge(h, n)
        pb.inc(s.nop(), 1)
    blk.sync(syncq)

    def gq(g):
        pb.inc(g.nop(), 1)
    blk.gpsimd(gq)

    base = pb.n - 2
    for e in ["tensor", "vector", "scalar"]:
        def body(eng):
            eng.drain()
            pb.inc(eng.nop(), 1)
        getattr(blk, e)(body)
    tot = base + 5
    assert pb.n == tot
    for e in ["tensor", "vector", "scalar", "sync", "gpsimd"]:
        def body(eng):
            eng.wait_ge(pb.h, tot)
        getattr(blk, e)(body)


def _build_program(c):
    nc = c.nc
    _p_init(c)
    barrier(c)

    with contextlib.ExitStack() as da:
        dtbuf = da.enter_context(nc.sbuf_tensor("dtA", [128, 2 * TOK], F32))
        with contextlib.ExitStack() as xa:
            xnT = xa.enter_context(nc.sbuf_tensor("xnT_A", [128, KD * TOK], BF16))
            _p_ln(c, c.x_first, xnT, absrow=c.absA, save_xn0=True, tag="lnA")
            barrier(c)
            if c.stage >= 2:
                _p_front(c, xnT, dtbuf, tag="feA", lite=True)
                barrier(c)
        if c.stage < 3:
            _finish(c)
            return

        dtbufB = dtbuf
        if True:
            with contextlib.ExitStack() as xb:
                xnTB = xb.enter_context(nc.sbuf_tensor("xnT_B", [128, KD * TOK], BF16))
                _p_ln(c, c.x_own, xnTB, absrow=None, save_xn0=False, tag="lnB")
                barrier(c)

                def chA_hook(ch):
                    if ch == 0:
                        _p_lnabs(c, c.x_second, c.absB, tag="lnM")
                    if ch == 4 and c.stage >= 4:
                        _p_scalars(c, end_barrier=False)
                _p_chunks(c, dtbuf, variant="A", tag="chA", hook=chA_hook)
                barrier(c)

                if c.stage >= 4:
                    _p_gate(c, xnTB)
                    barrier(c)
                    _p_router(c, xnTB)
                    barrier(c)
                if c.stage >= 5:
                    _p_ffn(c, xnTB)
                if c.stage >= 6:
                    _p_front(c, xnTB, dtbufB, tag="feB")
                    barrier(c)
            if c.stage >= 7:
                _p_chunks(c, dtbufB, variant="B", tag="chB")
                barrier(c)
        if c.stage >= 8:
            _p_outproj(c)
    _finish(c)


def _finish(c):
    blk = c.blk
    if c.stage < 8:
        nc = c.nc
        st = c.DS("st_zero", serial=False)
        with contextlib.ExitStack() as es:
            zb = es.enter_context(nc.sbuf_tensor("zero_out", [128, DM], F32))

            def vbody(v):
                ms = v.memset(zb[:, :], 0.0)
                c.ev.inc(ms, 1)
            blk.vector(vbody)
            evn = c.ev.n

            def sbody(s):
                s.wait_ge(c.ev.h, evn)
                for i in range(8):
                    st.dma(s.dma_start(out=c.out.rearrange("(n p) d -> n p d", p=128)[i], in_=zb[:, :]))
                s.wait_ge(st.h, st.n)
            blk.sync(sbody)
    else:
        snap = [(sem.h, sem.n) for name, sem in c.dsems.items() if name.startswith("st") and sem.n]

        def sbody(s):
            for h, n in snap:
                s.wait_ge(h, n)
        blk.sync(sbody)


def _p_init(c):
    nc, blk = c.nc, c.blk
    ev, sc2, mm = c.ev, c.sc2, c.mm
    sd = c.DS("ini", serial=False)
    wd = c.DS("ini_g", serial=False)

    c.dbc = c.sb("dbc_sb", [128, 128], BF16)
    alog_sb = c.sb("alog_sb", [128, 1], F32)
    drow_sb = c.sb("drow_sb", [1, 128], F32)
    drow_b = c.sb("drow_b", [1, 128], BF16)
    dbc = c.dbc

    def sbody(s):
        for dst, src in [(c.k_identb, c.ident_b), (c.k_identf, c.ident_f), (c.k_i64x2, c.i64x2_b),
                         (c.k_utf, c.ut_f), (c.k_addm, c.addmask), (c.k_onesrb, c.ones_rb),
                         (c.k_onesrf, c.ones_rf), (c.k_onescb, c.ones_cb), (c.k_ones2, c.ones2_b),
                         (c.k_convb, c.convb), (c.k_dtb, c.dtb), (c.k_flag, c.flag_col),
                         (c.k_nw, c.nwcol), (c.k_vism, c.vismask), (c.k_eps, c.epscol), (c.k_gwr, c.gwr), (c.k_gbr, c.gbr),
                         (alog_sb, c.alog), (drow_sb, c.drow)]:
            sd.dma(s.dma_start(out=dst[:, :], in_=src))
        sd.dma(s.dma_start(out=c.k_convw[:, :].rearrange("p (a b) -> p a b", b=4), in_=c.convw))
    blk.sync(sbody)
    sdn = sd.n

    def gbody(g):
        wd.dma(g.dma_start(out=c.k_gwm[:, :].rearrange("p (a b) -> p a b", b=3), in_=c.gwm))
    blk.gpsimd(gbody)

    def scb(se):
        se.wait_ge(sd.h, sdn)
        a1 = se.activation(c.k_acol[:, :], alog_sb[:, :], AF.Exp)
        sc2.inc(a1, 1)
    blk.scalar(scb)
    sc2n = sc2.n

    def vb(v):
        v.wait_ge(sc2.h, sc2n)
        v.tensor_scalar(c.k_acol[:, :], c.k_acol[:, :], -1.0, None, ALU.mult)
        cp = v.tensor_copy(drow_b[:, :], drow_sb[:, :])
        ev.inc(cp, 1)
    blk.vector(vb)
    evn = ev.n

    with contextlib.ExitStack() as pes:
        psd = pes.enter_context(nc.psum_tensor("ps_init", [128, 128], F32))

        def tb(t):
            t.wait_ge(ev.h, evn)
            m1 = t.matmul(psd[:, :], c.k_onesrb[0:1, 0:128], drow_b[0:1, :], start=True, stop=True)
            mm.inc(m1, 1)
        blk.tensor(tb)
        mmn = mm.n

        def vb2(v):
            v.wait_ge(mm.h, mmn)
            cp = v.tensor_copy(dbc[:, :], psd[:, :])
            ev.inc(cp, 1)
        blk.vector(vb2)
        barrier(c)


def _p_ln(c, x_dram, xnT, absrow, save_xn0, tag, end_barrier=True):
    nc, blk = c.nc, c.blk
    mm, ev, sc, sc2, fr, fr2 = c.mm, c.ev, c.sc, c.sc2, c.fr, c.fr2
    lx = [c.DS(f"ln_lx{i}") for i in range(2)]
    sd = c.DS("dbg", serial=False)

    with contextlib.ExitStack() as es:
        xt = [es.enter_context(nc.sbuf_tensor(f"{tag}_x{i}", [128, DM], F32)) for i in range(2)]
        xn_nb = 2 if xnT is not None else 1
        xn_t = [es.enter_context(nc.sbuf_tensor(f"{tag}_xn{i}", [128, DM], BF16)) for i in range(xn_nb)]
        scrA = es.enter_context(nc.sbuf_tensor(f"{tag}_scA", [128, DM], BF16)) if xnT is not None else None
        stats = es.enter_context(nc.sbuf_tensor(f"{tag}_st", [128, 40], F32))
        pst = [es.enter_context(nc.psum_tensor(f"{tag}_pst{i}", [128, 512], BF16)) for i in range(2)] if xnT is not None else None

        xv = x_dram.rearrange("(n p) d -> n p d", p=128)
        fr0 = fr.n
        lxm = {}

        def sbody(s):
            for i in range(8):
                if i >= 2:
                    s.wait_ge(fr.h, fr0 + i - 1)
                lxm[i] = dslot(s, lx[i % 2], xt[i % 2][:, :], xv[i])
        blk.sync(sbody)

        sc0 = sc.n
        ev0 = ev.n
        sc20 = sc2.n

        dw = c.dw
        abs_mark = {}

        abs_lag = 2 if xnT is not None else 1

        def emit_abs(se, j):
            dst = scrA[:, :] if scrA is not None else xn_t[j % xn_nb][:, :]
            ab = se.activation(dst, xn_t[j % xn_nb][:, :], AF.Abs, accum_out=absrow[:, j:j + 1])
            dw.inc(ab, 1)
            abs_mark[j] = dw.n

        def scbody(se):
            for i in range(8):
                se.wait_ge(lx[i % 2].h, lxm[i])
                if i >= 1:
                    se.wait_ge(sc.h, sc0 + 2 * i - 1)
                if absrow is not None and i >= abs_lag:
                    se.wait_ge(dw.h, abs_mark[i - abs_lag])
                d1 = scrA[:, :] if scrA is not None else xt[i % 2][:, :]
                d2 = scrA[:, :] if scrA is not None else xn_t[0][:, :]
                a1 = se.activation(d1, xt[i % 2][:, :], AF.Identity, accum_out=stats[:, i:i + 1])
                dw.inc(a1, 1)
                se.wait_ge(dw.h, dw.n)
                a2 = se.activation(d2, xt[i % 2][:, :], AF.Square, accum_out=stats[:, 8 + i:9 + i])
                a2.then_inc(sc.h, 1)
                se.wait_ge(ev.h, ev0 + 2 * i + 1)
                a3 = se.activation(stats[:, 32 + i:33 + i], stats[:, 24 + i:25 + i], AF.Ln, bias=c.k_eps[:, :])
                a3.then_inc(sc2.h, 1)
                se.wait_ge(sc2.h, sc20 + i + 1)
                a4 = se.activation(stats[:, 32 + i:33 + i], stats[:, 32 + i:33 + i], AF.Exp, scale=-0.5)
                a4.then_inc(sc.h, 1)
                if absrow is not None:
                    if xnT is None:
                        se.wait_ge(ev.h, ev0 + 2 * i + 2)
                        emit_abs(se, i)
                    elif i >= 1:
                        emit_abs(se, i - 1)
            if absrow is not None and xnT is not None:
                se.wait_ge(ev.h, ev0 + 16)
                se.wait_ge(dw.h, dw.n)
                emit_abs(se, 7)
        sc.n += 16
        sc2.n += 8
        blk.scalar(scbody)

        mm0 = mm.n
        fr20 = fr2.n

        def evac(v, it):
            for q in range(8):
                j = 8 * it + q
                v.wait_ge(mm.h, mm0 + j + 1)
                dt0 = q * 4
                dst = xnT[:, :].rearrange("p (a b) -> p a b", b=TOK)[:, dt0:dt0 + 4, it * 128:(it + 1) * 128]
                cp = v.tensor_copy(dst, pst[j % 2][:, :].rearrange("p (a b) -> p a b", b=128))
                fr2.inc(cp, 1)

        def vbody(v):
            for i in range(8):
                v.wait_ge(sc.h, sc0 + 2 * i + 1)
                m1 = v.tensor_scalar(stats[:, 16 + i:17 + i], stats[:, i:i + 1], 1.0 / DM, None, ALU.mult)
                if i >= 1:
                    fr.inc(m1, 1)
                v.tensor_scalar(stats[:, 24 + i:25 + i], stats[:, 8 + i:9 + i], 1.0 / DM, None, ALU.mult)
                v.drain()
                v.scalar_tensor_tensor(stats[:, 24 + i:25 + i], stats[:, 16 + i:17 + i],
                                       stats[:, 16 + i:17 + i], stats[:, 24 + i:25 + i],
                                       ALU.mult, ALU.subtract)
                v.drain()
                ng = v.tensor_scalar(stats[:, 24 + i:25 + i], stats[:, 24 + i:25 + i], -1.0, None, ALU.mult)
                ev.inc(ng, 1)
                v.wait_ge(sc.h, sc0 + 2 * i + 2)
                if xnT is not None and i >= 2:
                    v.wait_ge(mm.h, mm0 + 8 * (i - 1))
                if absrow is not None and i >= abs_lag:
                    v.wait_ge(dw.h, abs_mark[i - abs_lag])
                ts = v.tensor_scalar(xn_t[i % xn_nb][:, :], xt[i % 2][:, :],
                                     stats[:, 16 + i:17 + i], stats[:, 32 + i:33 + i],
                                     ALU.subtract, ALU.mult)
                ev.inc(ts, 1)
                if xnT is not None and i >= 1:
                    evac(v, i - 1)
            if xnT is not None:
                evac(v, 7)
            v.drain()
            fr.inc(v.nop(), 1)
        blk.vector(vbody)
        assert ev.n == ev0 + 16


        if xnT is not None:
            def tbody(t):
                for i in range(8):
                    t.wait_ge(ev.h, ev0 + 2 * i + 2)
                    for q in range(8):
                        j = 8 * i + q
                        if j >= 2:
                            t.wait_ge(fr2.h, fr20 + j - 1)
                        for jj in range(4):
                            dt_i = q * 4 + jj
                            tr = t.transpose(pst[j % 2][:, jj * 128:(jj + 1) * 128],
                                             xn_t[i % xn_nb][:, dt_i * 128:(dt_i + 1) * 128], c.k_identb[:, :])
                        mm.inc(tr, 1)
            blk.tensor(tbody)
            assert mm.n == mm0 + 64

        if save_xn0:
            fr2n = fr2.n

            def v3body(v):
                v.wait_ge(fr2.h, fr2n)
                src = xnT[:, :].rearrange("p (a b) -> p a b", b=TOK)[:, :, 0]
                cp = v.tensor_copy(c.xn0[:, :], src)
                ev.inc(cp, 1)
            blk.vector(v3body)

        if c.debug and tag == "lnA":
            evn, scn, dwn = ev.n, sc.n, c.dw.n

            def dbody(s):
                s.wait_ge(ev.h, evn)
                s.wait_ge(sc.h, scn)
                s.wait_ge(c.dw.h, dwn)
                sd.dma(s.dma_start(out=c.dbg["stats"][:, 0:40], in_=stats[:, :]))
                if absrow is not None:
                    sd.dma(s.dma_start(out=c.dbg["stats"][:, 40:48], in_=absrow[:, :]))
                if xnT is not None:
                    sd.dma(s.dma_start(out=c.dbg["xnT"][:, :], in_=xnT[:, 0:1024]))
            blk.sync(dbody)
        if end_barrier:
            barrier(c)


def _p_lnabs(c, x_dram, absrow, tag):
    nc, blk = c.nc, c.blk
    ev, sc, sc2, fr = c.ev, c.sc, c.sc2, c.fr
    dw = c.dw
    lx = [c.DS(f"ln_lx{i}") for i in range(2)]

    with contextlib.ExitStack() as es:
        xt = [es.enter_context(nc.sbuf_tensor(f"{tag}_x{i}", [128, DM], F32)) for i in range(2)]
        stats = es.enter_context(nc.sbuf_tensor(f"{tag}_st", [128, 40], F32))

        xv = x_dram.rearrange("(n p) d -> n p d", p=128)
        fr0 = fr.n
        lxm = {}

        def sbody(s):
            for i in range(8):
                if i >= 2:
                    s.wait_ge(fr.h, fr0 + i - 1)
                lxm[i] = dslot(s, lx[i % 2], xt[i % 2][:, :], xv[i])
        blk.sync(sbody)

        sc0 = sc.n
        ev0 = ev.n
        sc20 = sc2.n

        def scbody(se):
            for i in range(8):
                se.wait_ge(lx[i % 2].h, lxm[i])
                a1 = se.activation(xt[i % 2][:, :], xt[i % 2][:, :], AF.Identity,
                                   accum_out=stats[:, i:i + 1])
                a1.then_inc(sc.h, 1)
                se.wait_ge(ev.h, ev0 + 2 * i + 1)
                a5 = se.activation(xt[i % 2][:, :], xt[i % 2][:, :], AF.Abs,
                                   bias=stats[:, 16 + i:17 + i],
                                   accum_out=stats[:, 24 + i:25 + i])
                dw.inc(a5, 1)
                se.wait_ge(dw.h, dw.n)
                a6 = se.activation(xt[i % 2][:, :], xt[i % 2][:, :], AF.Square,
                                   accum_out=stats[:, 8 + i:9 + i])
                a6.then_inc(sc.h, 1)
                se.wait_ge(ev.h, ev0 + 2 * i + 2)
                a3 = se.activation(stats[:, 32 + i:33 + i], stats[:, 16 + i:17 + i], AF.Ln,
                                   bias=c.k_eps[:, :])
                fr.inc(a3, 1)
                se.wait_ge(fr.h, fr0 + i + 1)
                a4 = se.activation(stats[:, 32 + i:33 + i], stats[:, 32 + i:33 + i], AF.Exp,
                                   scale=-0.5)
                a4.then_inc(sc2.h, 1)
        sc.n += 16
        sc2.n += 8
        blk.scalar(scbody)

        def vbody(v):
            for i in range(8):
                v.wait_ge(sc.h, sc0 + 2 * i + 1)
                ng = v.tensor_scalar(stats[:, 16 + i:17 + i], stats[:, i:i + 1], -1.0 / DM, None, ALU.mult)
                ev.inc(ng, 1)
                v.wait_ge(sc.h, sc0 + 2 * i + 2)
                vr = v.tensor_scalar(stats[:, 16 + i:17 + i], stats[:, 8 + i:9 + i], 1.0 / DM, None, ALU.mult)
                ev.inc(vr, 1)
            v.wait_ge(sc2.h, sc20 + 8)
            v.drain()
            ab = v.tensor_tensor(absrow[:, :], stats[:, 24:32], stats[:, 32:40], ALU.mult)
            dw.inc(ab, 1)
        blk.vector(vbody)


def _p_front(c, xnT, dtbuf, tag, lite=False):
    nc, blk = c.nc, c.blk
    mm, ev, sc, sc2 = c.mm, c.ev, c.sc, c.sc2
    fr, fr2, fr3 = c.fr, c.fr2, c.fr3
    wsem = [c.DS(f"{tag}_w{i}") for i in range(2)]
    hsem = [c.DS(f"{tag}_h{i}") for i in range(4)]
    xsem = [c.DS(f"{tag}_x{i}") for i in range(2)]
    sd = c.DS("dbg", serial=False)

    fts = [ft for ft in range(FT_ALL) if not (lite and FT_X + GR <= ft < FT_CONV)]
    c_tiles = [ft for ft in range(FT_ALL) if lite and FT_X + GR <= ft < FT_CONV]
    NFT = len(fts)
    NIT = 2 * NFT

    def it_ft(i):
        return fts[i // 2], i % 2

    with contextlib.ExitStack() as es:
        wsb = [es.enter_context(nc.sbuf_tensor(f"{tag}_w{i}", [128, KD * 128], BF16)) for i in range(2)]
        sraw = [es.enter_context(nc.sbuf_tensor(f"{tag}_sr{i}", [128, 515], BF16)) for i in range(2)]
        ssil = [es.enter_context(nc.sbuf_tensor(f"{tag}_ss{i}", [128, 512], BF16)) for i in range(4)]
        cvt = [es.enter_context(nc.sbuf_tensor(f"{tag}_cv{i}", [128, 512], F32)) for i in range(3)]
        stg = [es.enter_context(nc.sbuf_tensor(f"{tag}_sg{i}", [128, 512], BF16)) for i in range(2)]
        sptmp = es.enter_context(nc.sbuf_tensor(f"{tag}_sp", [128, TOK], F32))
        psm = [es.enter_context(nc.psum_tensor(f"{tag}_ps{i}", [128, 512], F32)) for i in range(2)]
        pst = [es.enter_context(nc.psum_tensor(f"{tag}_pt{i}", [128, 512], BF16)) for i in range(2)]

        fr0 = fr.n
        mm0 = mm.n
        sc0 = sc.n
        ev0 = ev.n
        sc20 = sc2.n
        fr20 = fr2.n
        fr30 = fr3.n

        conv_idx = {}
        ci = 0
        for i in range(NIT):
            ft, b = it_ft(i)
            if ft < FT_CONV:
                conv_idx[i] = ci
                ci += 1
        NCONV = ci
        xsect_idx = {}
        xi = 0
        for i in range(NIT):
            ft, b = it_ft(i)
            if ft < FT_X:
                xsect_idx[i] = xi
                xi += 1
        NX = xi

        c._fe_xtok_mark = getattr(c, "_fe_xtok_mark", {})
        xs_base = [xsem[0].n, xsem[1].n]
        for j in range(NIT):
            if j in xsect_idx:
                xj = xsect_idx[j]
                c._fe_xtok_mark[(tag, xj)] = xs_base[xj % 2] + 16 * (xj // 2 + 1)

        wm = {}

        def gbody(g):
            for fi, ft in enumerate(fts):
                if fi >= 2:
                    g.wait_ge(fr.h, fr0 + fi - 1)
                wm[ft] = dslot(g, wsem[fi % 2], wsb[fi % 2][:, :].rearrange("p (a b) -> p a b", b=128),
                               c.whbc[ft], max_dma_last_dim=8192)
        blk.gpsimd(gbody)

        def tbody(t):
            done_tr = []
            for i in range(NIT):
                ft, b = it_ft(i)
                fi = i // 2
                if b == 0:
                    t.wait_ge(wsem[fi % 2].h, wm[ft])
                if i >= 2:
                    t.wait_ge(sc.h, sc0 + i - 1)
                for k in range(KD):
                    last = t.matmul(psm[i % 2][:, :], wsb[fi % 2][:, k * 128:(k + 1) * 128],
                                    xnT[:, :].rearrange("p (a b) -> p a b", b=TOK)[:, k, b * 512:(b + 1) * 512],
                                    start=(k == 0), stop=(k == KD - 1))
                    if k == 0 and b == 0 and fi >= 1:
                        fr.inc(last, 1)
                mm.inc(last, 1)
                j = i - 3
                if j >= 0 and j in xsect_idx:
                    xj = xsect_idx[j]
                    t.wait_ge(sc2.h, sc20 + conv_idx[j] + 1)
                    if xj >= 2:
                        t.wait_ge(fr3.h, fr30 + xj - 1)
                    for q in range(4):
                        tr = t.transpose(pst[xj % 2][:, q * 128:(q + 1) * 128],
                                         ssil[j % 4][:, q * 128:(q + 1) * 128], c.k_identb[:, :])
                    mm.inc(tr, 1)
                    done_tr.append(j)
            for j in range(NIT - 3, NIT):
                if j in xsect_idx and j not in done_tr:
                    xj = xsect_idx[j]
                    t.wait_ge(sc2.h, sc20 + conv_idx[j] + 1)
                    if xj >= 2:
                        t.wait_ge(fr3.h, fr30 + xj - 1)
                    for q in range(4):
                        tr = t.transpose(pst[xj % 2][:, q * 128:(q + 1) * 128],
                                         ssil[j % 4][:, q * 128:(q + 1) * 128], c.k_identb[:, :])
                    mm.inc(tr, 1)
            t.drain()
            fr.inc(t.nop(), 1)
        blk.tensor(tbody)
        assert fr.n == fr0 + NFT
        mm_of_i = {}
        tr_of_j = {}
        cnt = 0
        done_tr2 = []
        for i in range(NIT):
            cnt += 1
            mm_of_i[i] = mm0 + cnt
            j = i - 3
            if j >= 0 and j in xsect_idx:
                cnt += 1
                tr_of_j[j] = mm0 + cnt
                done_tr2.append(j)
        for j in range(NIT - 3, NIT):
            if j in xsect_idx and j not in done_tr2:
                cnt += 1
                tr_of_j[j] = mm0 + cnt
        assert mm.n == mm0 + cnt

        def scbody(se):
            for i in range(NIT):
                ft, b = it_ft(i)
                se.wait_ge(mm.h, mm_of_i[i])
                if ft < FT_CONV:
                    if i >= 2:
                        se.wait_ge(ev.h, ev0 + conv_idx.get(i - 2, 0) + 1)
                    cp = se.copy(sraw[i % 2][:, 3:515], psm[i % 2][:, :])
                    cp.then_inc(sc.h, 1)
                else:
                    a = se.activation(dtbuf[:, b * 512:(b + 1) * 512], psm[i % 2][:, :],
                                      AF.Identity, bias=c.k_dtb[:, :])
                    a.then_inc(sc.h, 1)
                j = i - 2
                if j >= 0 and j in conv_idx:
                    cj = conv_idx[j]
                    se.wait_ge(ev.h, ev0 + cj + 1)
                    if cj >= 4:
                        se.wait_ge(fr2.h, fr20 + cj - 3)
                    ftj = j // 2
                    a = se.activation(ssil[j % 4][:, :], cvt[cj % 3][:, :], AF.Silu,
                                      bias=c.k_convb[:, ftj:ftj + 1])
                    a.then_inc(sc2.h, 1)
            for j in range(NIT - 2, NIT):
                if j in conv_idx:
                    cj = conv_idx[j]
                    se.wait_ge(ev.h, ev0 + cj + 1)
                    if cj >= 4:
                        se.wait_ge(fr2.h, fr20 + cj - 3)
                    ftj = j // 2
                    a = se.activation(ssil[j % 4][:, :], cvt[cj % 3][:, :], AF.Silu,
                                      bias=c.k_convb[:, ftj:ftj + 1])
                    a.then_inc(sc2.h, 1)
        sc.n += NIT
        sc2.n += NCONV
        blk.scalar(scbody)

        tails = c.convtl

        def vbody(v):
            if tag == "feA":
                v.memset(tails[:, :], 0.0)
            else:
                v.tensor_scalar(tails[:, :], tails[:, :], c.k_flag[:, :], None, ALU.mult)
            v.drain()
            for i in range(NIT):
                ft, b = it_ft(i)
                if ft < FT_CONV:
                    cj = conv_idx[i]
                    sbuf = sraw[i % 2]
                    v.tensor_copy(sbuf[:, 0:3], tails[:, ft * 3:ft * 3 + 3])
                    v.wait_ge(sc.h, sc0 + i + 1)
                    v.drain()
                    v.tensor_copy(tails[:, ft * 3:ft * 3 + 3], sbuf[:, 512:515])
                    w = lambda k: c.k_convw[:, (ft * 4 + k):(ft * 4 + k + 1)]
                    cb = cvt[cj % 3]
                    if cj >= 2:
                        v.wait_ge(sc2.h, sc20 + cj - 1)
                    v.tensor_scalar(cb[:, :], sbuf[:, 0:512], w(0), None, ALU.mult)
                    for k in range(1, 4):
                        v.drain()
                        op = v.scalar_tensor_tensor(cb[:, :], sbuf[:, k:512 + k], w(k), cb[:, :],
                                                    ALU.mult, ALU.add)
                    ev.inc(op, 1)
                j = i - 3
                if j >= 0 and j in xsect_idx:
                    xj = xsect_idx[j]
                    v.wait_ge(mm.h, tr_of_j[j])
                    if xj >= 2:
                        v.wait_ge(xsem[xj % 2].h, c._fe_xtok_mark[(tag, xj - 2)])
                    v.drain()
                    cp = v.tensor_copy(stg[xj % 2][:, :], pst[xj % 2][:, :])
                    fr3.inc(cp, 1)
        blk.vector(vbody)
        assert ev.n == ev0 + NCONV

        hm = {}

        def sbody(s):
            for i in range(NIT):
                ft, b = it_ft(i)
                if ft < FT_CONV and (not lite or ft >= FT_X):
                    cj = conv_idx[i]
                    s.wait_ge(sc2.h, sc20 + cj + 1)
                    dst = c.sc_hbc[b * 4:(b + 1) * 4, :, ft, :].rearrange("a f t -> f a t")
                    src = ssil[i % 4][:, :].rearrange("p (a b) -> p a b", b=128)
                    hm[i] = dslot(s, hsem[i % 4], dst, src)
                j = i - 3
                if j >= 0 and j in xsect_idx:
                    xj = xsect_idx[j]
                    s.wait_ge(fr3.h, fr30 + xj + 1)
                    ftj, bj = it_ft(j)
                    dst = c.sc_xtok[bj * 4:(bj + 1) * 4, :, ftj, :].rearrange("q t f -> t q f")
                    src = stg[xj % 2][:, :].rearrange("p (a b) -> p a b", b=128)
                    xmark = dslot(s, xsem[xj % 2], dst, src)
                    assert xmark == c._fe_xtok_mark[(tag, xj)]
                jj = i - 3
                if jj >= 0 and jj in conv_idx:
                    if jj in hm:
                        s.wait_ge(hsem[jj % 4].h, hm[jj])
                    if jj in xsect_idx:
                        s.wait_ge(mm.h, tr_of_j[jj])
                    fr2.inc(s.nop(), 1)
            for j in range(NIT - 3, NIT):
                if j in xsect_idx:
                    xj = xsect_idx[j]
                    s.wait_ge(fr3.h, fr30 + xj + 1)
                    ftj, bj = it_ft(j)
                    dst = c.sc_xtok[bj * 4:(bj + 1) * 4, :, ftj, :].rearrange("q t f -> t q f")
                    src = stg[xj % 2][:, :].rearrange("p (a b) -> p a b", b=128)
                    xmark = dslot(s, xsem[xj % 2], dst, src)
                    assert xmark == c._fe_xtok_mark[(tag, xj)]
                if j in conv_idx:
                    if j in hm:
                        s.wait_ge(hsem[j % 4].h, hm[j])
                    if j in xsect_idx:
                        s.wait_ge(mm.h, tr_of_j[j])
                    fr2.inc(s.nop(), 1)
        blk.sync(sbody)
        assert fr2.n == fr20 + NCONV

        if lite:
            wcs = wsem
            frC0 = fr.n
            scC0 = sc0 + NIT
            wcm = {}

            def gcbody(g):
                for idx, ft in enumerate(c_tiles):
                    g.wait_ge(fr.h, frC0 + (idx - 1 if idx >= 2 else 0))
                    wcm[idx] = dslot(g, wcs[idx % 2], wsb[idx % 2][:, :].rearrange("p (a b) -> p a b", b=128),
                                     c.whbc[ft], max_dma_last_dim=8192)
            blk.gpsimd(gcbody)
            mmC = {}

            def tcbody(t):
                for idx, ft in enumerate(c_tiles):
                    t.wait_ge(wcs[idx % 2].h, wcm[idx])
                    if idx == 0:
                        t.wait_ge(sc.h, sc0 + NIT)
                    if idx >= 2:
                        t.wait_ge(sc.h, scC0 + idx - 1)
                    for k in range(KD):
                        last = t.matmul(psm[idx % 2][:, 0:3], wsb[idx % 2][:, k * 128:(k + 1) * 128],
                                        xnT[:, :].rearrange("p (a b) -> p a b", b=TOK)[:, k, TOK - 3:TOK],
                                        start=(k == 0), stop=(k == KD - 1))
                        if k == 0 and idx >= 1:
                            fr.inc(last, 1)
                    mm.inc(last, 1)
                    mmC[idx] = mm.n
                t.drain()
                fr.inc(t.nop(), 1)
            blk.tensor(tcbody)

            def scbody2(se):
                for idx, ft in enumerate(c_tiles):
                    se.wait_ge(mm.h, mmC[idx])
                    if idx == 0:
                        se.wait_ge(ev.h, ev0 + 1)
                    cp = se.copy(tails[:, ft * 3:ft * 3 + 3], psm[idx % 2][:, 0:3])
                    sc.inc(cp, 1)
            blk.scalar(scbody2)

        scn = sc.n
        sc2b = sc2.n

        def s4body(se):
            se.wait_ge(sc.h, scn)
            dtf = dtbuf[:, 0:1024]
            adt = dtbuf[:, 1024:2048]
            tmp = sptmp[:, :]
            a1 = se.activation(tmp, dtf, AF.Abs)
            a1.then_inc(sc2.h, 1)
            se.wait_ge(sc2.h, sc2b + 1)
            a2 = se.activation(tmp, tmp, AF.Exp, scale=-1.0)
            a2.then_inc(sc2.h, 1)
            se.wait_ge(sc2.h, sc2b + 2)
            a3 = se.activation(adt, tmp, AF.Ln, bias=1.0)
            a3.then_inc(sc.h, 1)
        sc2.n += 2
        sc.n += 1
        blk.scalar(s4body)
        scn2 = sc.n

        def v3body(v):
            v.wait_ge(sc.h, scn2)
            dtf = dtbuf[:, 0:1024]
            adt = dtbuf[:, 1024:2048]
            tmp = sptmp[:, :]
            v.tensor_scalar(tmp, dtf, 0.0, None, ALU.max)
            v.drain()
            v.tensor_tensor(dtf, tmp, adt, ALU.add)
            v.drain()
            ts = v.tensor_scalar(adt, dtf, c.k_acol[:, :], None, ALU.mult)
            ev.inc(ts, 1)
        blk.vector(v3body)

        if c.debug and tag == "feA":
            evn = ev.n
            sdn = sd.n

            def dbody(s):
                s.wait_ge(ev.h, evn)
                s.wait_ge(sd.h, sdn)
                sd.dma(s.dma_start(out=c.dbg["dtA"][:, :], in_=dtbuf[:, 0:2048]))
            blk.sync(dbody)
        barrier(c)


def _p_chunks(c, dtbuf, variant, tag, hook=None):
    nc, blk = c.nc, c.blk
    mm, ev, sc2 = c.mm, c.ev, c.sc2
    isB = variant == "B"
    sem_h2 = c.DS(f"{tag}_hb2")
    sem_x = c.DS(f"{tag}_xt")
    sem_x2 = c.DS(f"{tag}_xt2")
    sem_cr = c.DS(f"{tag}_cr")
    if isB:
        sem_h = c.DS(f"{tag}_hb")
        sem_g, sem_g2 = c.DS(f"{tag}_sg"), c.DS(f"{tag}_sg2")
        sem_rwh, sem_rwl = c.DS(f"{tag}_rwh"), c.DS(f"{tag}_rwl")
        sem_rrh, sem_rrl = c.DS(f"{tag}_rrh"), c.DS(f"{tag}_rrl")
        sem_z = [c.DS(f"{tag}_z{i}") for i in range(2)]
    sd = c.DS("dbg", serial=False)

    with contextlib.ExitStack() as es:
        run = es.enter_context(nc.sbuf_tensor(f"{tag}_run", [128, INTER], F32))
        hbc = es.enter_context(nc.sbuf_tensor(f"{tag}_hbc", [128, CONVD if isB else GR * 128], BF16))
        xtk = es.enter_context(nc.sbuf_tensor(f"{tag}_xtk", [128, INTER], BF16))
        xdd = es.enter_context(nc.sbuf_tensor(f"{tag}_xdd", [128, INTER], BF16))
        btok = es.enter_context(nc.sbuf_tensor(f"{tag}_btok", [128, GR * 128], BF16))
        dts = es.enter_context(nc.sbuf_tensor(f"{tag}_dts", [128, 9 * 128], F32))
        psmisc = es.enter_context(nc.psum_tensor(f"{tag}_pc", [128, 512], F32))
        psSt = es.enter_context(nc.psum_tensor(f"{tag}_pq", [128, 512], F32))
        psM = [es.enter_context(nc.psum_tensor(f"{tag}_pm{i}", [128, 512], F32)) for i in range(2)]
        psY1 = [es.enter_context(nc.psum_tensor(f"{tag}_py{i}", [128, 512], F32)) for i in range(2)]
        L = dict(run=run, hbc=hbc, xtk=xtk, xdd=xdd, btok=btok, dts=dts,
                 psmisc=psmisc, psSt=psSt, psM=psM, psY1=psY1, dtbuf=dtbuf)
        if isB:
            L["psY2"] = [es.enter_context(nc.psum_tensor(f"{tag}_pz{i}", [128, 512], F32)) for i in range(2)]
            L["bias_hi"] = es.enter_context(nc.sbuf_tensor(f"{tag}_bh", [128, 128], BF16))
            L["bias_lo"] = es.enter_context(nc.sbuf_tensor(f"{tag}_bl", [128, 128], BF16))
            L["hi_act"] = es.enter_context(nc.sbuf_tensor(f"{tag}_ha", [128, 128], BF16))
            L["lo_act"] = es.enter_context(nc.sbuf_tensor(f"{tag}_la", [128, 128], BF16))
            L["rows"] = [es.enter_context(nc.sbuf_tensor(f"{tag}_rw{i}", [2, 4096], BF16)) for i in range(2)]
            L["gm"] = es.enter_context(nc.sbuf_tensor(f"{tag}_gm", [128, GR * 128], BF16))
            L["mexp"] = [es.enter_context(nc.sbuf_tensor(f"{tag}_me{i}", [128, 512], F32)) for i in range(2)]
            L["mbuf"] = es.enter_context(nc.sbuf_tensor(f"{tag}_mb", [128, 4 * 512], BF16))
            L["stbf"] = es.enter_context(nc.sbuf_tensor(f"{tag}_sb", [128, INTER], BF16))
            L["sg"] = es.enter_context(nc.sbuf_tensor(f"{tag}_sgc", [128, INTER], BF16))
            L["ytok"] = es.enter_context(nc.sbuf_tensor(f"{tag}_y", [128, INTER], BF16))
            L["zstg"] = [es.enter_context(nc.sbuf_tensor(f"{tag}_zs{i}", [128, 512], BF16)) for i in range(2)]
            L["sumsq"] = es.enter_context(nc.sbuf_tensor(f"{tag}_sq", [128, 1], F32))
            L["rstd"] = es.enter_context(nc.sbuf_tensor(f"{tag}_rs", [128, 1], F32))
            L["dident"] = es.enter_context(nc.sbuf_tensor(f"{tag}_di", [128, HD * PH], BF16))

        c._marks = getattr(c, "_marks", {})
        mk = c._marks

        if isB:
            def ldc(s):
                dslot(s, sem_cr, run[:, :], c.sc_carry)
            blk.sync(ldc)
            mk[(tag, "init")] = sem_cr.n

            def vdid(v):
                in1 = c.k_i64x2[:, :].unsqueeze(1).broadcast_to([128, 128, 64])
                in2 = c.dbc[:, :].unsqueeze(2).broadcast_to([128, 128, 64])
                tt = v.tensor_tensor(L["dident"][:, :].rearrange("p (h q) -> p h q", q=64), in1, in2, ALU.mult)
                ev.inc(tt, 1)
            blk.vector(vdid)
            mk[(tag, "didev")] = ev.n
        else:
            def vinit(v):
                ms = v.memset(run[:, :], 0.0)
                ev.inc(ms, 1)
            blk.vector(vinit)
            mk[(tag, "initev")] = ev.n

        for ch in range(NCH):
            _chunk_one(c, L, ch, variant, tag)
            if hook is not None:
                hook(ch)
        if isB:
            _chunk_ztail(c, L, NCH - 1, tag)

        if not isB:
            prev_end = mk[(tag, "headev", NCH - 1)]

            def carrybody(v):
                v.wait_ge(ev.h, prev_end)
                ts = v.tensor_scalar(run[:, :], run[:, :], c.k_flag[:, :], None, ALU.mult)
                ev.inc(ts, 1)
            blk.vector(carrybody)
            evn = ev.n

            def spill(s):
                s.wait_ge(ev.h, evn)
                m = dslot(s, sem_cr, c.sc_carry, run[:, :])
                s.wait_ge(sem_cr.h, m)
            blk.sync(spill)
            if c.debug:
                def dbody(s):
                    sd.dma(s.dma_start(out=c.dbg["carry"], in_=run[:, :]))
                    sd.dma(s.dma_start(out=c.dbg["hbc0"][:, FT_X * 128:(FT_X + GR) * 128],
                                       in_=hbc[:, 0:GR * 128]))
                    sd.dma(s.dma_start(out=c.dbg["xtok0"][:, :], in_=xtk[:, :]))
                blk.sync(dbody)
        barrier(c)


def _chunk_ztail(c, L, ch, tag):
    nc, blk = c.nc, c.blk
    mm, ev = c.mm, c.ev
    sem_z = [c.DS(f"{tag}_z{i}") for i in range(2)]
    psY1 = L["psY1"]
    mk = c._marks
    ev_z = mk[(tag, "evz", ch)]

    mmZ0 = mm.n
    evZ0 = ev.n
    zm = {q: sem_z[q % 2].n + 16 * (q // 2 + 1) for q in range(16)}

    def t6body(t):
        t.wait_ge(ev.h, ev_z)
        for q in range(FT_X // 4):
            if q >= 2:
                t.wait_ge(ev.h, evZ0 + q - 1)
            for j in range(4):
                ft = q * 4 + j
                tr = t.transpose(psY1[q % 2][:, j * 64:(j + 1) * 64].bitcast(BF16),
                                 L["ytok"][:, ft * 128:(ft + 1) * 128], c.k_identb[:, :])
            mm.inc(tr, 1)
    blk.tensor(t6body)

    def v9body(v):
        for q in range(FT_X // 4):
            v.wait_ge(mm.h, mmZ0 + q + 1)
            if q < 2 and ch >= 1:
                v.wait_ge(sem_z[q % 2].h, mk[(tag, "zspill%d" % (q % 2), ch - 1)])
            if q >= 2:
                v.wait_ge(sem_z[q % 2].h, zm[q - 2])
            for j in range(4):
                ft = q * 4 + j
                ts = v.tensor_scalar(L["zstg"][q % 2][:, j * 128:(j + 1) * 128],
                                     psY1[q % 2][:, j * 64:(j + 1) * 64].bitcast(BF16),
                                     c.k_nw[:, ft:ft + 1], None, ALU.mult)
            ev.inc(ts, 1)
    blk.vector(v9body)
    mk[(tag, "zstg", ch)] = ev.n

    def s7body(s):
        for q in range(FT_X // 4):
            s.wait_ge(ev.h, evZ0 + q + 1)
            m = dslot(s, sem_z[q % 2], c.sc_z[ch][:, q * 512:(q + 1) * 512], L["zstg"][q % 2][:, :])
            assert m == zm[q]
    blk.sync(s7body)
    mk[(tag, "zspill0", ch)] = zm[14]
    mk[(tag, "zspill1", ch)] = zm[15]


def _chunk_one(c, L, ch, variant, tag):
    nc, blk = c.nc, c.blk
    mm, ev, sc2 = c.mm, c.ev, c.sc2
    isB = variant == "B"
    sem_h2, sem_x, sem_x2 = c.DS(f"{tag}_hb2"), c.DS(f"{tag}_xt"), c.DS(f"{tag}_xt2")
    sem_cr = c.DS(f"{tag}_cr")
    if isB:
        sem_h = c.DS(f"{tag}_hb")
        sem_g, sem_g2 = c.DS(f"{tag}_sg"), c.DS(f"{tag}_sg2")
        sem_rwh, sem_rwl = c.DS(f"{tag}_rwh"), c.DS(f"{tag}_rwl")
        sem_rrh, sem_rrl = c.DS(f"{tag}_rrh"), c.DS(f"{tag}_rrl")
        sem_z = [c.DS(f"{tag}_z{i}") for i in range(2)]
    sd = c.DS("dbg", serial=False)
    run, hbc, xtk, xdd, btok, dts = L["run"], L["hbc"], L["xtk"], L["xdd"], L["btok"], L["dts"]
    psmisc, psSt, psM, psY1 = L["psmisc"], L["psSt"], L["psM"], L["psY1"]
    dtbuf = L["dtbuf"]
    col = lambda i: dts[:, i * 128:(i + 1) * 128]
    ADT, CUMT, DTT, CUMA, SB, EXPS, D1, DDT, LNDT = range(9)
    EXPC, BIAS, SCR = ADT, LNDT, D1
    rb = ch % 2
    mk = c._marks

    rl = {}

    def sbody(s):
        if ch >= 1:
            s.wait_ge(mm.h, mk[(tag, "headmm", ch - 1)])
            s.wait_ge(ev.h, mk[(tag, "xddev", ch - 1)])
        if isB:
            rl["h"] = dslot(s, sem_h, hbc[:, 0:40 * 128].rearrange("p (a b) -> p a b", b=128),
                            c.sc_hbc[ch][:, 0:40, :])
            rl["h2"] = dslot(s, sem_h2, hbc[:, 40 * 128:].rearrange("p (a b) -> p a b", b=128),
                             c.sc_hbc[ch][:, 40:, :])
        else:
            rl["h2"] = dslot(s, sem_h2, hbc[:, :].rearrange("p (a b) -> p a b", b=128),
                             c.sc_hbc[ch][:, FT_X:FT_X + GR, :])
            rl["h"] = rl["h2"]
        rl["x"] = dslot(s, sem_x, xtk[:, 0:4096].rearrange("p (a b) -> p a b", b=128),
                        c.sc_xtok[ch][:, 0:32, :])
        rl["x2"] = dslot(s, sem_x2, xtk[:, 4096:].rearrange("p (a b) -> p a b", b=128),
                         c.sc_xtok[ch][:, 32:, :])
        if isB:
            if ch >= 1:
                s.wait_ge(ev.h, mk[(tag, "zdone", ch - 1)])
            rl["g"] = dslot(s, sem_g, L["sg"][:, 0:4096], c.sc_sg[ch * 128:(ch + 1) * 128, 0:4096])
            rl["g2"] = dslot(s, sem_g2, L["sg"][:, 4096:], c.sc_sg[ch * 128:(ch + 1) * 128, 4096:])
    blk.sync(sbody)

    mm0 = mm.n
    ev0 = ev.n

    def tbody(t):
        if ch >= 1:
            t.wait_ge(ev.h, mk[(tag, "headev", ch - 1)])
        elif isB:
            t.wait_ge(sem_cr.h, mk[(tag, "init")])
        t1 = t.transpose(psmisc[:, 0:128], dtbuf[:, 1024 + ch * 128:1024 + (ch + 1) * 128], c.k_identf[:, :])
        mm.inc(t1, 1)
        t.wait_ge(ev.h, ev0 + 1)
        t2 = t.matmul(psmisc[:, 128:256], c.k_utf[:, :], col(ADT), start=True, stop=True)
        mm.inc(t2, 1)
        t3 = t.transpose(psmisc[:, 256:384], dtbuf[:, ch * 128:(ch + 1) * 128], c.k_identf[:, :])
        mm.inc(t3, 1)
        t.wait_ge(ev.h, ev0 + 2)
        t4 = t.transpose(psmisc[:, 384:512], col(CUMT), c.k_identf[:, :])
        mm.inc(t4, 1)
        t.wait_ge(ev.h, ev0 + 3)
        t5 = t.transpose(psmisc[0:1, 0:128], col(CUMA)[:, 127:128], c.k_identf[:, :])
        mm.inc(t5, 1)
        t.wait_ge(ev.h, ev0 + 4)
        t6 = t.matmul(psmisc[:, 128:256], c.k_onesrf[0:1, 0:128], col(SB)[0:1, 0:128], start=True, stop=True)
        mm.inc(t6, 1)
    blk.tensor(tbody)

    def vbody(v):
        v.wait_ge(mm.h, mm0 + 1)
        cp = v.tensor_copy(col(ADT), psmisc[:, 0:128])
        ev.inc(cp, 1)
        v.wait_ge(mm.h, mm0 + 3)
        v.tensor_copy(col(CUMT), psmisc[:, 128:256])
        cp = v.tensor_copy(col(DTT), psmisc[:, 256:384])
        ev.inc(cp, 1)
        v.wait_ge(mm.h, mm0 + 4)
        cp = v.tensor_copy(col(CUMA), psmisc[:, 384:512])
        ev.inc(cp, 1)
        v.wait_ge(mm.h, mm0 + 5)
        cp = v.tensor_copy(col(SB)[0:1, 0:128], psmisc[0:1, 0:128])
        ev.inc(cp, 1)
        v.wait_ge(mm.h, mm0 + 6)
        v.tensor_copy(col(SB), psmisc[:, 128:256])
        v.drain()
        tt = v.tensor_tensor(col(D1), col(SB), col(CUMT), ALU.subtract)
        ev.inc(tt, 1)
    blk.vector(vbody)
    evA = ev.n

    sc20 = sc2.n

    def scbody(se):
        se.wait_ge(ev.h, evA)
        a1 = se.activation(col(EXPS), col(SB), AF.Exp)
        a1.then_inc(sc2.h, 1)
        a2 = se.activation(col(D1), col(D1), AF.Exp)
        a2.then_inc(sc2.h, 1)
        if isB:
            a3 = se.activation(col(EXPC), col(CUMT), AF.Exp)
            a3.then_inc(sc2.h, 1)
            a4 = se.activation(col(LNDT), col(DTT), AF.Ln)
            a4.then_inc(sc2.h, 1)
    sc2.n += 4 if isB else 2
    blk.scalar(scbody)
    sc2A = sc2.n

    def v2body(v):
        v.wait_ge(sc2.h, sc2A)
        if isB and ch >= 1:
            v.wait_ge(mm.h, mk[(tag, "Mdone", ch - 1)])
        tt = v.tensor_tensor(col(DDT), col(D1), col(DTT), ALU.mult)
        ev.inc(tt, 1)
        if isB:
            v.tensor_tensor(col(BIAS), col(LNDT), col(CUMT), ALU.subtract)
            v.drain()
            v.tensor_copy(L["bias_hi"][:, :], col(BIAS))
            v.drain()
            v.tensor_tensor(col(SCR), col(BIAS), L["bias_hi"][:, :], ALU.subtract)
            v.drain()
            v.tensor_copy(L["bias_lo"][:, :], col(SCR))
            v.tensor_copy(L["hi_act"][:, :], col(CUMA))
            v.drain()
            v.tensor_tensor(col(SCR), col(CUMA), L["hi_act"][:, :], ALU.subtract)
            v.drain()
            cp = v.tensor_copy(L["lo_act"][:, :], col(SCR))
            ev.inc(cp, 1)
    blk.vector(v2body)
    mk[(tag, "dtev", ch)] = ev.n
    ev_dt = ev.n

    if isB:
        def rbody(s):
            s.wait_ge(ev.h, ev_dt)
            if ch >= 1:
                s.wait_ge(mm.h, mk[(tag, "Mdone", ch - 1)])
            mk[(tag, "rwh", ch)] = dslot(s, sem_rwh, c.sc_rows[0], L["hi_act"][:, :])
            mk[(tag, "rwl", ch)] = dslot(s, sem_rwl, c.sc_rows[1], L["lo_act"][:, :])
        blk.sync(rbody)

    mmB0 = mm.n

    def t2body(t):
        t.wait_ge(sem_h2.h, rl["h2"])
        if ch >= 1:
            t.wait_ge(ev.h, mk[(tag, "headev", ch - 1)])
            t.wait_ge(ev.h, mk[(tag, "bpdone", ch - 1)])
        hoff = FT_X if isB else 0
        for g in range(GR):
            tr = t.transpose(psSt[:, g * 64:(g + 1) * 64].bitcast(BF16),
                             hbc[:, (hoff + g) * 128:(hoff + g + 1) * 128], c.k_identb[:, :])
        mm.inc(tr, 1)
    blk.tensor(t2body)

    def v3body(v):
        v.wait_ge(mm.h, mmB0 + 1)
        cp = v.tensor_copy(btok[:, :], psSt[:, :].bitcast(BF16))
        ev.inc(cp, 1)
        v.wait_ge(ev.h, ev_dt)
        v.wait_ge(sem_x.h, rl["x"])
        v.wait_ge(sem_x2.h, rl["x2"])
        if ch >= 1:
            v.wait_ge(mm.h, mk[(tag, "headmm", ch - 1)])
        in1 = xtk[:, :].rearrange("p (a b) -> p a b", b=64)
        in2 = col(DDT).unsqueeze(2).broadcast_to([128, 128, 64])
        tt = v.tensor_tensor(xdd[:, :].rearrange("p (a b) -> p a b", b=64), in1, in2, ALU.mult)
        ev.inc(tt, 1)
    blk.vector(v3body)
    mk[(tag, "bpdone", ch)] = ev.n
    mk[(tag, "xddev", ch)] = ev.n
    ev_bp = ev.n

    if isB:
        def v4body(v):
            if ch == 0:
                v.wait_ge(sem_cr.h, mk[(tag, "init")])
            cp = v.tensor_copy(L["stbf"][:, :], run[:, :])
            ev.inc(cp, 1)
        blk.vector(v4body)
        ev_stbf = ev.n

        mmG0 = mm.n
        evG0 = ev.n

        def t3body(t):
            t.wait_ge(sem_h2.h, rl["h2"])
            if ch >= 1:
                t.wait_ge(sc2.h, mk[(tag, "mexpsc", ch - 1)])
            for g in range(GR):
                if g >= 2:
                    t.wait_ge(ev.h, evG0 + g - 1)
                mmi = t.matmul(psM[g % 2][:, 0:128], hbc[:, (FT_X + g) * 128:(FT_X + g + 1) * 128],
                               hbc[:, (FT_X + GR + g) * 128:(FT_X + GR + g + 1) * 128],
                               start=True, stop=True)
                mm.inc(mmi, 1)
        blk.tensor(t3body)

        def v5body(v):
            for g in range(GR):
                v.wait_ge(mm.h, mmG0 + g + 1)
                cp = v.tensor_copy(L["gm"][:, g * 128:(g + 1) * 128], psM[g % 2][:, 0:128])
                ev.inc(cp, 1)
        blk.vector(v5body)
        ev_gm = ev.n


    if isB and ch >= 1:
        _chunk_ztail(c, L, ch - 1, tag)

    nsg = HD // 8
    piece_mm = {}
    group_mm = {}
    mmH0 = mm.n
    sc2P0 = sc2.n
    evP0 = None

    ev_sched = {}
    evc = ev.n
    if isB:
        for sgp in range(nsg):
            for p in (2 * sgp, 2 * sgp + 1):
                evc += 1
                ev_sched[("piece", p)] = evc
            evc += 1
            ev_sched[("grp", sgp)] = evc
    else:
        for sgp in range(nsg):
            evc += 1
            ev_sched[("grp", sgp)] = evc

    if isB:
        rows_mark = {q: (sem_rrh.n + 16 * (q + 1), sem_rrl.n + 16 * (q + 1)) for q in range(4)}

    def t5body(t):
        t.wait_ge(ev.h, ev_bp)
        if isB:
            t.wait_ge(sem_h.h, rl["h"])
            t.wait_ge(sem_h2.h, rl["h2"])
            t.wait_ge(ev.h, mk[(tag, "didev")])
            t.wait_ge(ev.h, ev_gm)
            t.wait_ge(ev.h, ev_stbf)
            if ch >= 1:
                t.wait_ge(ev.h, mk[(tag, "zstg", ch - 1)])
        def emit_piece(p):
            if p % 8 == 0:
                t.wait_ge(sem_rrh.h, rows_mark[p // 8][0])
                t.wait_ge(sem_rrl.h, rows_mark[p // 8][1])
            if p >= 2:
                t.wait_ge(sc2.h, sc2P0 + p - 1)
            h0loc = (p % 8) * 4
            pm = psM[p % 2]
            rw = L["rows"][(p // 8) % 2]
            t.matmul(pm[:, :], c.k_ones2[:, :], rw[0:2, h0loc * 128:(h0loc + 4) * 128],
                     start=True, stop=False)
            t.matmul(pm[:, :], c.k_identb[:, :],
                     c.k_addm[:, :].unsqueeze(1).broadcast_to([128, 4, 128]), start=False, stop=False)
            h0 = p * 4
            t.matmul(pm[:, :], c.k_identb[:, :],
                     L["bias_hi"][:, h0:h0 + 4].unsqueeze(2).broadcast_to([128, 4, 128]),
                     start=False, stop=False)
            last = t.matmul(pm[:, :], c.k_identb[:, :],
                            L["bias_lo"][:, h0:h0 + 4].unsqueeze(2).broadcast_to([128, 4, 128]),
                            start=False, stop=True)
            mm.inc(last, 1)
            piece_mm[p] = mm.n

        if isB:
            emit_piece(0)
            emit_piece(1)
        for sgp in range(nsg):
            if isB and sgp + 1 < nsg:
                emit_piece(2 * sgp + 2)
                emit_piece(2 * sgp + 3)
            stb = psSt if sgp % 2 == 0 else psmisc
            y1b = psY1[sgp % 2]
            y2b = L["psY2"][sgp % 2] if isB else None
            if isB:
                t.wait_ge(ev.h, ev_sched[("piece", 2 * sgp + 1)])
            if sgp >= 2:
                t.wait_ge(ev.h, ev_sched[("grp", sgp - 2)])
            g = sgp // 2
            last = t.matmul(stb[:, :], btok[:, g * 128:(g + 1) * 128],
                            xdd[:, sgp * 512:(sgp + 1) * 512], start=True, stop=True)
            if isB:
                for j in range(8):
                    h = sgp * 8 + j
                    stt = (j == 0)
                    base = (h % 2) * 64
                    mb = L["mbuf"][:, (2 * sgp % 4) * 512 + j * 128: (2 * sgp % 4) * 512 + (j + 1) * 128] \
                        if j < 4 else \
                        L["mbuf"][:, ((2 * sgp + 1) % 4) * 512 + (j - 4) * 128: ((2 * sgp + 1) % 4) * 512 + (j - 3) * 128]
                    t.matmul(y1b[:, j * 64:(j + 1) * 64], mb,
                             xtk[:, h * 64:(h + 1) * 64], start=stt, stop=False)
                    t.matmul(y1b[:, j * 64:(j + 1) * 64],
                             hbc[base:base + 64, (h // 2) * 128:(h // 2) * 128 + 128],
                             L["dident"][base:base + 64, h * 64:(h + 1) * 64],
                             start=False, stop=(j == 7))
                last = t.matmul(y2b[:, :],
                                hbc[:, (FT_X + GR + g) * 128:(FT_X + GR + g + 1) * 128],
                                L["stbf"][:, sgp * 512:(sgp + 1) * 512], start=True, stop=True)
            mm.inc(last, 1)
            group_mm[sgp] = mm.n
    blk.tensor(t5body)
    mk[(tag, "headmm", ch)] = mm.n

    if isB:
        def s5body(se):
            for p in range(KD):
                se.wait_ge(mm.h, piece_mm[p])
                if p >= 2:
                    se.wait_ge(ev.h, ev_sched[("piece", p - 2)])
                a = se.activation(L["mexp"][p % 2][:, :], psM[p % 2][:, :], AF.Exp)
                a.then_inc(sc2.h, 1)
        sc2.n += KD
        blk.scalar(s5body)
        mk[(tag, "mexpsc", ch)] = sc2.n

    def v7body(v):
        for sgp in range(nsg):
            if isB:
                for p in (2 * sgp, 2 * sgp + 1):
                    v.wait_ge(sc2.h, sc2P0 + p + 1)
                    if sgp >= 2:
                        v.wait_ge(mm.h, group_mm[sgp - 2])
                    g4 = p * 4 // 16
                    in2 = L["gm"][:, g4 * 128:(g4 + 1) * 128].unsqueeze(1).broadcast_to([128, 4, 128])
                    tt = v.tensor_tensor(L["mbuf"][:, (p % 4) * 512:(p % 4 + 1) * 512].rearrange("p (a b) -> p a b", b=128),
                                         L["mexp"][p % 2][:, :].rearrange("p (a b) -> p a b", b=128), in2, ALU.mult)
                    ev.inc(tt, 1)
                    assert ev.n == ev_sched[("piece", p)]
            stb = psSt if sgp % 2 == 0 else psmisc
            y1b = psY1[sgp % 2]
            y2b = L["psY2"][sgp % 2] if isB else None
            v.wait_ge(mm.h, group_mm[sgp])
            h0 = sgp * 8
            if isB:
                in2 = col(EXPC)[:, h0:h0 + 8].unsqueeze(2).broadcast_to([128, 8, 64])
                v.tensor_tensor(L["ytok"][:, h0 * 64:(h0 + 8) * 64].rearrange("p (a b) -> p a b", b=64),
                                y2b[:, :].rearrange("p (a b) -> p a b", b=64), in2, ALU.mult)
                v.drain()
                v.tensor_tensor(L["ytok"][:, h0 * 64:(h0 + 8) * 64],
                                L["ytok"][:, h0 * 64:(h0 + 8) * 64], y1b[:, :], ALU.add)
            in2b = col(EXPS)[:, h0:h0 + 8].unsqueeze(2).broadcast_to([128, 8, 64])
            v.drain()
            v.tensor_tensor(run[:, h0 * 64:(h0 + 8) * 64].rearrange("p (a b) -> p a b", b=64),
                            run[:, h0 * 64:(h0 + 8) * 64].rearrange("p (a b) -> p a b", b=64), in2b, ALU.mult)
            v.drain()
            tt = v.tensor_tensor(run[:, h0 * 64:(h0 + 8) * 64], run[:, h0 * 64:(h0 + 8) * 64], stb[:, :], ALU.add)
            ev.inc(tt, 1)
            assert ev.n == ev_sched[("grp", sgp)]
    blk.vector(v7body)
    mk[(tag, "headev", ch)] = ev.n
    if isB:
        mk[(tag, "Mdone", ch)] = mk[(tag, "headmm", ch)]
    ev_heads = ev.n

    if isB:
        def rqbody(s):
            for q in range(4):
                if q < 2:
                    s.wait_ge(sem_rwh.h, mk[(tag, "rwh", ch)])
                    s.wait_ge(sem_rwl.h, mk[(tag, "rwl", ch)])
                    if ch >= 1:
                        s.wait_ge(mm.h, mk[(tag, "Mdone", ch - 1)])
                else:
                    s.wait_ge(mm.h, piece_mm[8 * (q - 2) + 7])
                rb_ = L["rows"][q % 2]
                mh = dslot(s, sem_rrh, rb_[0:1, 0:4096],
                           c.sc_rows[0][32 * q:32 * q + 32, :].rearrange("a b -> (a b)").unsqueeze(0))
                ml = dslot(s, sem_rrl, rb_[1:2, 0:4096],
                           c.sc_rows[1][32 * q:32 * q + 32, :].rearrange("a b -> (a b)").unsqueeze(0))
                assert (mh, ml) == rows_mark[q]
        blk.sync(rqbody)

    if isB:
        sc2F0 = sc2.n

        def s6body(se):
            se.wait_ge(ev.h, ev_heads)
            a1 = se.activation(L["stbf"][:, :], L["ytok"][:, :], AF.Square, accum_out=L["sumsq"][:, :])
            a1.then_inc(sc2.h, 1)
            se.wait_ge(sc2.h, sc2F0 + 1)
            a2 = se.activation(L["rstd"][:, :], L["sumsq"][:, :], AF.Ln, scale=1.0 / INTER, bias=c.k_eps[:, :])
            a2.then_inc(sc2.h, 1)
            se.wait_ge(sc2.h, sc2F0 + 2)
            a3 = se.activation(L["rstd"][:, :], L["rstd"][:, :], AF.Exp, scale=-0.5)
            a3.then_inc(sc2.h, 1)
        sc2.n += 3
        blk.scalar(s6body)
        sc2F = sc2.n

        def v8body(v):
            v.wait_ge(sc2.h, sc2F)
            v.wait_ge(sem_g.h, rl["g"])
            v.wait_ge(sem_g2.h, rl["g2"])
            v.tensor_scalar(L["ytok"][:, :], L["ytok"][:, :], L["rstd"][:, :], None, ALU.mult)
            v.drain()
            tt = v.tensor_tensor(L["ytok"][:, :], L["ytok"][:, :], L["sg"][:, :], ALU.mult)
            ev.inc(tt, 1)
        blk.vector(v8body)
        mk[(tag, "zdone", ch)] = ev.n
        mk[(tag, "evz", ch)] = ev.n

        if c.debug and ch == 0:
            evn = ev.n

            def dbody(s):
                s.wait_ge(ev.h, evn)
                sd.dma(s.dma_start(out=c.dbg["y0"][:, :], in_=L["ytok"][:, :]))
                sd.dma(s.dma_start(out=c.dbg["cum0"][:, :], in_=col(CUMT)))
                sd.dma(s.dma_start(out=c.dbg["sg0"][:, :], in_=L["sg"][:, :]))
                s.wait_ge(sd.h, sd.n)
            blk.sync(dbody)


def _p_scalars(c, end_barrier=True):
    nc, blk = c.nc, c.blk
    mm, ev, sc2 = c.mm, c.ev, c.sc2
    bw = c.DS("psc_bw")
    br = c.DS("psc_br")
    sd = c.DS("dbg", serial=False)

    with contextlib.ExitStack() as es:
        tmpb = es.enter_context(nc.sbuf_tensor("psc_tb", [128, 24], BF16))
        tmpf = es.enter_context(nc.sbuf_tensor("psc_tf", [128, 16], F32))
        srow = es.enter_context(nc.sbuf_tensor("psc_sr", [1, 32], F32))
        lg = es.enter_context(nc.sbuf_tensor("psc_lg", [1, 32], F32))
        t2 = es.enter_context(nc.sbuf_tensor("psc_t2", [8, 4], F32))
        ps = es.enter_context(nc.psum_tensor("psc_ps", [128, 32], F32))

        ev0 = ev.n

        dwn = c.dw.n

        def vbody(v):
            v.wait_ge(c.dw.h, dwn)
            v.memset(lg[0:1, :], 0.0)
            v.memset(srow[0:1, :], 0.0)
            v.tensor_tensor(tmpf[:, 0:8], c.absA[:, :], c.k_vism[:, :], ALU.mult)
            v.drain()
            v.tensor_tensor(tmpf[:, 8:16], c.absA[:, :], tmpf[:, 0:8], ALU.subtract)
            v.tensor_copy(tmpb[:, 0:8], tmpf[:, 0:8])
            v.drain()
            v.tensor_copy(tmpb[:, 8:16], tmpf[:, 8:16])
            cp = v.tensor_copy(tmpb[:, 16:24], c.absB[:, :])
            ev.inc(cp, 1)
        blk.vector(vbody)
        evv = ev.n
        mm0 = mm.n

        def tbody(t):
            t.wait_ge(ev.h, evv)
            for k in range(KD):
                mx = t.matmul(ps[0:1, 4:7], c.xn0[:, k:k + 1], c.k_gwm[:, k * 3:(k + 1) * 3],
                              start=(k == 0), stop=(k == KD - 1))
            mm.inc(mx, 1)
            m1 = t.matmul(ps[0:8, 0:1], tmpb[:, 0:8], c.k_onescb[:, :], start=True, stop=True,
                          skip_group_check=True)
            mm.inc(m1, 1)
            m2 = t.matmul(ps[0:8, 1:2], tmpb[:, 8:16], c.k_onescb[:, :], start=True, stop=True,
                          skip_group_check=True)
            mm.inc(m2, 1)
            m3 = t.matmul(ps[0:8, 2:3], tmpb[:, 16:24], c.k_onescb[:, :], start=True, stop=True,
                          skip_group_check=True)
            mm.inc(m3, 1)
        blk.tensor(tbody)
        mmv = mm.n

        def v2body(v):
            v.wait_ge(mm.h, mmv)
            v.memset(ps[0:8, 3:4], 0.0)
            v.drain()
            cp = v.tensor_copy(t2[:, :], ps[0:8, 0:4])
            ev.inc(cp, 1)
        blk.vector(v2body)
        evb = ev.n

        def sbody(s):
            s.wait_ge(ev.h, evb)
            mw = dslot(s, bw, c.sc_srow, t2[:, :])
            s.wait_ge(bw.h, mw)
            dslot(s, br, srow[0:1, :], c.sc_srow.rearrange("a b -> (a b)").unsqueeze(0))
        blk.sync(sbody)
        brn = br.n

        def v3body(v):
            v.wait_ge(br.h, brn)
            rv2 = srow[0:1, :].rearrange("a (b cd) -> a cd b", cd=4)
            v.tensor_reduce(lg[0:1, 16:20], rv2, AX.X, ALU.add)
            v.drain()
            v.tensor_scalar(lg[0:1, 20:21], lg[0:1, 16:17], 1.0 / (614 * DM), None, ALU.mult)
            v.tensor_tensor(lg[0:1, 21:22], lg[0:1, 17:18], lg[0:1, 18:19], ALU.add)
            v.drain()
            v.tensor_scalar(lg[0:1, 21:22], lg[0:1, 21:22], 1.0 / (1434 * DM), None, ALU.mult)
            v.tensor_scalar(lg[0:1, 20:21], lg[0:1, 20:21], 1e-8, None, ALU.add)
            v.drain()
            v.reciprocal(lg[0:1, 22:23], lg[0:1, 20:21])
            v.drain()
            rr = v.tensor_tensor(c.ratio[0:1, :], lg[0:1, 21:22], lg[0:1, 22:23], ALU.mult)
            ev.inc(rr, 1)
            v.tensor_copy(lg[0:1, 0:3], ps[0:1, 4:7])
            v.drain()
            v.tensor_scalar(lg[0:1, 4:7], c.k_gwr[0:1, :], c.ratio[0:1, 0:1], None, ALU.mult)
            v.drain()
            v.tensor_tensor(lg[0:1, 0:3], lg[0:1, 0:3], lg[0:1, 4:7], ALU.add)
            v.drain()
            v.tensor_tensor(lg[0:1, 0:3], lg[0:1, 0:3], c.k_gbr[0:1, :], ALU.add)
            v.drain()
            v.tensor_reduce(lg[0:1, 8:9], lg[0:1, 0:3], AX.X, ALU.max)
            v.drain()
            ts = v.tensor_scalar(lg[0:1, 0:3], lg[0:1, 0:3], lg[0:1, 8:9], None, ALU.subtract)
            ev.inc(ts, 1)
        blk.vector(v3body)
        evl = ev.n

        def s8body(se):
            se.wait_ge(ev.h, evl)
            a = se.activation(lg[0:1, 0:3], lg[0:1, 0:3], AF.Exp)
            a.then_inc(sc2.h, 1)
        sc2.n += 1
        blk.scalar(s8body)
        scl = sc2.n

        def v4body(v):
            v.wait_ge(sc2.h, scl)
            v.tensor_reduce(lg[0:1, 9:10], lg[0:1, 0:3], AX.X, ALU.add)
            v.drain()
            v.reciprocal(lg[0:1, 10:11], lg[0:1, 9:10])
            v.drain()
            v.tensor_scalar(lg[0:1, 0:3], lg[0:1, 0:3], lg[0:1, 10:11], None, ALU.mult)
            v.drain()
            v.tensor_reduce(lg[0:1, 11:12], lg[0:1, 0:3], AX.X, ALU.max)
            v.tensor_reduce(lg[0:1, 12:13], lg[0:1, 0:3], AX.X, ALU.min)
            v.drain()
            v.tensor_scalar(lg[0:1, 4:7], lg[0:1, 0:3], lg[0:1, 11:12], None, ALU.is_equal)
            v.tensor_scalar(lg[0:1, 13:16], lg[0:1, 0:3], lg[0:1, 12:13], None, ALU.is_equal)
            v.drain()
            v.tensor_tensor(lg[0:1, 13:16], lg[0:1, 4:7], lg[0:1, 13:16], ALU.add)
            v.drain()
            i1 = v.tensor_scalar(lg[0:1, 13:16], lg[0:1, 13:16], -1.0, 1.0, ALU.mult, ALU.add)
            ev.inc(i1, 1)
        blk.vector(v4body)
        evi = ev.n
        mmi0 = mm.n

        def t2body(t):
            t.wait_ge(ev.h, evi)
            m1 = t.matmul(ps[:, 8:11], c.k_onesrf[0:1, 0:128], lg[0:1, 4:7], start=True, stop=True,
                          skip_group_check=True)
            mm.inc(m1, 1)
            m2 = t.matmul(ps[:, 12:15], c.k_onesrf[0:1, 0:128], lg[0:1, 13:16], start=True, stop=True,
                          skip_group_check=True)
            mm.inc(m2, 1)
        blk.tensor(t2body)
        mmi = mm.n

        def v5body(v):
            v.wait_ge(mm.h, mmi)
            v.tensor_copy(c.i0bc[:, :], ps[:, 8:11])
            cp = v.tensor_copy(c.i1bc[:, :], ps[:, 12:15])
            ev.inc(cp, 1)
        blk.vector(v5body)

        if c.debug:
            evn = ev.n

            def dbody(s):
                s.wait_ge(ev.h, evn)
                sd.dma(s.dma_start(out=c.dbg["scal"][0:1, 0:32], in_=lg[0:1, :]))
                sd.dma(s.dma_start(out=c.dbg["scal"][0:1, 32:33], in_=c.ratio[0:1, :]))
            blk.sync(dbody)
        if end_barrier:
            barrier(c)


def _p_gate(c, xnT):
    nc, blk = c.nc, c.blk
    mm, ev, sc2, fr = c.mm, c.ev, c.sc2, c.fr
    wsem = [c.DS(f"g_w{i}") for i in range(2)]
    gsem = [c.DS(f"g_s{i}") for i in range(8)]

    with contextlib.ExitStack() as es:
        wq = [es.enter_context(nc.sbuf_tensor(f"g_w{i}", [128, 4 * 512], BF16)) for i in range(2)]
        psg = [es.enter_context(nc.psum_tensor(f"g_ps{i}", [128, 512], F32)) for i in range(8)]
        sgt = [es.enter_context(nc.sbuf_tensor(f"g_s{i}", [128, 512], BF16)) for i in range(8)]

        fr0 = fr.n
        mm0 = mm.n
        sc20 = sc2.n
        wm = {}
        sgm = {i: gsem[i % 8].n + 16 * (i // 8 + 1) for i in range(128)}

        def gbody(g):
            i = 0
            for fs in range(16):
                for kq in range(8):
                    if i >= 2:
                        g.wait_ge(fr.h, fr0 + i - 1)
                    src = c.wgateT[kq * 512:(kq + 1) * 512, fs * 512:(fs + 1) * 512]
                    wm[i] = dslot(g, wsem[i % 2], wq[i % 2][:, :].rearrange("p (a b) -> p a b", b=512),
                                  src.rearrange("(a p) b -> p a b", p=128),
                                  max_dma_last_dim=2048)
                    i += 1
        blk.gpsimd(gbody)

        def tbody(t):
            i = 0
            for fs in range(16):
                for kq in range(8):
                    t.wait_ge(wsem[i % 2].h, wm[i])
                    for kj in range(4):
                        k = kq * 4 + kj
                        for tt in range(8):
                            if k == 0 and fs > 0:
                                t.wait_ge(sc2.h, sc20 + (fs - 1) * 8 + tt + 1)
                            last = t.matmul(psg[tt][:, :],
                                            xnT[:, :].rearrange("p (a b) -> p a b", b=TOK)[:, k, tt * 128:(tt + 1) * 128],
                                            wq[i % 2][:, kj * 512:(kj + 1) * 512],
                                            start=(k == 0), stop=(k == KD - 1))
                            if kj == 0 and tt == 0 and i >= 1:
                                fr.inc(last, 1)
                            if k == KD - 1:
                                mm.inc(last, 1)
                    i += 1
            t.drain()
            fr.inc(t.nop(), 1)
        blk.tensor(tbody)
        assert fr.n == fr0 + 128

        def scbody(se):
            for fs in range(16):
                for tt in range(8):
                    i = fs * 8 + tt
                    se.wait_ge(mm.h, mm0 + fs * 8 + tt + 1)
                    if i >= 8:
                        se.wait_ge(gsem[i % 8].h, sgm[i - 8])
                    a = se.activation(sgt[i % 8][:, :], psg[tt][:, :], AF.Silu)
                    a.then_inc(sc2.h, 1)
        sc2.n += 128
        blk.scalar(scbody)

        def sbody(s):
            for fs in range(16):
                for tt in range(8):
                    i = fs * 8 + tt
                    s.wait_ge(sc2.h, sc20 + i + 1)
                    m = dslot(s, gsem[i % 8], c.sc_sg[tt * 128:(tt + 1) * 128, fs * 512:(fs + 1) * 512],
                              sgt[i % 8][:, :])
                    assert m == sgm[i]
        blk.sync(sbody)
        barrier(c)


def _p_router(c, xnT):
    nc, blk = c.nc, c.blk
    mm, ev, sc2 = c.mm, c.ev, c.sc2
    sd = c.DS("dbg", serial=False)

    with contextlib.ExitStack() as es:
        rgb = es.enter_context(nc.sbuf_tensor("r_rgb", [1, 3], BF16))
        gbb = es.enter_context(nc.sbuf_tensor("r_gbb", [1, 3], BF16))
        sc3 = es.enter_context(nc.sbuf_tensor("r_sc", [128, 24 + 8 * 8], F32))
        rbuf = es.enter_context(nc.sbuf_tensor("r_rb", [24, 128], BF16))
        wcb = es.enter_context(nc.sbuf_tensor("r_wcb", [128, 24], BF16))
        psr = es.enter_context(nc.psum_tensor("r_psr", [128, 24], F32))
        pst = es.enter_context(nc.psum_tensor("r_pst", [128, 128], BF16))

        E3 = sc3[:, 0:24].rearrange("p (a b) -> p a b", b=3)
        col8 = lambda i: sc3[:, 24 + i * 8:24 + (i + 1) * 8]

        def vb0(v):
            v.tensor_scalar(rgb[0:1, :], c.k_gwr[0:1, :], c.ratio[0:1, 0:1], None, ALU.mult)
            cp = v.tensor_copy(gbb[0:1, :], c.k_gbr[0:1, :])
            ev.inc(cp, 1)
        blk.vector(vb0)
        evr = ev.n

        def tbody(t):
            t.wait_ge(ev.h, evr)
            for tt in range(8):
                for k in range(KD):
                    t.matmul(psr[:, tt * 3:tt * 3 + 3],
                             xnT[:, :].rearrange("p (a b) -> p a b", b=TOK)[:, k, tt * 128:(tt + 1) * 128],
                             c.k_gwm[:, k * 3:(k + 1) * 3],
                             start=(tt == 0 and k == 0), stop=False)
                t.matmul(psr[:, tt * 3:tt * 3 + 3], c.k_onesrb[0:1, 0:128], rgb[0:1, :],
                         start=False, stop=False)
                last = t.matmul(psr[:, tt * 3:tt * 3 + 3], c.k_onesrb[0:1, 0:128], gbb[0:1, :],
                                start=False, stop=(tt == 7))
            mm.inc(last, 1)
        blk.tensor(tbody)
        mmt = mm.n

        def v2body(v):
            v.wait_ge(mm.h, mmt)
            v.tensor_copy(sc3[:, 0:24], psr[:, 0:24])
            v.drain()
            v.tensor_reduce(col8(0).unsqueeze(2), E3, AX.X, ALU.max)
            v.drain()
            ts = v.tensor_tensor(E3, E3, col8(0).unsqueeze(2).broadcast_to([128, 8, 3]), ALU.subtract)
            ev.inc(ts, 1)
        blk.vector(v2body)
        evs = ev.n

        def sbody(se):
            se.wait_ge(ev.h, evs)
            a = se.activation(sc3[:, 0:24], sc3[:, 0:24], AF.Exp)
            a.then_inc(sc2.h, 1)
        sc2.n += 1
        blk.scalar(sbody)
        scs = sc2.n

        def v3body(v):
            v.wait_ge(sc2.h, scs)
            v.tensor_reduce(col8(1).unsqueeze(2), E3, AX.X, ALU.add)
            v.drain()
            v.reciprocal(col8(2), col8(1))
            v.drain()
            v.tensor_tensor(E3, E3, col8(2).unsqueeze(2).broadcast_to([128, 8, 3]), ALU.mult)
            v.drain()
            v.tensor_reduce(col8(3).unsqueeze(2), E3, AX.X, ALU.max)
            v.tensor_reduce(col8(4).unsqueeze(2), E3, AX.X, ALU.min)
            v.tensor_reduce(col8(5).unsqueeze(2), E3, AX.X, ALU.add)
            v.drain()
            v.tensor_tensor(col8(5), col8(5), col8(3), ALU.subtract)
            v.drain()
            v.tensor_tensor(col8(5), col8(5), col8(4), ALU.subtract)
            v.drain()
            v.tensor_tensor(col8(6), col8(3), col8(5), ALU.add)
            v.drain()
            v.tensor_scalar(col8(6), col8(6), 1e-8, None, ALU.add)
            v.drain()
            v.reciprocal(col8(7), col8(6))
            v.drain()
            v.tensor_tensor(col8(3), col8(3), col8(7), ALU.mult)
            v.tensor_tensor(col8(5), col8(5), col8(7), ALU.mult)
            v.drain()
            W3 = c.wcols[:, :].rearrange("p (a b) -> p a b", b=3)
            v.tensor_tensor(W3, col8(3).unsqueeze(2).broadcast_to([128, 8, 3]),
                            c.i0bc[:, :].unsqueeze(1).broadcast_to([128, 8, 3]), ALU.mult)
            v.tensor_tensor(E3, col8(5).unsqueeze(2).broadcast_to([128, 8, 3]),
                            c.i1bc[:, :].unsqueeze(1).broadcast_to([128, 8, 3]), ALU.mult)
            v.drain()
            v.tensor_tensor(c.wcols[:, :], c.wcols[:, :], sc3[:, 0:24], ALU.add)
            v.drain()
            cp = v.tensor_copy(wcb[:, :], c.wcols[:, :])
            ev.inc(cp, 1)
        blk.vector(v3body)
        evw = ev.n

        def tc(t):
            t.wait_ge(ev.h, evw)
            tr = t.transpose(pst[0:24, 0:128], wcb[:, :], c.k_identb[:, :])
            mm.inc(tr, 1)
        blk.tensor(tc)
        mmc = mm.n

        def vc2(v):
            v.wait_ge(mm.h, mmc)
            cp = v.tensor_copy(rbuf[0:24, :], pst[0:24, 0:128])
            ev.inc(cp, 1)
        blk.vector(vc2)
        evc = ev.n
        bw = c.DS("psc_bw")
        br = c.DS("psc_br")

        def sc_b(s):
            s.wait_ge(ev.h, evc)
            mwr = dslot(s, bw, c.sc_wr, rbuf[0:24, :])
            s.wait_ge(bw.h, mwr)
            src = c.sc_wr.rearrange("(t e) f -> e t f", e=3)
            for e in range(3):
                dslot(s, br, c.wrow_e[e][0:1, :].rearrange("a (t f) -> a t f", f=128),
                      src[e].unsqueeze(0))
        blk.sync(sc_b)

        if c.debug:
            evn = ev.n

            def dbody(s):
                s.wait_ge(ev.h, evn)
                sd.dma(s.dma_start(out=c.dbg["wcols"][:, :], in_=c.wcols[:, :]))
            blk.sync(dbody)
        barrier(c)


def _p_ffn(c, xnT):
    nc, blk = c.nc, c.blk
    mm, ev, sc2, fr, fr2 = c.mm, c.ev, c.sc2, c.fr, c.fr2
    w1s = [c.DS(f"f_w1{i}") for i in range(2)]
    w2s = [c.DS(f"f_w2{i}") for i in range(3)]
    bes = c.DS("f_be")
    prs = [c.DS(f"f_pv{i}") for i in range(4)]
    acs = [c.DS(f"f_ac{i}") for i in range(4)]
    sd = c.DS("dbg", serial=False)
    funcs = [AF.Tanh, AF.Gelu, AF.Silu]

    with contextlib.ExitStack() as es:
        w1b = [es.enter_context(nc.sbuf_tensor(f"f_w1{i}", [128, KD * 128], BF16)) for i in range(2)]
        w2b = [es.enter_context(nc.sbuf_tensor(f"f_w2{i}", [128, 4 * 512], BF16)) for i in range(3)]
        h1 = es.enter_context(nc.sbuf_tensor("f_h1", [128, KD * TOK], BF16))
        pso = [es.enter_context(nc.psum_tensor(f"f_ps{i}", [128, 512], F32)) for i in range(8)]
        acc = [es.enter_context(nc.sbuf_tensor(f"f_ac{i}", [128, 512], F32)) for i in range(4)]
        prev = [es.enter_context(nc.sbuf_tensor(f"f_pv{i}", [128, 512], F32)) for i in range(4)]
        be = es.enter_context(nc.sbuf_tensor("f_be", [1, DM], BF16))

        for e in range(3):
            bem = {}
            mmprev = mm.n
            evprev = ev.n

            def gb0(g, e=e):
                if e > 0:
                    g.wait_ge(mm.h, mmprev)
                bem["b1"] = dslot(g, bes, be[:, :], c.b1r[0:1, e * DM:(e + 1) * DM], max_dma_last_dim=8192)
            blk.gpsimd(gb0)
            fr0 = fr.n
            mm0 = mm.n
            sc20 = sc2.n
            w1m = c.__dict__.pop("_ffn_pf", {})

            def gbody(g, e=e):
                for ft in range(KD):
                    if ft in w1m:
                        continue
                    if ft >= 2:
                        g.wait_ge(fr.h, fr0 + ft - 1)
                    w1m[ft] = dslot(g, w1s[ft % 2], w1b[ft % 2][:, :].rearrange("p (a b) -> p a b", b=128),
                                    c.w1t[e, ft], max_dma_last_dim=8192)
            blk.gpsimd(gbody)

            def tbody(t, e=e):
                if e > 0:
                    t.wait_ge(ev.h, evprev - 7)
                for ft in range(KD):
                    t.wait_ge(w1s[ft % 2].h, w1m[ft])
                    for tb in range(2):
                        i = 2 * ft + tb
                        if i >= 2:
                            t.wait_ge(sc2.h, sc20 + i - 1)
                        elif i == 1 and e > 0:
                            t.wait_ge(ev.h, evprev - 6)
                        for k in range(KD):
                            fm = t.matmul(pso[i % 2][:, :], w1b[ft % 2][:, k * 128:(k + 1) * 128],
                                          xnT[:, :].rearrange("p (a b) -> p a b", b=TOK)[:, k, tb * 512:(tb + 1) * 512],
                                          start=(k == 0), stop=False)
                            if k == 0 and tb == 0 and ft >= 1:
                                fr.inc(fm, 1)
                        if ft == 0 and tb == 0:
                            t.wait_ge(bes.h, bem["b1"])
                        last = t.matmul(pso[i % 2][:, :],
                                        be[0:1, ft * 128:(ft + 1) * 128],
                                        c.k_onesrb[0:1, 0:512], start=False, stop=True)
                        mm.inc(last, 1)
                t.drain()
                fr.inc(t.nop(), 1)
            blk.tensor(tbody)
            assert fr.n == fr0 + KD

            def scbody(se, e=e):
                if e > 0:
                    se.wait_ge(mm.h, mmprev)
                for ft in range(KD):
                    for tb in range(2):
                        i = 2 * ft + tb
                        se.wait_ge(mm.h, mm0 + i + 1)
                        a = se.activation(h1[:, :].rearrange("p (a b) -> p a b", b=TOK)[:, ft, tb * 512:(tb + 1) * 512],
                                          pso[i % 2][:, :], funcs[e])
                        a.then_inc(sc2.h, 1)
            sc2.n += 2 * KD
            blk.scalar(scbody)
            sc2h = sc2.n

            if e < 2:
                pf = {}

                def spf(g, e=e):
                    g.wait_ge(fr.h, fr0 + KD)
                    pf[0] = dslot(g, w1s[0], w1b[0][:, :].rearrange("p (a b) -> p a b", b=128),
                                  c.w1t[e + 1, 0], max_dma_last_dim=8192)
                    pf[1] = dslot(g, w1s[1], w1b[1][:, :].rearrange("p (a b) -> p a b", b=128),
                                  c.w1t[e + 1, 1], max_dma_last_dim=8192)
                blk.gpsimd(spf)
                c._ffn_pf = pf

            def gb2(g, e=e):
                g.wait_ge(sc2.h, sc2h)
                bem["b2"] = dslot(g, bes, be[:, :], c.b2r[0:1, e * DM:(e + 1) * DM], max_dma_last_dim=8192)
            blk.gpsimd(gb2)

            fr20 = fr2.n
            mm20 = mm.n
            ev20 = ev.n
            w2m = {}

            def g2body(g, e=e):
                i = 0
                for ds in range(8):
                    for kq in range(8):
                        if i >= 3:
                            g.wait_ge(fr2.h, fr20 + i - 2)
                        src = c.w2[e, kq * 512:(kq + 1) * 512, ds * 512:(ds + 1) * 512]
                        w2m[i] = dslot(g, w2s[i % 3], w2b[i % 3][:, :].rearrange("p (a b) -> p a b", b=512),
                                       src.rearrange("(a p) b -> p a b", p=128),
                                       max_dma_last_dim=2048)
                        i += 1
            blk.gpsimd(g2body)

            def t2body(t, e=e):
                t.wait_ge(sc2.h, sc2h)
                if e > 0:
                    t.wait_ge(ev.h, evprev)
                i = 0
                for ds in range(8):
                    for kq in range(8):
                        t.wait_ge(w2s[i % 3].h, w2m[i])
                        for kj in range(4):
                            kf = kq * 4 + kj
                            for tt in range(8):
                                if kf == 0 and ds > 0:
                                    t.wait_ge(ev.h, ev20 + (ds - 1) * 8 + tt + 1)
                                fm = t.matmul(pso[tt][:, :],
                                              h1[:, :].rearrange("p (a b) -> p a b", b=TOK)[:, kf, tt * 128:(tt + 1) * 128],
                                              w2b[i % 3][:, kj * 512:(kj + 1) * 512],
                                              start=(kf == 0), stop=False)
                                if kj == 0 and tt == 0 and i >= 1:
                                    fr2.inc(fm, 1)
                        i += 1
                    if ds == 0:
                        t.wait_ge(bes.h, bem["b2"])
                    for tt in range(8):
                        last = t.matmul(pso[tt][:, :], c.wrow_e[e][0:1, tt * 128:(tt + 1) * 128],
                                        be[0:1, ds * 512:(ds + 1) * 512],
                                        start=False, stop=True)
                        mm.inc(last, 1)
                t.drain()
                fr2.inc(t.nop(), 1)
            blk.tensor(t2body)
            assert fr2.n == fr20 + 64

            sc2g0 = sc2.n
            sp_prev = c.__dict__.get("_ffn_sp_prev")
            rel_mark = {i: prs[i % 4].n + 16 * (i // 4 + 1) for i in range(64)} if e > 0 else {}
            sp_mark = {j: acs[j % 4].n + 16 * (j // 4 + 1) for j in range(64)}

            def v2body(v, e=e):
                for ds in range(8):
                    for tt in range(8):
                        i = ds * 8 + tt
                        v.wait_ge(mm.h, mm20 + ds * 8 + tt + 1)
                        if i >= 4:
                            v.wait_ge(acs[i % 4].h, sp_mark[i - 4])
                        elif e > 0:
                            v.wait_ge(acs[i % 4].h, sp_prev[60 + i])
                        wcol = c.wcols[:, tt * 3 + e:tt * 3 + e + 1]
                        if e == 0:
                            ts = v.tensor_scalar(acc[i % 4][:, :], pso[tt][:, :], wcol, None, ALU.mult)
                            ev.inc(ts, 1)
                        else:
                            v.wait_ge(prs[i % 4].h, rel_mark[i])
                            v.tensor_scalar(acc[i % 4][:, :], pso[tt][:, :], wcol, None, ALU.mult)
                            v.drain()
                            tt2 = v.tensor_tensor(acc[i % 4][:, :], acc[i % 4][:, :], prev[i % 4][:, :], ALU.add)
                            ev.inc(tt2, 1)
            blk.vector(v2body)

            if e == 2:
                def s3body(se, e=e):
                    for i in range(64):
                        se.wait_ge(ev.h, ev20 + i + 1)
                        a = se.activation(acc[i % 4][:, :], acc[i % 4][:, :], AF.Silu)
                        a.then_inc(sc2.h, 1)
                sc2.n += 64
                blk.scalar(s3body)

            def s2body(s, e=e):
                for i in range(66):
                    if i < 64 and e > 0:
                        ds, tt = i // 8, i % 8
                        if i >= 4:
                            s.wait_ge(ev.h, ev20 + i - 3)
                        else:
                            s.wait_ge(ev.h, evprev)
                        s.wait_ge(acs[i % 4].h, sp_prev[i])
                        m = dslot(s, prs[i % 4], prev[i % 4][:, :],
                                  c.sc_moe[tt * 128:(tt + 1) * 128, ds * 512:(ds + 1) * 512])
                        assert m == rel_mark[i]
                    j = i - 2
                    if 0 <= j < 64:
                        ds, tt = j // 8, j % 8
                        if e == 2:
                            s.wait_ge(sc2.h, sc2g0 + j + 1)
                            dst = c.sc_g[tt * 128:(tt + 1) * 128, ds * 512:(ds + 1) * 512]
                        else:
                            s.wait_ge(ev.h, ev20 + j + 1)
                            dst = c.sc_moe[tt * 128:(tt + 1) * 128, ds * 512:(ds + 1) * 512]
                        m = dslot(s, acs[j % 4], dst, acc[j % 4][:, :])
                        assert m == sp_mark[j]
            blk.sync(s2body)
            c._ffn_sp_prev = sp_mark
        barrier(c)

        if c.debug:
            def dbody(s):
                sd.dma(s.dma_start(out=c.dbg["g0"][:, :], in_=c.sc_g[0:128, :]))
            blk.sync(dbody)
            barrier(c)


def _p_outproj(c):
    nc, blk = c.nc, c.blk
    mm, ev, fr = c.mm, c.ev, c.fr
    zls = [c.DS(f"o_zl{i}", serial=False) for i in range(4)]
    wsem = [c.DS(f"o_w{i}") for i in range(3)]
    gls = [c.DS(f"o_g{i}") for i in range(2)]
    xls = [c.DS(f"o_x{i}") for i in range(2)]
    sts = [c.DS(f"st_o{i}") for i in range(2)]

    with contextlib.ExitStack() as es:
        zall = es.enter_context(nc.sbuf_tensor("o_z", [128, NCH * FT_X * 128], BF16))
        wb = [es.enter_context(nc.sbuf_tensor(f"o_w{i}", [128, 4 * 512], BF16)) for i in range(3)]
        pso = [es.enter_context(nc.psum_tensor(f"o_ps{i}", [128, 512], F32)) for i in range(8)]
        gld = [es.enter_context(nc.sbuf_tensor(f"o_g{i}", [128, 512], F32)) for i in range(2)]
        xld = [es.enter_context(nc.sbuf_tensor(f"o_x{i}", [128, 512], F32)) for i in range(2)]
        ob = [es.enter_context(nc.sbuf_tensor(f"o_o{i}", [128, 512], F32)) for i in range(2)]

        def sbody(s):
            for chk in range(NCH):
                zls[chk % 4].dma(s.dma_start(out=zall[:, :].rearrange("p (a b) -> p a b", b=FT_X * 128)[:, chk, :],
                                             in_=c.sc_z[chk]))
        blk.sync(sbody)
        zl_ns = [sem.n for sem in zls]

        fr0 = fr.n
        wm = {}

        def gbody(g):
            i = 0
            for ds in range(8):
                for kq in range(16):
                    if i >= 3:
                        g.wait_ge(fr.h, fr0 + i - 2)
                    src = c.woutT[kq * 512:(kq + 1) * 512, ds * 512:(ds + 1) * 512]
                    wm[i] = dslot(g, wsem[i % 3], wb[i % 3][:, :].rearrange("p (a b) -> p a b", b=512),
                                  src.rearrange("(a p) b -> p a b", p=128),
                                  max_dma_last_dim=2048)
                    i += 1
        blk.gpsimd(gbody)

        mm0 = mm.n
        ev0 = ev.n

        def tbody(t):
            for sem, n in zip(zls, zl_ns):
                t.wait_ge(sem.h, n)
            i = 0
            for ds in range(8):
                for kq in range(16):
                    t.wait_ge(wsem[i % 3].h, wm[i])
                    for kj in range(4):
                        kf = kq * 4 + kj
                        for chk in range(8):
                            if kf == 0 and ds > 0:
                                t.wait_ge(ev.h, ev0 + (ds - 1) * 8 + chk + 1)
                            last = t.matmul(pso[chk][:, :],
                                            zall[:, :].rearrange("p (a b) -> p a b", b=128)[:, chk * FT_X + kf, :],
                                            wb[i % 3][:, kj * 512:(kj + 1) * 512],
                                            start=(kf == 0), stop=(kf == FT_X - 1))
                            if kj == 0 and chk == 0 and i >= 1:
                                fr.inc(last, 1)
                            if kf == FT_X - 1:
                                mm.inc(last, 1)
                    i += 1
            t.drain()
            fr.inc(t.nop(), 1)
        blk.tensor(tbody)
        assert fr.n == fr0 + 128

        gld_mark = {i: gls[i % 2].n + 16 * (i // 2 + 1) for i in range(64)}
        xld_mark = {i: xls[i % 2].n + 16 * (i // 2 + 1) for i in range(64)}
        st_mark = {j: sts[j % 2].n + 16 * (j // 2 + 1) for j in range(64)}

        def s23body(s):
            for i in range(66):
                if i < 64:
                    ds, chk = i // 8, i % 8
                    if i >= 2:
                        s.wait_ge(ev.h, ev0 + i - 1)
                    mg = dslot(s, gls[i % 2], gld[i % 2][:, :],
                               c.sc_g[chk * 128:(chk + 1) * 128, ds * 512:(ds + 1) * 512])
                    mx = dslot(s, xls[i % 2], xld[i % 2][:, :],
                               c.x_own.rearrange("(a p) b -> a p b", p=128)[chk][:, ds * 512:(ds + 1) * 512])
                    assert (mg, mx) == (gld_mark[i], xld_mark[i])
                j = i - 2
                if 0 <= j < 64:
                    ds, chk = j // 8, j % 8
                    s.wait_ge(ev.h, ev0 + j + 1)
                    ms = dslot(s, sts[j % 2],
                               c.out.rearrange("(a p) b -> a p b", p=128)[chk][:, ds * 512:(ds + 1) * 512],
                               ob[j % 2][:, :])
                    assert ms == st_mark[j]
        blk.sync(s23body)

        def vbody(v):
            for ds in range(8):
                for chk in range(8):
                    i = ds * 8 + chk
                    v.wait_ge(mm.h, mm0 + i + 1)
                    v.wait_ge(gls[i % 2].h, gld_mark[i])
                    v.wait_ge(xls[i % 2].h, xld_mark[i])
                    if i >= 2:
                        v.wait_ge(sts[i % 2].h, st_mark[i - 2])
                    v.tensor_tensor(ob[i % 2][:, :], pso[chk][:, :], gld[i % 2][:, :], ALU.mult)
                    v.drain()
                    tt = v.tensor_tensor(ob[i % 2][:, :], ob[i % 2][:, :], xld[i % 2][:, :], ALU.add)
                    ev.inc(tt, 1)
        blk.vector(vbody)


def _prep_inputs(inputs):
    ipw = np.ascontiguousarray(inputs["L0_attn_in_proj_w"], np.float32)
    wgateT = np.ascontiguousarray(ipw[0:INTER].T)
    whbcT = np.ascontiguousarray(ipw[INTER:INTER + CONVD + 128].T)
    whbc = np.ascontiguousarray(whbcT.reshape(KD, 128, FT_ALL, 128).transpose(2, 1, 0, 3))
    woutT = np.ascontiguousarray(np.asarray(inputs["L0_attn_out_proj_w"], np.float32).T)
    w1 = np.ascontiguousarray(inputs["L0_ffn_w1"], np.float32)
    w1t = np.ascontiguousarray(w1.reshape(3, KD, 128, KD, 128).transpose(0, 3, 2, 1, 4))
    w2 = np.ascontiguousarray(inputs["L0_ffn_w2"], np.float32)
    b1r = np.ascontiguousarray(np.asarray(inputs["L0_ffn_b1"], np.float32).reshape(1, 3 * DM))
    b2r = np.ascontiguousarray(np.asarray(inputs["L0_ffn_b2"], np.float32).reshape(1, 3 * DM))
    gw = np.ascontiguousarray(inputs["L0_ffn_gate_w"], np.float32)
    gwm = np.ascontiguousarray(gw[0:DM].reshape(KD, 128, 3).transpose(1, 0, 2))
    gwr = np.ascontiguousarray(gw[DM:DM + 1])
    gbr = np.ascontiguousarray(np.asarray(inputs["L0_ffn_gate_b"], np.float32).reshape(1, 3))
    convw = np.ascontiguousarray(np.asarray(inputs["L0_attn_conv_w"], np.float32).reshape(FT_CONV, 128, 4).transpose(1, 0, 2))
    convb = np.ascontiguousarray(np.asarray(inputs["L0_attn_conv_b"], np.float32).reshape(FT_CONV, 128).T)
    dtb = np.ascontiguousarray(np.asarray(inputs["L0_attn_dt_bias"], np.float32).reshape(128, 1))
    alog = np.ascontiguousarray(np.asarray(inputs["L0_attn_A_log"], np.float32).reshape(128, 1))
    drow = np.ascontiguousarray(np.asarray(inputs["L0_attn_D"], np.float32).reshape(1, 128))
    nw = np.ascontiguousarray(np.asarray(inputs["L0_attn_norm_w"], np.float32).reshape(FT_X, 128).T)

    ident = np.eye(128, dtype=np.float32)
    i64x2 = np.concatenate([np.eye(64), np.eye(64)], axis=0).astype(np.float32)
    sl = np.arange(128)
    addmask = np.where(sl[:, None] <= sl[None, :], 0.0, NEG).astype(np.float32)
    ut = np.triu(np.ones((128, 128), np.float32))
    ones_r = np.ones((1, 512), np.float32)
    ones_c = np.ones((128, 1), np.float32)
    tokidx = (np.arange(8)[None, :] * 128 + np.arange(128)[:, None])
    vism = (tokidx < 614).astype(np.float32)

    return {
        "whbc": whbc, "wgateT": wgateT, "woutT": woutT, "w1t": w1t, "w2": w2,
        "b1r": b1r, "b2r": b2r, "gwm": gwm, "gwr": gwr, "gbr": gbr,
        "convw": convw, "convb": convb, "dtb": dtb, "alog": alog, "drow": drow,
        "nwcol": nw,
        "ident_b": ident.astype(BF), "ident_f": ident, "i64x2_b": i64x2.astype(BF),
        "ut_f": ut, "addmask": addmask.astype(BF),
        "ones_rb": ones_r.astype(BF), "ones_rf": ones_r, "ones_cb": ones_c.astype(BF),
        "ones2_b": np.ones((2, 128), BF),
        "vismask": vism,
        "epscol": np.full((128, 1), 1e-5, np.float32),
    }


def make_in_maps(inputs, shared=None):
    if shared is None:
        shared = _prep_inputs(inputs)
    x = np.ascontiguousarray(np.asarray(inputs["L0_attn_hidden_states"], np.float32))
    in_maps = []
    for core in range(8):
        b, h = core // 2, core % 2
        m = dict(shared)
        m["x_own"] = np.ascontiguousarray(x[b, h * TOK:(h + 1) * TOK])
        m["x_first"] = np.ascontiguousarray(x[b, 0:TOK])
        m["x_second"] = np.ascontiguousarray(x[b, TOK:2 * TOK])
        m["flag_col"] = np.full((128, 1), float(h), np.float32)
        in_maps.append(m)
    return in_maps


def kernel(**inputs):
    nc, _ = build_nc(stage=STAGE, debug=False)
    in_maps = make_in_maps(inputs)
    res = run_bass_kernel_spmd(nc, in_maps, core_ids=list(range(8)))
    out = np.empty((4, 2048, DM), np.float32)
    for core in range(8):
        b, h = core // 2, core % 2
        out[b, h * TOK:(h + 1) * TOK] = res.results[core]["out"]
    return out


# version 08
# speedup vs baseline: 1.0459x; 1.0459x over previous
import contextlib
import numpy as np
import ml_dtypes
import concourse.bass as bass
import concourse.mybir as mybir
from concourse.bass_utils import run_bass_kernel_spmd

F32 = mybir.dt.float32
BF16 = mybir.dt.bfloat16
AF = mybir.ActivationFunctionType
ALU = mybir.AluOpType
AX = mybir.AxisListType

TOK = 1024
CS = 128
NCH = 8
HD = 128
PH = 64
GR = 8
NS = 128
INTER = 8192
CONVD = 10240
FT_X = 64
FT_CONV = 80
FT_ALL = 81
DM = 4096
KD = 32
NEG = -1.0e30
BF = ml_dtypes.bfloat16

STAGE = 9
DEBUG = False


class Sem:
    def __init__(self, es, nc, name, serial=True):
        self.h = es.enter_context(nc.semaphore(name))
        self.n = 0
        self.serial = serial

    def inc(self, instr, k=1):
        instr.then_inc(self.h, k)
        self.n += k
        return self.n

    def dma(self, instr):
        return self.inc(instr, 16)


def dslot(eng, sem, out, in_, **kw):
    if sem.serial and sem.n:
        eng.wait_ge(sem.h, sem.n)
    sem.dma(eng.dma_start(out=out, in_=in_, **kw))
    return sem.n


class Ctx:
    pass


class BlkProxy:

    def __init__(self, blk):
        self._blk = blk

    @staticmethod
    def _wrap(f):
        def g(eng):
            eng.drain()
            return f(eng)
        return g

    def vector(self, body):
        return self._blk.vector(self._wrap(body))

    def scalar(self, body):
        return self._blk.scalar(self._wrap(body))

    def tensor(self, body):
        return self._blk.tensor(body)

    def sync(self, body):
        return self._blk.sync(body)

    def gpsimd(self, body):
        return self._blk.gpsimd(body)


def build_nc(stage=STAGE, debug=DEBUG):
    nc = bass.Bass("TRN2", target_bir_lowering=False, debug=False)
    c = Ctx()
    c.nc = nc
    c.stage = stage
    c.debug = debug

    def inp(name, shape, dt=F32):
        return nc.dram_tensor(name, shape, dt, kind="ExternalInput").ap()

    c.x_own = inp("x_own", [TOK, DM])
    c.x_first = inp("x_first", [TOK, DM])
    c.x_second = inp("x_second", [TOK, DM])
    c.flag_col = inp("flag_col", [128, 1])
    c.whbc = inp("whbc", [FT_ALL, 128, KD, 128])
    c.wgateT = inp("wgateT", [DM, INTER])
    c.woutT = inp("woutT", [INTER, DM])
    c.w1t = inp("w1t", [3, KD, 128, KD, 128])
    c.w2 = inp("w2", [3, DM, DM])
    c.b1r = inp("b1r", [1, 3 * DM])
    c.b2r = inp("b2r", [1, 3 * DM])
    c.gwm = inp("gwm", [128, KD, 3])
    c.gwr = inp("gwr", [1, 3])
    c.gbr = inp("gbr", [1, 3])
    c.convw = inp("convw", [128, FT_CONV, 4])
    c.convb = inp("convb", [128, FT_CONV])
    c.dtb = inp("dtb", [128, 1])
    c.alog = inp("alog", [128, 1])
    c.drow = inp("drow", [1, 128])
    c.nwcol = inp("nwcol", [128, FT_X])
    c.ident_b = inp("ident_b", [128, 128], BF16)
    c.ident_f = inp("ident_f", [128, 128])
    c.i64x2_b = inp("i64x2_b", [128, 64], BF16)
    c.ut_f = inp("ut_f", [128, 128])
    c.addmask = inp("addmask", [128, 128], BF16)
    c.ones_rb = inp("ones_rb", [1, 512], BF16)
    c.ones_rf = inp("ones_rf", [1, 512])
    c.ones_cb = inp("ones_cb", [128, 1], BF16)
    c.ones2_b = inp("ones2_b", [2, 128], BF16)
    c.vismask = inp("vismask", [128, 8])
    c.epscol = inp("epscol", [128, 1])

    c.out = nc.dram_tensor("out", [TOK, DM], F32, kind="ExternalOutput").ap()

    c.dbg = {}
    if debug:
        def dbg(name, shape, dt=F32):
            c.dbg[name] = nc.dram_tensor("dbg_" + name, shape, dt, kind="ExternalOutput").ap()
        dbg("stats", [128, 64])
        dbg("xnT", [128, 1024], BF16)
        dbg("dtA", [128, 2048])
        dbg("carry", [128, INTER])
        dbg("hbc0", [128, CONVD], BF16)
        dbg("xtok0", [128, INTER], BF16)
        dbg("scal", [1, 64])
        dbg("wcols", [128, 24])
        dbg("sg0", [128, INTER], BF16)
        dbg("g0", [128, DM])
        dbg("y0", [128, INTER], BF16)
        dbg("z0", [128, FT_X * 128], BF16)
        dbg("cum0", [128, 128])

    def scr(name, shape, dt):
        return nc.dram_tensor(name, shape, dt).ap()

    c.sc_hbc = scr("sc_hbc", [NCH, 128, FT_CONV, 128], BF16)
    c.sc_xtok = scr("sc_xtok", [NCH, 128, FT_X, 128], BF16)
    c.sc_sg = scr("sc_sg", [TOK, INTER], BF16)
    c.sc_g = scr("sc_g", [TOK, DM], F32)
    c.sc_moe = scr("sc_moe", [TOK, DM], F32)
    c.sc_z = scr("sc_z", [NCH, 128, FT_X * 128], BF16)
    c.sc_carry = scr("sc_carry", [128, INTER], F32)
    c.sc_rows = scr("sc_rows", [2, 128, 128], BF16)
    c.sc_srow = scr("sc_srow", [8, 4], F32)
    c.sc_wr = scr("sc_wr", [24, 128], BF16)

    es = contextlib.ExitStack()
    with es:
        sb = lambda name, shape, dt: es.enter_context(nc.sbuf_tensor(name, shape, dt))
        c.sb = sb
        c.k_identb = sb("k_identb", [128, 128], BF16)
        c.k_identf = sb("k_identf", [128, 128], F32)
        c.k_i64x2 = sb("k_i64x2", [128, 64], BF16)
        c.k_utf = sb("k_utf", [128, 128], F32)
        c.k_addm = sb("k_addm", [128, 128], BF16)
        c.k_onesrb = sb("k_onesrb", [1, 512], BF16)
        c.k_onesrf = sb("k_onesrf", [1, 512], F32)
        c.k_onescb = sb("k_onescb", [128, 1], BF16)
        c.k_ones2 = sb("k_ones2", [2, 128], BF16)
        c.k_convw = sb("k_convw", [128, FT_CONV * 4], F32)
        c.k_convb = sb("k_convb", [128, FT_CONV], F32)
        c.k_dtb = sb("k_dtb", [128, 1], F32)
        c.k_acol = sb("k_acol", [128, 1], F32)
        c.k_flag = sb("k_flag", [128, 1], F32)
        c.k_nw = sb("k_nw", [128, FT_X], F32)
        c.k_vism = sb("k_vism", [128, 8], F32)
        c.k_eps = sb("k_eps", [128, 1], F32)
        c.k_gwm = sb("k_gwm", [128, KD * 3], BF16)
        c.k_gwr = sb("k_gwr", [1, 3], F32)
        c.k_gbr = sb("k_gbr", [1, 3], F32)

        c.absA = sb("absA", [128, 8], F32)
        c.absB = sb("absB", [128, 8], F32)
        c.xn0 = sb("xn0", [128, KD], BF16)
        c.ratio = sb("ratio", [1, 1], F32)
        c.i0bc = sb("i0bc", [128, 3], F32)
        c.i1bc = sb("i1bc", [128, 3], F32)
        c.wcols = sb("wcols", [128, 8 * 3], F32)
        c.convtl = sb("convtl", [128, FT_CONV * 3], BF16)
        c.wrow_e = [sb(f"wrow{e}", [1, 8 * 128], BF16) for e in range(3)]

        S = lambda name: Sem(es, nc, name)
        c.mm = S("mm")
        c.ev = S("ev")
        c.sc = S("sc")
        c.sc2 = S("sc2")
        c.fr = S("fr")
        c.fr2 = S("fr2")
        c.fr3 = S("fr3")
        c.pb = S("pb")
        c.dw = S("dw")
        c.dsems = {}

        def DS(name, serial=True):
            if name not in c.dsems:
                c.dsems[name] = Sem(es, nc, name, serial=serial)
            return c.dsems[name]
        c.DS = DS

        c.blk = BlkProxy(es.enter_context(nc.Block()))
        _build_program(c)

    return nc, c


def barrier(c):
    blk, pb = c.blk, c.pb
    snap = [(sem.h, sem.n) for sem in c.dsems.values() if sem.n]

    def syncq(s):
        for h, n in snap:
            s.wait_ge(h, n)
        pb.inc(s.nop(), 1)
    blk.sync(syncq)

    def gq(g):
        pb.inc(g.nop(), 1)
    blk.gpsimd(gq)

    base = pb.n - 2
    for e in ["tensor", "vector", "scalar"]:
        def body(eng):
            eng.drain()
            pb.inc(eng.nop(), 1)
        getattr(blk, e)(body)
    tot = base + 5
    assert pb.n == tot
    for e in ["tensor", "vector", "scalar", "sync", "gpsimd"]:
        def body(eng):
            eng.wait_ge(pb.h, tot)
        getattr(blk, e)(body)


def _build_program(c):
    nc = c.nc
    _p_init(c)
    barrier(c)

    with contextlib.ExitStack() as da:
        dtbuf = da.enter_context(nc.sbuf_tensor("dtA", [128, 2 * TOK], F32))
        with contextlib.ExitStack() as xa:
            xnT = xa.enter_context(nc.sbuf_tensor("xnT_A", [128, KD * TOK], BF16))
            _p_ln(c, c.x_first, xnT, absrow=c.absA, save_xn0=True, tag="lnA")
            barrier(c)
            if c.stage >= 2:
                _p_front(c, xnT, dtbuf, tag="feA", lite=True)
                barrier(c)
        if c.stage < 3:
            _finish(c)
            return

        dtbufB = dtbuf
        if True:
            with contextlib.ExitStack() as xb:
                xnTB = xb.enter_context(nc.sbuf_tensor("xnT_B", [128, KD * TOK], BF16))
                _p_ln(c, c.x_own, xnTB, absrow=None, save_xn0=False, tag="lnB")
                barrier(c)

                def chA_hook(ch):
                    if ch == 0:
                        _p_lnabs(c, c.x_second, c.absB, tag="lnM")
                    if ch == 4 and c.stage >= 4:
                        _p_scalars(c, end_barrier=False)
                _p_chunks(c, dtbuf, variant="A", tag="chA", hook=chA_hook)
                barrier(c)

                if c.stage >= 4:
                    _p_gate(c, xnTB)
                    barrier(c)
                    _p_router(c, xnTB)
                    barrier(c)
                if c.stage >= 5:
                    _p_ffn(c, xnTB)
                if c.stage >= 6:
                    _p_front(c, xnTB, dtbufB, tag="feB")
                    barrier(c)
            if c.stage >= 7:
                _p_chunks(c, dtbufB, variant="B", tag="chB")
                barrier(c)
        if c.stage >= 8:
            _p_outproj(c)
    _finish(c)


def _finish(c):
    blk = c.blk
    if c.stage < 8:
        nc = c.nc
        st = c.DS("st_zero", serial=False)
        with contextlib.ExitStack() as es:
            zb = es.enter_context(nc.sbuf_tensor("zero_out", [128, DM], F32))

            def vbody(v):
                ms = v.memset(zb[:, :], 0.0)
                c.ev.inc(ms, 1)
            blk.vector(vbody)
            evn = c.ev.n

            def sbody(s):
                s.wait_ge(c.ev.h, evn)
                for i in range(8):
                    st.dma(s.dma_start(out=c.out.rearrange("(n p) d -> n p d", p=128)[i], in_=zb[:, :]))
                s.wait_ge(st.h, st.n)
            blk.sync(sbody)
    else:
        snap = [(sem.h, sem.n) for name, sem in c.dsems.items() if name.startswith("st") and sem.n]

        def sbody(s):
            for h, n in snap:
                s.wait_ge(h, n)
        blk.sync(sbody)


def _p_init(c):
    nc, blk = c.nc, c.blk
    ev, sc2, mm = c.ev, c.sc2, c.mm
    sd = c.DS("ini", serial=False)
    wd = c.DS("ini_g", serial=False)

    c.dbc = c.sb("dbc_sb", [128, 128], BF16)
    alog_sb = c.sb("alog_sb", [128, 1], F32)
    drow_sb = c.sb("drow_sb", [1, 128], F32)
    drow_b = c.sb("drow_b", [1, 128], BF16)
    dbc = c.dbc

    def sbody(s):
        for dst, src in [(c.k_identb, c.ident_b), (c.k_identf, c.ident_f), (c.k_i64x2, c.i64x2_b),
                         (c.k_utf, c.ut_f), (c.k_addm, c.addmask), (c.k_onesrb, c.ones_rb),
                         (c.k_onesrf, c.ones_rf), (c.k_onescb, c.ones_cb), (c.k_ones2, c.ones2_b),
                         (c.k_convb, c.convb), (c.k_dtb, c.dtb), (c.k_flag, c.flag_col),
                         (c.k_nw, c.nwcol), (c.k_vism, c.vismask), (c.k_eps, c.epscol), (c.k_gwr, c.gwr), (c.k_gbr, c.gbr),
                         (alog_sb, c.alog), (drow_sb, c.drow)]:
            sd.dma(s.dma_start(out=dst[:, :], in_=src))
        sd.dma(s.dma_start(out=c.k_convw[:, :].rearrange("p (a b) -> p a b", b=4), in_=c.convw))
    blk.sync(sbody)
    sdn = sd.n

    def gbody(g):
        wd.dma(g.dma_start(out=c.k_gwm[:, :].rearrange("p (a b) -> p a b", b=3), in_=c.gwm))
    blk.gpsimd(gbody)

    def scb(se):
        se.wait_ge(sd.h, sdn)
        a1 = se.activation(c.k_acol[:, :], alog_sb[:, :], AF.Exp)
        sc2.inc(a1, 1)
    blk.scalar(scb)
    sc2n = sc2.n

    def vb(v):
        v.wait_ge(sc2.h, sc2n)
        v.tensor_scalar(c.k_acol[:, :], c.k_acol[:, :], -1.0, None, ALU.mult)
        cp = v.tensor_copy(drow_b[:, :], drow_sb[:, :])
        ev.inc(cp, 1)
    blk.vector(vb)
    evn = ev.n

    with contextlib.ExitStack() as pes:
        psd = pes.enter_context(nc.psum_tensor("ps_init", [128, 128], F32))

        def tb(t):
            t.wait_ge(ev.h, evn)
            m1 = t.matmul(psd[:, :], c.k_onesrb[0:1, 0:128], drow_b[0:1, :], start=True, stop=True)
            mm.inc(m1, 1)
        blk.tensor(tb)
        mmn = mm.n

        def vb2(v):
            v.wait_ge(mm.h, mmn)
            cp = v.tensor_copy(dbc[:, :], psd[:, :])
            ev.inc(cp, 1)
        blk.vector(vb2)
        barrier(c)


def _p_ln(c, x_dram, xnT, absrow, save_xn0, tag, end_barrier=True):
    nc, blk = c.nc, c.blk
    mm, ev, sc, sc2, fr, fr2 = c.mm, c.ev, c.sc, c.sc2, c.fr, c.fr2
    lx = [c.DS(f"ln_lx{i}") for i in range(2)]
    sd = c.DS("dbg", serial=False)

    with contextlib.ExitStack() as es:
        xt = [es.enter_context(nc.sbuf_tensor(f"{tag}_x{i}", [128, DM], F32)) for i in range(2)]
        xn_nb = 2 if xnT is not None else 1
        xn_t = [es.enter_context(nc.sbuf_tensor(f"{tag}_xn{i}", [128, DM], BF16)) for i in range(xn_nb)]
        scrA = es.enter_context(nc.sbuf_tensor(f"{tag}_scA", [128, DM], BF16)) if xnT is not None else None
        stats = es.enter_context(nc.sbuf_tensor(f"{tag}_st", [128, 40], F32))
        pst = [es.enter_context(nc.psum_tensor(f"{tag}_pst{i}", [128, 512], BF16)) for i in range(2)] if xnT is not None else None

        xv = x_dram.rearrange("(n p) d -> n p d", p=128)
        fr0 = fr.n
        lxm = {}

        def sbody(s):
            for i in range(8):
                if i >= 2:
                    s.wait_ge(fr.h, fr0 + i - 1)
                lxm[i] = dslot(s, lx[i % 2], xt[i % 2][:, :], xv[i])
        blk.sync(sbody)

        sc0 = sc.n
        ev0 = ev.n
        sc20 = sc2.n

        dw = c.dw
        abs_mark = {}

        abs_lag = 2 if xnT is not None else 1

        def emit_abs(se, j):
            dst = scrA[:, :] if scrA is not None else xn_t[j % xn_nb][:, :]
            ab = se.activation(dst, xn_t[j % xn_nb][:, :], AF.Abs, accum_out=absrow[:, j:j + 1])
            dw.inc(ab, 1)
            abs_mark[j] = dw.n

        def scbody(se):
            for i in range(8):
                se.wait_ge(lx[i % 2].h, lxm[i])
                if i >= 1:
                    se.wait_ge(sc.h, sc0 + 2 * i - 1)
                if absrow is not None and i >= abs_lag:
                    se.wait_ge(dw.h, abs_mark[i - abs_lag])
                d1 = scrA[:, :] if scrA is not None else xt[i % 2][:, :]
                d2 = scrA[:, :] if scrA is not None else xn_t[0][:, :]
                a1 = se.activation(d1, xt[i % 2][:, :], AF.Identity, accum_out=stats[:, i:i + 1])
                dw.inc(a1, 1)
                se.wait_ge(dw.h, dw.n)
                a2 = se.activation(d2, xt[i % 2][:, :], AF.Square, accum_out=stats[:, 8 + i:9 + i])
                a2.then_inc(sc.h, 1)
                se.wait_ge(ev.h, ev0 + 2 * i + 1)
                a3 = se.activation(stats[:, 32 + i:33 + i], stats[:, 24 + i:25 + i], AF.Ln, bias=c.k_eps[:, :])
                a3.then_inc(sc2.h, 1)
                se.wait_ge(sc2.h, sc20 + i + 1)
                a4 = se.activation(stats[:, 32 + i:33 + i], stats[:, 32 + i:33 + i], AF.Exp, scale=-0.5)
                a4.then_inc(sc.h, 1)
                if absrow is not None:
                    if xnT is None:
                        se.wait_ge(ev.h, ev0 + 2 * i + 2)
                        emit_abs(se, i)
                    elif i >= 1:
                        emit_abs(se, i - 1)
            if absrow is not None and xnT is not None:
                se.wait_ge(ev.h, ev0 + 16)
                se.wait_ge(dw.h, dw.n)
                emit_abs(se, 7)
        sc.n += 16
        sc2.n += 8
        blk.scalar(scbody)

        mm0 = mm.n
        fr20 = fr2.n

        def evac(v, it):
            for q in range(8):
                j = 8 * it + q
                v.wait_ge(mm.h, mm0 + j + 1)
                dt0 = q * 4
                dst = xnT[:, :].rearrange("p (a b) -> p a b", b=TOK)[:, dt0:dt0 + 4, it * 128:(it + 1) * 128]
                cp = v.tensor_copy(dst, pst[j % 2][:, :].rearrange("p (a b) -> p a b", b=128))
                fr2.inc(cp, 1)

        def vbody(v):
            for i in range(8):
                v.wait_ge(sc.h, sc0 + 2 * i + 1)
                m1 = v.tensor_scalar(stats[:, 16 + i:17 + i], stats[:, i:i + 1], 1.0 / DM, None, ALU.mult)
                if i >= 1:
                    fr.inc(m1, 1)
                v.tensor_scalar(stats[:, 24 + i:25 + i], stats[:, 8 + i:9 + i], 1.0 / DM, None, ALU.mult)
                v.drain()
                v.scalar_tensor_tensor(stats[:, 24 + i:25 + i], stats[:, 16 + i:17 + i],
                                       stats[:, 16 + i:17 + i], stats[:, 24 + i:25 + i],
                                       ALU.mult, ALU.subtract)
                v.drain()
                ng = v.tensor_scalar(stats[:, 24 + i:25 + i], stats[:, 24 + i:25 + i], -1.0, None, ALU.mult)
                ev.inc(ng, 1)
                v.wait_ge(sc.h, sc0 + 2 * i + 2)
                if xnT is not None and i >= 2:
                    v.wait_ge(mm.h, mm0 + 8 * (i - 1))
                if absrow is not None and i >= abs_lag:
                    v.wait_ge(dw.h, abs_mark[i - abs_lag])
                ts = v.tensor_scalar(xn_t[i % xn_nb][:, :], xt[i % 2][:, :],
                                     stats[:, 16 + i:17 + i], stats[:, 32 + i:33 + i],
                                     ALU.subtract, ALU.mult)
                ev.inc(ts, 1)
                if xnT is not None and i >= 1:
                    evac(v, i - 1)
            if xnT is not None:
                evac(v, 7)
            v.drain()
            fr.inc(v.nop(), 1)
        blk.vector(vbody)
        assert ev.n == ev0 + 16


        if xnT is not None:
            def tbody(t):
                for i in range(8):
                    t.wait_ge(ev.h, ev0 + 2 * i + 2)
                    for q in range(8):
                        j = 8 * i + q
                        if j >= 2:
                            t.wait_ge(fr2.h, fr20 + j - 1)
                        for jj in range(4):
                            dt_i = q * 4 + jj
                            tr = t.transpose(pst[j % 2][:, jj * 128:(jj + 1) * 128],
                                             xn_t[i % xn_nb][:, dt_i * 128:(dt_i + 1) * 128], c.k_identb[:, :])
                        mm.inc(tr, 1)
            blk.tensor(tbody)
            assert mm.n == mm0 + 64

        if save_xn0:
            fr2n = fr2.n

            def v3body(v):
                v.wait_ge(fr2.h, fr2n)
                src = xnT[:, :].rearrange("p (a b) -> p a b", b=TOK)[:, :, 0]
                cp = v.tensor_copy(c.xn0[:, :], src)
                ev.inc(cp, 1)
            blk.vector(v3body)

        if c.debug and tag == "lnA":
            evn, scn, dwn = ev.n, sc.n, c.dw.n

            def dbody(s):
                s.wait_ge(ev.h, evn)
                s.wait_ge(sc.h, scn)
                s.wait_ge(c.dw.h, dwn)
                sd.dma(s.dma_start(out=c.dbg["stats"][:, 0:40], in_=stats[:, :]))
                if absrow is not None:
                    sd.dma(s.dma_start(out=c.dbg["stats"][:, 40:48], in_=absrow[:, :]))
                if xnT is not None:
                    sd.dma(s.dma_start(out=c.dbg["xnT"][:, :], in_=xnT[:, 0:1024]))
            blk.sync(dbody)
        if end_barrier:
            barrier(c)


def _p_lnabs(c, x_dram, absrow, tag):
    nc, blk = c.nc, c.blk
    ev, sc, sc2, fr = c.ev, c.sc, c.sc2, c.fr
    dw = c.dw
    lx = [c.DS(f"ln_lx{i}") for i in range(2)]

    with contextlib.ExitStack() as es:
        xt = [es.enter_context(nc.sbuf_tensor(f"{tag}_x{i}", [128, DM], F32)) for i in range(2)]
        stats = es.enter_context(nc.sbuf_tensor(f"{tag}_st", [128, 40], F32))

        xv = x_dram.rearrange("(n p) d -> n p d", p=128)
        fr0 = fr.n
        lxm = {}

        def sbody(s):
            for i in range(8):
                if i >= 2:
                    s.wait_ge(fr.h, fr0 + i - 1)
                lxm[i] = dslot(s, lx[i % 2], xt[i % 2][:, :], xv[i])
        blk.sync(sbody)

        sc0 = sc.n
        ev0 = ev.n
        sc20 = sc2.n

        def scbody(se):
            for i in range(8):
                se.wait_ge(lx[i % 2].h, lxm[i])
                a1 = se.activation(xt[i % 2][:, :], xt[i % 2][:, :], AF.Identity,
                                   accum_out=stats[:, i:i + 1])
                a1.then_inc(sc.h, 1)
                se.wait_ge(ev.h, ev0 + 2 * i + 1)
                a5 = se.activation(xt[i % 2][:, :], xt[i % 2][:, :], AF.Abs,
                                   bias=stats[:, 16 + i:17 + i],
                                   accum_out=stats[:, 24 + i:25 + i])
                dw.inc(a5, 1)
                se.wait_ge(dw.h, dw.n)
                a6 = se.activation(xt[i % 2][:, :], xt[i % 2][:, :], AF.Square,
                                   accum_out=stats[:, 8 + i:9 + i])
                a6.then_inc(sc.h, 1)
                se.wait_ge(ev.h, ev0 + 2 * i + 2)
                a3 = se.activation(stats[:, 32 + i:33 + i], stats[:, 16 + i:17 + i], AF.Ln,
                                   bias=c.k_eps[:, :])
                fr.inc(a3, 1)
                se.wait_ge(fr.h, fr0 + i + 1)
                a4 = se.activation(stats[:, 32 + i:33 + i], stats[:, 32 + i:33 + i], AF.Exp,
                                   scale=-0.5)
                a4.then_inc(sc2.h, 1)
        sc.n += 16
        sc2.n += 8
        blk.scalar(scbody)

        def vbody(v):
            for i in range(8):
                v.wait_ge(sc.h, sc0 + 2 * i + 1)
                ng = v.tensor_scalar(stats[:, 16 + i:17 + i], stats[:, i:i + 1], -1.0 / DM, None, ALU.mult)
                ev.inc(ng, 1)
                v.wait_ge(sc.h, sc0 + 2 * i + 2)
                vr = v.tensor_scalar(stats[:, 16 + i:17 + i], stats[:, 8 + i:9 + i], 1.0 / DM, None, ALU.mult)
                ev.inc(vr, 1)
            v.wait_ge(sc2.h, sc20 + 8)
            v.drain()
            ab = v.tensor_tensor(absrow[:, :], stats[:, 24:32], stats[:, 32:40], ALU.mult)
            dw.inc(ab, 1)
        blk.vector(vbody)


def _p_front(c, xnT, dtbuf, tag, lite=False):
    nc, blk = c.nc, c.blk
    mm, ev, sc, sc2 = c.mm, c.ev, c.sc, c.sc2
    fr, fr2, fr3 = c.fr, c.fr2, c.fr3
    wsem = [c.DS(f"{tag}_w{i}") for i in range(3)]
    hsem = [c.DS(f"{tag}_h{i}") for i in range(4)]
    xsem = [c.DS(f"{tag}_x{i}") for i in range(2)]
    sd = c.DS("dbg", serial=False)

    fts = [ft for ft in range(FT_ALL) if not (lite and FT_X + GR <= ft < FT_CONV)]
    c_tiles = [ft for ft in range(FT_ALL) if lite and FT_X + GR <= ft < FT_CONV]
    NFT = len(fts)
    NIT = 2 * NFT

    def it_ft(i):
        return fts[i // 2], i % 2

    with contextlib.ExitStack() as es:
        wsb = [es.enter_context(nc.sbuf_tensor(f"{tag}_w{i}", [128, KD * 128], BF16)) for i in range(3)]
        sraw = [es.enter_context(nc.sbuf_tensor(f"{tag}_sr{i}", [128, 515], BF16)) for i in range(2)]
        ssil = [es.enter_context(nc.sbuf_tensor(f"{tag}_ss{i}", [128, 512], BF16)) for i in range(4)]
        cvt = [es.enter_context(nc.sbuf_tensor(f"{tag}_cv{i}", [128, 512], F32)) for i in range(3)]
        stg = [es.enter_context(nc.sbuf_tensor(f"{tag}_sg{i}", [128, 512], BF16)) for i in range(2)]
        sptmp = es.enter_context(nc.sbuf_tensor(f"{tag}_sp", [128, TOK], F32))
        psm = [es.enter_context(nc.psum_tensor(f"{tag}_ps{i}", [128, 512], F32)) for i in range(2)]
        pst = [es.enter_context(nc.psum_tensor(f"{tag}_pt{i}", [128, 512], BF16)) for i in range(2)]

        fr0 = fr.n
        mm0 = mm.n
        sc0 = sc.n
        ev0 = ev.n
        sc20 = sc2.n
        fr20 = fr2.n
        fr30 = fr3.n

        conv_idx = {}
        ci = 0
        for i in range(NIT):
            ft, b = it_ft(i)
            if ft < FT_CONV:
                conv_idx[i] = ci
                ci += 1
        NCONV = ci
        xsect_idx = {}
        xi = 0
        for i in range(NIT):
            ft, b = it_ft(i)
            if ft < FT_X:
                xsect_idx[i] = xi
                xi += 1
        NX = xi

        c._fe_xtok_mark = getattr(c, "_fe_xtok_mark", {})
        xs_base = [xsem[0].n, xsem[1].n]
        for j in range(NIT):
            if j in xsect_idx:
                xj = xsect_idx[j]
                c._fe_xtok_mark[(tag, xj)] = xs_base[xj % 2] + 16 * (xj // 2 + 1)

        wm = {}

        def gbody(g):
            for fi, ft in enumerate(fts):
                if fi >= 3:
                    g.wait_ge(fr.h, fr0 + fi - 2)
                wm[ft] = dslot(g, wsem[fi % 3], wsb[fi % 3][:, :].rearrange("p (a b) -> p a b", b=128),
                               c.whbc[ft], max_dma_last_dim=8192)
        blk.gpsimd(gbody)

        def tbody(t):
            done_tr = []
            for i in range(NIT):
                ft, b = it_ft(i)
                fi = i // 2
                if b == 0:
                    t.wait_ge(wsem[fi % 3].h, wm[ft])
                if i >= 2:
                    t.wait_ge(sc.h, sc0 + i - 1)
                for k in range(KD):
                    last = t.matmul(psm[i % 2][:, :], wsb[fi % 3][:, k * 128:(k + 1) * 128],
                                    xnT[:, :].rearrange("p (a b) -> p a b", b=TOK)[:, k, b * 512:(b + 1) * 512],
                                    start=(k == 0), stop=(k == KD - 1))
                    if k == 0 and b == 0 and fi >= 1:
                        fr.inc(last, 1)
                mm.inc(last, 1)
                j = i - 3
                if j >= 0 and j in xsect_idx:
                    xj = xsect_idx[j]
                    t.wait_ge(sc2.h, sc20 + conv_idx[j] + 1)
                    if xj >= 2:
                        t.wait_ge(fr3.h, fr30 + xj - 1)
                    for q in range(4):
                        tr = t.transpose(pst[xj % 2][:, q * 128:(q + 1) * 128],
                                         ssil[j % 4][:, q * 128:(q + 1) * 128], c.k_identb[:, :])
                    mm.inc(tr, 1)
                    done_tr.append(j)
            for j in range(NIT - 3, NIT):
                if j in xsect_idx and j not in done_tr:
                    xj = xsect_idx[j]
                    t.wait_ge(sc2.h, sc20 + conv_idx[j] + 1)
                    if xj >= 2:
                        t.wait_ge(fr3.h, fr30 + xj - 1)
                    for q in range(4):
                        tr = t.transpose(pst[xj % 2][:, q * 128:(q + 1) * 128],
                                         ssil[j % 4][:, q * 128:(q + 1) * 128], c.k_identb[:, :])
                    mm.inc(tr, 1)
            t.drain()
            fr.inc(t.nop(), 1)
        blk.tensor(tbody)
        assert fr.n == fr0 + NFT
        mm_of_i = {}
        tr_of_j = {}
        cnt = 0
        done_tr2 = []
        for i in range(NIT):
            cnt += 1
            mm_of_i[i] = mm0 + cnt
            j = i - 3
            if j >= 0 and j in xsect_idx:
                cnt += 1
                tr_of_j[j] = mm0 + cnt
                done_tr2.append(j)
        for j in range(NIT - 3, NIT):
            if j in xsect_idx and j not in done_tr2:
                cnt += 1
                tr_of_j[j] = mm0 + cnt
        assert mm.n == mm0 + cnt

        def scbody(se):
            for i in range(NIT):
                ft, b = it_ft(i)
                se.wait_ge(mm.h, mm_of_i[i])
                if ft < FT_CONV:
                    if i >= 2:
                        se.wait_ge(ev.h, ev0 + conv_idx.get(i - 2, 0) + 1)
                    cp = se.copy(sraw[i % 2][:, 3:515], psm[i % 2][:, :])
                    cp.then_inc(sc.h, 1)
                else:
                    a = se.activation(dtbuf[:, b * 512:(b + 1) * 512], psm[i % 2][:, :],
                                      AF.Identity, bias=c.k_dtb[:, :])
                    a.then_inc(sc.h, 1)
                j = i - 2
                if j >= 0 and j in conv_idx:
                    cj = conv_idx[j]
                    se.wait_ge(ev.h, ev0 + cj + 1)
                    if cj >= 4:
                        se.wait_ge(fr2.h, fr20 + cj - 3)
                    ftj = j // 2
                    a = se.activation(ssil[j % 4][:, :], cvt[cj % 3][:, :], AF.Silu,
                                      bias=c.k_convb[:, ftj:ftj + 1])
                    a.then_inc(sc2.h, 1)
            for j in range(NIT - 2, NIT):
                if j in conv_idx:
                    cj = conv_idx[j]
                    se.wait_ge(ev.h, ev0 + cj + 1)
                    if cj >= 4:
                        se.wait_ge(fr2.h, fr20 + cj - 3)
                    ftj = j // 2
                    a = se.activation(ssil[j % 4][:, :], cvt[cj % 3][:, :], AF.Silu,
                                      bias=c.k_convb[:, ftj:ftj + 1])
                    a.then_inc(sc2.h, 1)
        sc.n += NIT
        sc2.n += NCONV
        blk.scalar(scbody)

        tails = c.convtl

        def vbody(v):
            if tag == "feA":
                v.memset(tails[:, :], 0.0)
            else:
                v.tensor_scalar(tails[:, :], tails[:, :], c.k_flag[:, :], None, ALU.mult)
            v.drain()
            for i in range(NIT):
                ft, b = it_ft(i)
                if ft < FT_CONV:
                    cj = conv_idx[i]
                    sbuf = sraw[i % 2]
                    v.tensor_copy(sbuf[:, 0:3], tails[:, ft * 3:ft * 3 + 3])
                    v.wait_ge(sc.h, sc0 + i + 1)
                    v.drain()
                    v.tensor_copy(tails[:, ft * 3:ft * 3 + 3], sbuf[:, 512:515])
                    w = lambda k: c.k_convw[:, (ft * 4 + k):(ft * 4 + k + 1)]
                    cb = cvt[cj % 3]
                    if cj >= 2:
                        v.wait_ge(sc2.h, sc20 + cj - 1)
                    v.tensor_scalar(cb[:, :], sbuf[:, 0:512], w(0), None, ALU.mult)
                    for k in range(1, 4):
                        v.drain()
                        op = v.scalar_tensor_tensor(cb[:, :], sbuf[:, k:512 + k], w(k), cb[:, :],
                                                    ALU.mult, ALU.add)
                    ev.inc(op, 1)
                j = i - 3
                if j >= 0 and j in xsect_idx:
                    xj = xsect_idx[j]
                    v.wait_ge(mm.h, tr_of_j[j])
                    if xj >= 2:
                        v.wait_ge(xsem[xj % 2].h, c._fe_xtok_mark[(tag, xj - 2)])
                    v.drain()
                    cp = v.tensor_copy(stg[xj % 2][:, :], pst[xj % 2][:, :])
                    fr3.inc(cp, 1)
        blk.vector(vbody)
        assert ev.n == ev0 + NCONV

        hm = {}

        def sbody(s):
            for i in range(NIT):
                ft, b = it_ft(i)
                if ft < FT_CONV and (not lite or ft >= FT_X):
                    cj = conv_idx[i]
                    s.wait_ge(sc2.h, sc20 + cj + 1)
                    dst = c.sc_hbc[b * 4:(b + 1) * 4, :, ft, :].rearrange("a f t -> f a t")
                    src = ssil[i % 4][:, :].rearrange("p (a b) -> p a b", b=128)
                    hm[i] = dslot(s, hsem[i % 4], dst, src)
                j = i - 3
                if j >= 0 and j in xsect_idx:
                    xj = xsect_idx[j]
                    s.wait_ge(fr3.h, fr30 + xj + 1)
                    ftj, bj = it_ft(j)
                    dst = c.sc_xtok[bj * 4:(bj + 1) * 4, :, ftj, :].rearrange("q t f -> t q f")
                    src = stg[xj % 2][:, :].rearrange("p (a b) -> p a b", b=128)
                    xmark = dslot(s, xsem[xj % 2], dst, src)
                    assert xmark == c._fe_xtok_mark[(tag, xj)]
                jj = i - 3
                if jj >= 0 and jj in conv_idx:
                    if jj in hm:
                        s.wait_ge(hsem[jj % 4].h, hm[jj])
                    if jj in xsect_idx:
                        s.wait_ge(mm.h, tr_of_j[jj])
                    fr2.inc(s.nop(), 1)
            for j in range(NIT - 3, NIT):
                if j in xsect_idx:
                    xj = xsect_idx[j]
                    s.wait_ge(fr3.h, fr30 + xj + 1)
                    ftj, bj = it_ft(j)
                    dst = c.sc_xtok[bj * 4:(bj + 1) * 4, :, ftj, :].rearrange("q t f -> t q f")
                    src = stg[xj % 2][:, :].rearrange("p (a b) -> p a b", b=128)
                    xmark = dslot(s, xsem[xj % 2], dst, src)
                    assert xmark == c._fe_xtok_mark[(tag, xj)]
                if j in conv_idx:
                    if j in hm:
                        s.wait_ge(hsem[j % 4].h, hm[j])
                    if j in xsect_idx:
                        s.wait_ge(mm.h, tr_of_j[j])
                    fr2.inc(s.nop(), 1)
        blk.sync(sbody)
        assert fr2.n == fr20 + NCONV

        if lite:
            wcs = wsem
            frC0 = fr.n
            scC0 = sc0 + NIT
            wcm = {}

            def gcbody(g):
                for idx, ft in enumerate(c_tiles):
                    g.wait_ge(fr.h, frC0 + (idx - 1 if idx >= 2 else 0))
                    wcm[idx] = dslot(g, wcs[idx % 2], wsb[idx % 2][:, :].rearrange("p (a b) -> p a b", b=128),
                                     c.whbc[ft], max_dma_last_dim=8192)
            blk.gpsimd(gcbody)
            mmC = {}

            def tcbody(t):
                for idx, ft in enumerate(c_tiles):
                    t.wait_ge(wcs[idx % 2].h, wcm[idx])
                    if idx == 0:
                        t.wait_ge(sc.h, sc0 + NIT)
                    if idx >= 2:
                        t.wait_ge(sc.h, scC0 + idx - 1)
                    for k in range(KD):
                        last = t.matmul(psm[idx % 2][:, 0:3], wsb[idx % 2][:, k * 128:(k + 1) * 128],
                                        xnT[:, :].rearrange("p (a b) -> p a b", b=TOK)[:, k, TOK - 3:TOK],
                                        start=(k == 0), stop=(k == KD - 1))
                        if k == 0 and idx >= 1:
                            fr.inc(last, 1)
                    mm.inc(last, 1)
                    mmC[idx] = mm.n
                t.drain()
                fr.inc(t.nop(), 1)
            blk.tensor(tcbody)

            def scbody2(se):
                for idx, ft in enumerate(c_tiles):
                    se.wait_ge(mm.h, mmC[idx])
                    if idx == 0:
                        se.wait_ge(ev.h, ev0 + 1)
                    cp = se.copy(tails[:, ft * 3:ft * 3 + 3], psm[idx % 2][:, 0:3])
                    sc.inc(cp, 1)
            blk.scalar(scbody2)

        scn = sc.n
        sc2b = sc2.n

        def s4body(se):
            se.wait_ge(sc.h, scn)
            dtf = dtbuf[:, 0:1024]
            adt = dtbuf[:, 1024:2048]
            tmp = sptmp[:, :]
            a1 = se.activation(tmp, dtf, AF.Abs)
            a1.then_inc(sc2.h, 1)
            se.wait_ge(sc2.h, sc2b + 1)
            a2 = se.activation(tmp, tmp, AF.Exp, scale=-1.0)
            a2.then_inc(sc2.h, 1)
            se.wait_ge(sc2.h, sc2b + 2)
            a3 = se.activation(adt, tmp, AF.Ln, bias=1.0)
            a3.then_inc(sc.h, 1)
        sc2.n += 2
        sc.n += 1
        blk.scalar(s4body)
        scn2 = sc.n

        def v3body(v):
            v.wait_ge(sc.h, scn2)
            dtf = dtbuf[:, 0:1024]
            adt = dtbuf[:, 1024:2048]
            tmp = sptmp[:, :]
            v.tensor_scalar(tmp, dtf, 0.0, None, ALU.max)
            v.drain()
            v.tensor_tensor(dtf, tmp, adt, ALU.add)
            v.drain()
            ts = v.tensor_scalar(adt, dtf, c.k_acol[:, :], None, ALU.mult)
            ev.inc(ts, 1)
        blk.vector(v3body)

        if c.debug and tag == "feA":
            evn = ev.n
            sdn = sd.n

            def dbody(s):
                s.wait_ge(ev.h, evn)
                s.wait_ge(sd.h, sdn)
                sd.dma(s.dma_start(out=c.dbg["dtA"][:, :], in_=dtbuf[:, 0:2048]))
            blk.sync(dbody)
        barrier(c)


def _p_chunks(c, dtbuf, variant, tag, hook=None):
    nc, blk = c.nc, c.blk
    mm, ev, sc2 = c.mm, c.ev, c.sc2
    isB = variant == "B"
    sem_h2 = c.DS(f"{tag}_hb2")
    sem_x = c.DS(f"{tag}_xt")
    sem_x2 = c.DS(f"{tag}_xt2")
    sem_cr = c.DS(f"{tag}_cr")
    if isB:
        sem_h = c.DS(f"{tag}_hb")
        sem_g, sem_g2 = c.DS(f"{tag}_sg"), c.DS(f"{tag}_sg2")
        sem_rwh, sem_rwl = c.DS(f"{tag}_rwh"), c.DS(f"{tag}_rwl")
        sem_rrh, sem_rrl = c.DS(f"{tag}_rrh"), c.DS(f"{tag}_rrl")
        sem_z = [c.DS(f"{tag}_z{i}") for i in range(2)]
    sd = c.DS("dbg", serial=False)

    with contextlib.ExitStack() as es:
        run = es.enter_context(nc.sbuf_tensor(f"{tag}_run", [128, INTER], F32))
        hbc = es.enter_context(nc.sbuf_tensor(f"{tag}_hbc", [128, CONVD if isB else GR * 128], BF16))
        xtk = es.enter_context(nc.sbuf_tensor(f"{tag}_xtk", [128, INTER], BF16))
        xdd = es.enter_context(nc.sbuf_tensor(f"{tag}_xdd", [128, INTER], BF16))
        btok = es.enter_context(nc.sbuf_tensor(f"{tag}_btok", [128, GR * 128], BF16))
        dts = es.enter_context(nc.sbuf_tensor(f"{tag}_dts", [128, 9 * 128], F32))
        psmisc = es.enter_context(nc.psum_tensor(f"{tag}_pc", [128, 512], F32))
        psSt = es.enter_context(nc.psum_tensor(f"{tag}_pq", [128, 512], F32))
        psM = [es.enter_context(nc.psum_tensor(f"{tag}_pm{i}", [128, 512], F32)) for i in range(2)]
        psY1 = [es.enter_context(nc.psum_tensor(f"{tag}_py{i}", [128, 512], F32)) for i in range(2)]
        L = dict(run=run, hbc=hbc, xtk=xtk, xdd=xdd, btok=btok, dts=dts,
                 psmisc=psmisc, psSt=psSt, psM=psM, psY1=psY1, dtbuf=dtbuf)
        if isB:
            L["psY2"] = [es.enter_context(nc.psum_tensor(f"{tag}_pz{i}", [128, 512], F32)) for i in range(2)]
            L["bias_hi"] = es.enter_context(nc.sbuf_tensor(f"{tag}_bh", [128, 128], BF16))
            L["bias_lo"] = es.enter_context(nc.sbuf_tensor(f"{tag}_bl", [128, 128], BF16))
            L["hi_act"] = es.enter_context(nc.sbuf_tensor(f"{tag}_ha", [128, 128], BF16))
            L["lo_act"] = es.enter_context(nc.sbuf_tensor(f"{tag}_la", [128, 128], BF16))
            L["rows"] = [es.enter_context(nc.sbuf_tensor(f"{tag}_rw{i}", [2, 4096], BF16)) for i in range(2)]
            L["gm"] = es.enter_context(nc.sbuf_tensor(f"{tag}_gm", [128, GR * 128], BF16))
            L["mexp"] = [es.enter_context(nc.sbuf_tensor(f"{tag}_me{i}", [128, 512], F32)) for i in range(2)]
            L["mbuf"] = es.enter_context(nc.sbuf_tensor(f"{tag}_mb", [128, 4 * 512], BF16))
            L["stbf"] = es.enter_context(nc.sbuf_tensor(f"{tag}_sb", [128, INTER], BF16))
            L["sg"] = es.enter_context(nc.sbuf_tensor(f"{tag}_sgc", [128, INTER], BF16))
            L["ytok"] = es.enter_context(nc.sbuf_tensor(f"{tag}_y", [128, INTER], BF16))
            L["zstg"] = [es.enter_context(nc.sbuf_tensor(f"{tag}_zs{i}", [128, 512], BF16)) for i in range(2)]
            L["sumsq"] = es.enter_context(nc.sbuf_tensor(f"{tag}_sq", [128, 1], F32))
            L["rstd"] = es.enter_context(nc.sbuf_tensor(f"{tag}_rs", [128, 1], F32))
            L["dident"] = es.enter_context(nc.sbuf_tensor(f"{tag}_di", [128, HD * PH], BF16))

        c._marks = getattr(c, "_marks", {})
        mk = c._marks

        if isB:
            def ldc(s):
                dslot(s, sem_cr, run[:, :], c.sc_carry)
            blk.sync(ldc)
            mk[(tag, "init")] = sem_cr.n

            def vdid(v):
                in1 = c.k_i64x2[:, :].unsqueeze(1).broadcast_to([128, 128, 64])
                in2 = c.dbc[:, :].unsqueeze(2).broadcast_to([128, 128, 64])
                tt = v.tensor_tensor(L["dident"][:, :].rearrange("p (h q) -> p h q", q=64), in1, in2, ALU.mult)
                ev.inc(tt, 1)
            blk.vector(vdid)
            mk[(tag, "didev")] = ev.n
        else:
            def vinit(v):
                ms = v.memset(run[:, :], 0.0)
                ev.inc(ms, 1)
            blk.vector(vinit)
            mk[(tag, "initev")] = ev.n

        for ch in range(NCH):
            _chunk_one(c, L, ch, variant, tag)
            if hook is not None:
                hook(ch)
        if isB:
            _chunk_ztail(c, L, NCH - 1, tag)

        if not isB:
            prev_end = mk[(tag, "headev", NCH - 1)]

            def carrybody(v):
                v.wait_ge(ev.h, prev_end)
                ts = v.tensor_scalar(run[:, :], run[:, :], c.k_flag[:, :], None, ALU.mult)
                ev.inc(ts, 1)
            blk.vector(carrybody)
            evn = ev.n

            def spill(s):
                s.wait_ge(ev.h, evn)
                m = dslot(s, sem_cr, c.sc_carry, run[:, :])
                s.wait_ge(sem_cr.h, m)
            blk.sync(spill)
            if c.debug:
                def dbody(s):
                    sd.dma(s.dma_start(out=c.dbg["carry"], in_=run[:, :]))
                    sd.dma(s.dma_start(out=c.dbg["hbc0"][:, FT_X * 128:(FT_X + GR) * 128],
                                       in_=hbc[:, 0:GR * 128]))
                    sd.dma(s.dma_start(out=c.dbg["xtok0"][:, :], in_=xtk[:, :]))
                blk.sync(dbody)
        barrier(c)


def _chunk_ztail(c, L, ch, tag):
    nc, blk = c.nc, c.blk
    mm, ev = c.mm, c.ev
    sem_z = [c.DS(f"{tag}_z{i}") for i in range(2)]
    psY1 = L["psY1"]
    mk = c._marks
    ev_z = mk[(tag, "evz", ch)]

    mmZ0 = mm.n
    evZ0 = ev.n
    zm = {q: sem_z[q % 2].n + 16 * (q // 2 + 1) for q in range(16)}

    def t6body(t):
        t.wait_ge(ev.h, ev_z)
        for q in range(FT_X // 4):
            if q >= 2:
                t.wait_ge(ev.h, evZ0 + q - 1)
            for j in range(4):
                ft = q * 4 + j
                tr = t.transpose(psY1[q % 2][:, j * 64:(j + 1) * 64].bitcast(BF16),
                                 L["ytok"][:, ft * 128:(ft + 1) * 128], c.k_identb[:, :])
            mm.inc(tr, 1)
    blk.tensor(t6body)

    def v9body(v):
        for q in range(FT_X // 4):
            v.wait_ge(mm.h, mmZ0 + q + 1)
            if q < 2 and ch >= 1:
                v.wait_ge(sem_z[q % 2].h, mk[(tag, "zspill%d" % (q % 2), ch - 1)])
            if q >= 2:
                v.wait_ge(sem_z[q % 2].h, zm[q - 2])
            for j in range(4):
                ft = q * 4 + j
                ts = v.tensor_scalar(L["zstg"][q % 2][:, j * 128:(j + 1) * 128],
                                     psY1[q % 2][:, j * 64:(j + 1) * 64].bitcast(BF16),
                                     c.k_nw[:, ft:ft + 1], None, ALU.mult)
            ev.inc(ts, 1)
    blk.vector(v9body)
    mk[(tag, "zstg", ch)] = ev.n

    def s7body(s):
        for q in range(FT_X // 4):
            s.wait_ge(ev.h, evZ0 + q + 1)
            m = dslot(s, sem_z[q % 2], c.sc_z[ch][:, q * 512:(q + 1) * 512], L["zstg"][q % 2][:, :])
            assert m == zm[q]
    blk.sync(s7body)
    mk[(tag, "zspill0", ch)] = zm[14]
    mk[(tag, "zspill1", ch)] = zm[15]


def _chunk_one(c, L, ch, variant, tag):
    nc, blk = c.nc, c.blk
    mm, ev, sc2 = c.mm, c.ev, c.sc2
    isB = variant == "B"
    sem_h2, sem_x, sem_x2 = c.DS(f"{tag}_hb2"), c.DS(f"{tag}_xt"), c.DS(f"{tag}_xt2")
    sem_cr = c.DS(f"{tag}_cr")
    if isB:
        sem_h = c.DS(f"{tag}_hb")
        sem_g, sem_g2 = c.DS(f"{tag}_sg"), c.DS(f"{tag}_sg2")
        sem_rwh, sem_rwl = c.DS(f"{tag}_rwh"), c.DS(f"{tag}_rwl")
        sem_rrh, sem_rrl = c.DS(f"{tag}_rrh"), c.DS(f"{tag}_rrl")
        sem_z = [c.DS(f"{tag}_z{i}") for i in range(2)]
    sd = c.DS("dbg", serial=False)
    run, hbc, xtk, xdd, btok, dts = L["run"], L["hbc"], L["xtk"], L["xdd"], L["btok"], L["dts"]
    psmisc, psSt, psM, psY1 = L["psmisc"], L["psSt"], L["psM"], L["psY1"]
    dtbuf = L["dtbuf"]
    col = lambda i: dts[:, i * 128:(i + 1) * 128]
    ADT, CUMT, DTT, CUMA, SB, EXPS, D1, DDT, LNDT = range(9)
    EXPC, BIAS, SCR = ADT, LNDT, D1
    rb = ch % 2
    mk = c._marks

    rl = {}

    def sbody(s):
        if ch >= 1:
            s.wait_ge(mm.h, mk[(tag, "headmm", ch - 1)])
            s.wait_ge(ev.h, mk[(tag, "xddev", ch - 1)])
        if isB:
            rl["h"] = dslot(s, sem_h, hbc[:, 0:40 * 128].rearrange("p (a b) -> p a b", b=128),
                            c.sc_hbc[ch][:, 0:40, :])
            rl["h2"] = dslot(s, sem_h2, hbc[:, 40 * 128:].rearrange("p (a b) -> p a b", b=128),
                             c.sc_hbc[ch][:, 40:, :])
        else:
            rl["h2"] = dslot(s, sem_h2, hbc[:, :].rearrange("p (a b) -> p a b", b=128),
                             c.sc_hbc[ch][:, FT_X:FT_X + GR, :])
            rl["h"] = rl["h2"]
        rl["x"] = dslot(s, sem_x, xtk[:, 0:4096].rearrange("p (a b) -> p a b", b=128),
                        c.sc_xtok[ch][:, 0:32, :])
        rl["x2"] = dslot(s, sem_x2, xtk[:, 4096:].rearrange("p (a b) -> p a b", b=128),
                         c.sc_xtok[ch][:, 32:, :])
        if isB:
            if ch >= 1:
                s.wait_ge(ev.h, mk[(tag, "zdone", ch - 1)])
            rl["g"] = dslot(s, sem_g, L["sg"][:, 0:4096], c.sc_sg[ch * 128:(ch + 1) * 128, 0:4096])
            rl["g2"] = dslot(s, sem_g2, L["sg"][:, 4096:], c.sc_sg[ch * 128:(ch + 1) * 128, 4096:])
    blk.sync(sbody)

    mm0 = mm.n
    ev0 = ev.n

    def tbody(t):
        if ch >= 1:
            t.wait_ge(ev.h, mk[(tag, "headev", ch - 1)])
        elif isB:
            t.wait_ge(sem_cr.h, mk[(tag, "init")])
        t1 = t.transpose(psmisc[:, 0:128], dtbuf[:, 1024 + ch * 128:1024 + (ch + 1) * 128], c.k_identf[:, :])
        mm.inc(t1, 1)
        t.wait_ge(ev.h, ev0 + 1)
        t2 = t.matmul(psmisc[:, 128:256], c.k_utf[:, :], col(ADT), start=True, stop=True)
        mm.inc(t2, 1)
        t3 = t.transpose(psmisc[:, 256:384], dtbuf[:, ch * 128:(ch + 1) * 128], c.k_identf[:, :])
        mm.inc(t3, 1)
        t.wait_ge(ev.h, ev0 + 2)
        t4 = t.transpose(psmisc[:, 384:512], col(CUMT), c.k_identf[:, :])
        mm.inc(t4, 1)
        t.wait_ge(ev.h, ev0 + 3)
        t5 = t.transpose(psmisc[0:1, 0:128], col(CUMA)[:, 127:128], c.k_identf[:, :])
        mm.inc(t5, 1)
        t.wait_ge(ev.h, ev0 + 4)
        t6 = t.matmul(psmisc[:, 128:256], c.k_onesrf[0:1, 0:128], col(SB)[0:1, 0:128], start=True, stop=True)
        mm.inc(t6, 1)
    blk.tensor(tbody)

    def vbody(v):
        v.wait_ge(mm.h, mm0 + 1)
        cp = v.tensor_copy(col(ADT), psmisc[:, 0:128])
        ev.inc(cp, 1)
        v.wait_ge(mm.h, mm0 + 3)
        v.tensor_copy(col(CUMT), psmisc[:, 128:256])
        cp = v.tensor_copy(col(DTT), psmisc[:, 256:384])
        ev.inc(cp, 1)
        v.wait_ge(mm.h, mm0 + 4)
        cp = v.tensor_copy(col(CUMA), psmisc[:, 384:512])
        ev.inc(cp, 1)
        v.wait_ge(mm.h, mm0 + 5)
        cp = v.tensor_copy(col(SB)[0:1, 0:128], psmisc[0:1, 0:128])
        ev.inc(cp, 1)
        v.wait_ge(mm.h, mm0 + 6)
        v.tensor_copy(col(SB), psmisc[:, 128:256])
        v.drain()
        tt = v.tensor_tensor(col(D1), col(SB), col(CUMT), ALU.subtract)
        ev.inc(tt, 1)
    blk.vector(vbody)
    evA = ev.n

    sc20 = sc2.n

    def scbody(se):
        se.wait_ge(ev.h, evA)
        a1 = se.activation(col(EXPS), col(SB), AF.Exp)
        a1.then_inc(sc2.h, 1)
        a2 = se.activation(col(D1), col(D1), AF.Exp)
        a2.then_inc(sc2.h, 1)
        if isB:
            a3 = se.activation(col(EXPC), col(CUMT), AF.Exp)
            a3.then_inc(sc2.h, 1)
            a4 = se.activation(col(LNDT), col(DTT), AF.Ln)
            a4.then_inc(sc2.h, 1)
    sc2.n += 4 if isB else 2
    blk.scalar(scbody)
    sc2A = sc2.n

    def v2body(v):
        v.wait_ge(sc2.h, sc2A)
        if isB and ch >= 1:
            v.wait_ge(mm.h, mk[(tag, "Mdone", ch - 1)])
        tt = v.tensor_tensor(col(DDT), col(D1), col(DTT), ALU.mult)
        ev.inc(tt, 1)
        if isB:
            v.tensor_tensor(col(BIAS), col(LNDT), col(CUMT), ALU.subtract)
            v.drain()
            v.tensor_copy(L["bias_hi"][:, :], col(BIAS))
            v.drain()
            v.tensor_tensor(col(SCR), col(BIAS), L["bias_hi"][:, :], ALU.subtract)
            v.drain()
            v.tensor_copy(L["bias_lo"][:, :], col(SCR))
            v.tensor_copy(L["hi_act"][:, :], col(CUMA))
            v.drain()
            v.tensor_tensor(col(SCR), col(CUMA), L["hi_act"][:, :], ALU.subtract)
            v.drain()
            cp = v.tensor_copy(L["lo_act"][:, :], col(SCR))
            ev.inc(cp, 1)
    blk.vector(v2body)
    mk[(tag, "dtev", ch)] = ev.n
    ev_dt = ev.n

    if isB:
        def rbody(s):
            s.wait_ge(ev.h, ev_dt)
            if ch >= 1:
                s.wait_ge(mm.h, mk[(tag, "Mdone", ch - 1)])
            mk[(tag, "rwh", ch)] = dslot(s, sem_rwh, c.sc_rows[0], L["hi_act"][:, :])
            mk[(tag, "rwl", ch)] = dslot(s, sem_rwl, c.sc_rows[1], L["lo_act"][:, :])
        blk.sync(rbody)

    mmB0 = mm.n

    def t2body(t):
        t.wait_ge(sem_h2.h, rl["h2"])
        if ch >= 1:
            t.wait_ge(ev.h, mk[(tag, "headev", ch - 1)])
            t.wait_ge(ev.h, mk[(tag, "bpdone", ch - 1)])
        hoff = FT_X if isB else 0
        for g in range(GR):
            tr = t.transpose(psSt[:, g * 64:(g + 1) * 64].bitcast(BF16),
                             hbc[:, (hoff + g) * 128:(hoff + g + 1) * 128], c.k_identb[:, :])
        mm.inc(tr, 1)
    blk.tensor(t2body)

    def v3body(v):
        v.wait_ge(mm.h, mmB0 + 1)
        cp = v.tensor_copy(btok[:, :], psSt[:, :].bitcast(BF16))
        ev.inc(cp, 1)
        v.wait_ge(ev.h, ev_dt)
        v.wait_ge(sem_x.h, rl["x"])
        v.wait_ge(sem_x2.h, rl["x2"])
        if ch >= 1:
            v.wait_ge(mm.h, mk[(tag, "headmm", ch - 1)])
        in1 = xtk[:, :].rearrange("p (a b) -> p a b", b=64)
        in2 = col(DDT).unsqueeze(2).broadcast_to([128, 128, 64])
        tt = v.tensor_tensor(xdd[:, :].rearrange("p (a b) -> p a b", b=64), in1, in2, ALU.mult)
        ev.inc(tt, 1)
    blk.vector(v3body)
    mk[(tag, "bpdone", ch)] = ev.n
    mk[(tag, "xddev", ch)] = ev.n
    ev_bp = ev.n

    if isB:
        def v4body(v):
            if ch == 0:
                v.wait_ge(sem_cr.h, mk[(tag, "init")])
            cp = v.tensor_copy(L["stbf"][:, :], run[:, :])
            ev.inc(cp, 1)
        blk.vector(v4body)
        ev_stbf = ev.n

        mmG0 = mm.n
        evG0 = ev.n

        def t3body(t):
            t.wait_ge(sem_h2.h, rl["h2"])
            if ch >= 1:
                t.wait_ge(sc2.h, mk[(tag, "mexpsc", ch - 1)])
            for g in range(GR):
                if g >= 2:
                    t.wait_ge(ev.h, evG0 + g - 1)
                mmi = t.matmul(psM[g % 2][:, 0:128], hbc[:, (FT_X + g) * 128:(FT_X + g + 1) * 128],
                               hbc[:, (FT_X + GR + g) * 128:(FT_X + GR + g + 1) * 128],
                               start=True, stop=True)
                mm.inc(mmi, 1)
        blk.tensor(t3body)

        def v5body(v):
            for g in range(GR):
                v.wait_ge(mm.h, mmG0 + g + 1)
                cp = v.tensor_copy(L["gm"][:, g * 128:(g + 1) * 128], psM[g % 2][:, 0:128])
                ev.inc(cp, 1)
        blk.vector(v5body)
        ev_gm = ev.n


    if isB and ch >= 1:
        _chunk_ztail(c, L, ch - 1, tag)

    nsg = HD // 8
    piece_mm = {}
    group_mm = {}
    mmH0 = mm.n
    sc2P0 = sc2.n
    evP0 = None

    ev_sched = {}
    evc = ev.n
    if isB:
        for sgp in range(nsg):
            for p in (2 * sgp, 2 * sgp + 1):
                evc += 1
                ev_sched[("piece", p)] = evc
            evc += 1
            ev_sched[("grp", sgp)] = evc
    else:
        for sgp in range(nsg):
            evc += 1
            ev_sched[("grp", sgp)] = evc

    if isB:
        rows_mark = {q: (sem_rrh.n + 16 * (q + 1), sem_rrl.n + 16 * (q + 1)) for q in range(4)}

    def t5body(t):
        t.wait_ge(ev.h, ev_bp)
        if isB:
            t.wait_ge(sem_h.h, rl["h"])
            t.wait_ge(sem_h2.h, rl["h2"])
            t.wait_ge(ev.h, mk[(tag, "didev")])
            t.wait_ge(ev.h, ev_gm)
            t.wait_ge(ev.h, ev_stbf)
            if ch >= 1:
                t.wait_ge(ev.h, mk[(tag, "zstg", ch - 1)])
        def emit_piece(p):
            if p % 8 == 0:
                t.wait_ge(sem_rrh.h, rows_mark[p // 8][0])
                t.wait_ge(sem_rrl.h, rows_mark[p // 8][1])
            if p >= 2:
                t.wait_ge(sc2.h, sc2P0 + p - 1)
            h0loc = (p % 8) * 4
            pm = psM[p % 2]
            rw = L["rows"][(p // 8) % 2]
            t.matmul(pm[:, :], c.k_ones2[:, :], rw[0:2, h0loc * 128:(h0loc + 4) * 128],
                     start=True, stop=False)
            t.matmul(pm[:, :], c.k_identb[:, :],
                     c.k_addm[:, :].unsqueeze(1).broadcast_to([128, 4, 128]), start=False, stop=False)
            h0 = p * 4
            t.matmul(pm[:, :], c.k_identb[:, :],
                     L["bias_hi"][:, h0:h0 + 4].unsqueeze(2).broadcast_to([128, 4, 128]),
                     start=False, stop=False)
            last = t.matmul(pm[:, :], c.k_identb[:, :],
                            L["bias_lo"][:, h0:h0 + 4].unsqueeze(2).broadcast_to([128, 4, 128]),
                            start=False, stop=True)
            mm.inc(last, 1)
            piece_mm[p] = mm.n

        if isB:
            emit_piece(0)
            emit_piece(1)
        for sgp in range(nsg):
            if isB and sgp + 1 < nsg:
                emit_piece(2 * sgp + 2)
                emit_piece(2 * sgp + 3)
            stb = psSt if sgp % 2 == 0 else psmisc
            y1b = psY1[sgp % 2]
            y2b = L["psY2"][sgp % 2] if isB else None
            if isB:
                t.wait_ge(ev.h, ev_sched[("piece", 2 * sgp + 1)])
            if sgp >= 2:
                t.wait_ge(ev.h, ev_sched[("grp", sgp - 2)])
            g = sgp // 2
            last = t.matmul(stb[:, :], btok[:, g * 128:(g + 1) * 128],
                            xdd[:, sgp * 512:(sgp + 1) * 512], start=True, stop=True)
            if isB:
                for j in range(8):
                    h = sgp * 8 + j
                    stt = (j == 0)
                    base = (h % 2) * 64
                    mb = L["mbuf"][:, (2 * sgp % 4) * 512 + j * 128: (2 * sgp % 4) * 512 + (j + 1) * 128] \
                        if j < 4 else \
                        L["mbuf"][:, ((2 * sgp + 1) % 4) * 512 + (j - 4) * 128: ((2 * sgp + 1) % 4) * 512 + (j - 3) * 128]
                    t.matmul(y1b[:, j * 64:(j + 1) * 64], mb,
                             xtk[:, h * 64:(h + 1) * 64], start=stt, stop=False)
                    t.matmul(y1b[:, j * 64:(j + 1) * 64],
                             hbc[base:base + 64, (h // 2) * 128:(h // 2) * 128 + 128],
                             L["dident"][base:base + 64, h * 64:(h + 1) * 64],
                             start=False, stop=(j == 7))
                last = t.matmul(y2b[:, :],
                                hbc[:, (FT_X + GR + g) * 128:(FT_X + GR + g + 1) * 128],
                                L["stbf"][:, sgp * 512:(sgp + 1) * 512], start=True, stop=True)
            mm.inc(last, 1)
            group_mm[sgp] = mm.n
    blk.tensor(t5body)
    mk[(tag, "headmm", ch)] = mm.n

    if isB:
        def s5body(se):
            for p in range(KD):
                se.wait_ge(mm.h, piece_mm[p])
                if p >= 2:
                    se.wait_ge(ev.h, ev_sched[("piece", p - 2)])
                a = se.activation(L["mexp"][p % 2][:, :], psM[p % 2][:, :], AF.Exp)
                a.then_inc(sc2.h, 1)
        sc2.n += KD
        blk.scalar(s5body)
        mk[(tag, "mexpsc", ch)] = sc2.n

    def v7body(v):
        for sgp in range(nsg):
            if isB:
                for p in (2 * sgp, 2 * sgp + 1):
                    v.wait_ge(sc2.h, sc2P0 + p + 1)
                    if sgp >= 2:
                        v.wait_ge(mm.h, group_mm[sgp - 2])
                    g4 = p * 4 // 16
                    in2 = L["gm"][:, g4 * 128:(g4 + 1) * 128].unsqueeze(1).broadcast_to([128, 4, 128])
                    tt = v.tensor_tensor(L["mbuf"][:, (p % 4) * 512:(p % 4 + 1) * 512].rearrange("p (a b) -> p a b", b=128),
                                         L["mexp"][p % 2][:, :].rearrange("p (a b) -> p a b", b=128), in2, ALU.mult)
                    ev.inc(tt, 1)
                    assert ev.n == ev_sched[("piece", p)]
            stb = psSt if sgp % 2 == 0 else psmisc
            y1b = psY1[sgp % 2]
            y2b = L["psY2"][sgp % 2] if isB else None
            v.wait_ge(mm.h, group_mm[sgp])
            h0 = sgp * 8
            if isB:
                in2 = col(EXPC)[:, h0:h0 + 8].unsqueeze(2).broadcast_to([128, 8, 64])
                v.tensor_tensor(L["ytok"][:, h0 * 64:(h0 + 8) * 64].rearrange("p (a b) -> p a b", b=64),
                                y2b[:, :].rearrange("p (a b) -> p a b", b=64), in2, ALU.mult)
                v.drain()
                v.tensor_tensor(L["ytok"][:, h0 * 64:(h0 + 8) * 64],
                                L["ytok"][:, h0 * 64:(h0 + 8) * 64], y1b[:, :], ALU.add)
            in2b = col(EXPS)[:, h0:h0 + 8].unsqueeze(2).broadcast_to([128, 8, 64])
            v.drain()
            v.tensor_tensor(run[:, h0 * 64:(h0 + 8) * 64].rearrange("p (a b) -> p a b", b=64),
                            run[:, h0 * 64:(h0 + 8) * 64].rearrange("p (a b) -> p a b", b=64), in2b, ALU.mult)
            v.drain()
            tt = v.tensor_tensor(run[:, h0 * 64:(h0 + 8) * 64], run[:, h0 * 64:(h0 + 8) * 64], stb[:, :], ALU.add)
            ev.inc(tt, 1)
            assert ev.n == ev_sched[("grp", sgp)]
    blk.vector(v7body)
    mk[(tag, "headev", ch)] = ev.n
    if isB:
        mk[(tag, "Mdone", ch)] = mk[(tag, "headmm", ch)]
    ev_heads = ev.n

    if isB:
        def rqbody(s):
            for q in range(4):
                if q < 2:
                    s.wait_ge(sem_rwh.h, mk[(tag, "rwh", ch)])
                    s.wait_ge(sem_rwl.h, mk[(tag, "rwl", ch)])
                    if ch >= 1:
                        s.wait_ge(mm.h, mk[(tag, "Mdone", ch - 1)])
                else:
                    s.wait_ge(mm.h, piece_mm[8 * (q - 2) + 7])
                rb_ = L["rows"][q % 2]
                mh = dslot(s, sem_rrh, rb_[0:1, 0:4096],
                           c.sc_rows[0][32 * q:32 * q + 32, :].rearrange("a b -> (a b)").unsqueeze(0))
                ml = dslot(s, sem_rrl, rb_[1:2, 0:4096],
                           c.sc_rows[1][32 * q:32 * q + 32, :].rearrange("a b -> (a b)").unsqueeze(0))
                assert (mh, ml) == rows_mark[q]
        blk.sync(rqbody)

    if isB:
        sc2F0 = sc2.n

        def s6body(se):
            se.wait_ge(ev.h, ev_heads)
            a1 = se.activation(L["stbf"][:, :], L["ytok"][:, :], AF.Square, accum_out=L["sumsq"][:, :])
            a1.then_inc(sc2.h, 1)
            se.wait_ge(sc2.h, sc2F0 + 1)
            a2 = se.activation(L["rstd"][:, :], L["sumsq"][:, :], AF.Ln, scale=1.0 / INTER, bias=c.k_eps[:, :])
            a2.then_inc(sc2.h, 1)
            se.wait_ge(sc2.h, sc2F0 + 2)
            a3 = se.activation(L["rstd"][:, :], L["rstd"][:, :], AF.Exp, scale=-0.5)
            a3.then_inc(sc2.h, 1)
        sc2.n += 3
        blk.scalar(s6body)
        sc2F = sc2.n

        def v8body(v):
            v.wait_ge(sc2.h, sc2F)
            v.wait_ge(sem_g.h, rl["g"])
            v.wait_ge(sem_g2.h, rl["g2"])
            v.tensor_scalar(L["ytok"][:, :], L["ytok"][:, :], L["rstd"][:, :], None, ALU.mult)
            v.drain()
            tt = v.tensor_tensor(L["ytok"][:, :], L["ytok"][:, :], L["sg"][:, :], ALU.mult)
            ev.inc(tt, 1)
        blk.vector(v8body)
        mk[(tag, "zdone", ch)] = ev.n
        mk[(tag, "evz", ch)] = ev.n

        if c.debug and ch == 0:
            evn = ev.n

            def dbody(s):
                s.wait_ge(ev.h, evn)
                sd.dma(s.dma_start(out=c.dbg["y0"][:, :], in_=L["ytok"][:, :]))
                sd.dma(s.dma_start(out=c.dbg["cum0"][:, :], in_=col(CUMT)))
                sd.dma(s.dma_start(out=c.dbg["sg0"][:, :], in_=L["sg"][:, :]))
                s.wait_ge(sd.h, sd.n)
            blk.sync(dbody)


def _p_scalars(c, end_barrier=True):
    nc, blk = c.nc, c.blk
    mm, ev, sc2 = c.mm, c.ev, c.sc2
    bw = c.DS("psc_bw")
    br = c.DS("psc_br")
    sd = c.DS("dbg", serial=False)

    with contextlib.ExitStack() as es:
        tmpb = es.enter_context(nc.sbuf_tensor("psc_tb", [128, 24], BF16))
        tmpf = es.enter_context(nc.sbuf_tensor("psc_tf", [128, 16], F32))
        srow = es.enter_context(nc.sbuf_tensor("psc_sr", [1, 32], F32))
        lg = es.enter_context(nc.sbuf_tensor("psc_lg", [1, 32], F32))
        t2 = es.enter_context(nc.sbuf_tensor("psc_t2", [8, 4], F32))
        ps = es.enter_context(nc.psum_tensor("psc_ps", [128, 32], F32))

        ev0 = ev.n

        dwn = c.dw.n

        def vbody(v):
            v.wait_ge(c.dw.h, dwn)
            v.memset(lg[0:1, :], 0.0)
            v.memset(srow[0:1, :], 0.0)
            v.tensor_tensor(tmpf[:, 0:8], c.absA[:, :], c.k_vism[:, :], ALU.mult)
            v.drain()
            v.tensor_tensor(tmpf[:, 8:16], c.absA[:, :], tmpf[:, 0:8], ALU.subtract)
            v.tensor_copy(tmpb[:, 0:8], tmpf[:, 0:8])
            v.drain()
            v.tensor_copy(tmpb[:, 8:16], tmpf[:, 8:16])
            cp = v.tensor_copy(tmpb[:, 16:24], c.absB[:, :])
            ev.inc(cp, 1)
        blk.vector(vbody)
        evv = ev.n
        mm0 = mm.n

        def tbody(t):
            t.wait_ge(ev.h, evv)
            for k in range(KD):
                mx = t.matmul(ps[0:1, 4:7], c.xn0[:, k:k + 1], c.k_gwm[:, k * 3:(k + 1) * 3],
                              start=(k == 0), stop=(k == KD - 1))
            mm.inc(mx, 1)
            m1 = t.matmul(ps[0:8, 0:1], tmpb[:, 0:8], c.k_onescb[:, :], start=True, stop=True,
                          skip_group_check=True)
            mm.inc(m1, 1)
            m2 = t.matmul(ps[0:8, 1:2], tmpb[:, 8:16], c.k_onescb[:, :], start=True, stop=True,
                          skip_group_check=True)
            mm.inc(m2, 1)
            m3 = t.matmul(ps[0:8, 2:3], tmpb[:, 16:24], c.k_onescb[:, :], start=True, stop=True,
                          skip_group_check=True)
            mm.inc(m3, 1)
        blk.tensor(tbody)
        mmv = mm.n

        def v2body(v):
            v.wait_ge(mm.h, mmv)
            v.memset(ps[0:8, 3:4], 0.0)
            v.drain()
            cp = v.tensor_copy(t2[:, :], ps[0:8, 0:4])
            ev.inc(cp, 1)
        blk.vector(v2body)
        evb = ev.n

        def sbody(s):
            s.wait_ge(ev.h, evb)
            mw = dslot(s, bw, c.sc_srow, t2[:, :])
            s.wait_ge(bw.h, mw)
            dslot(s, br, srow[0:1, :], c.sc_srow.rearrange("a b -> (a b)").unsqueeze(0))
        blk.sync(sbody)
        brn = br.n

        def v3body(v):
            v.wait_ge(br.h, brn)
            rv2 = srow[0:1, :].rearrange("a (b cd) -> a cd b", cd=4)
            v.tensor_reduce(lg[0:1, 16:20], rv2, AX.X, ALU.add)
            v.drain()
            v.tensor_scalar(lg[0:1, 20:21], lg[0:1, 16:17], 1.0 / (614 * DM), None, ALU.mult)
            v.tensor_tensor(lg[0:1, 21:22], lg[0:1, 17:18], lg[0:1, 18:19], ALU.add)
            v.drain()
            v.tensor_scalar(lg[0:1, 21:22], lg[0:1, 21:22], 1.0 / (1434 * DM), None, ALU.mult)
            v.tensor_scalar(lg[0:1, 20:21], lg[0:1, 20:21], 1e-8, None, ALU.add)
            v.drain()
            v.reciprocal(lg[0:1, 22:23], lg[0:1, 20:21])
            v.drain()
            rr = v.tensor_tensor(c.ratio[0:1, :], lg[0:1, 21:22], lg[0:1, 22:23], ALU.mult)
            ev.inc(rr, 1)
            v.tensor_copy(lg[0:1, 0:3], ps[0:1, 4:7])
            v.drain()
            v.tensor_scalar(lg[0:1, 4:7], c.k_gwr[0:1, :], c.ratio[0:1, 0:1], None, ALU.mult)
            v.drain()
            v.tensor_tensor(lg[0:1, 0:3], lg[0:1, 0:3], lg[0:1, 4:7], ALU.add)
            v.drain()
            v.tensor_tensor(lg[0:1, 0:3], lg[0:1, 0:3], c.k_gbr[0:1, :], ALU.add)
            v.drain()
            v.tensor_reduce(lg[0:1, 8:9], lg[0:1, 0:3], AX.X, ALU.max)
            v.drain()
            ts = v.tensor_scalar(lg[0:1, 0:3], lg[0:1, 0:3], lg[0:1, 8:9], None, ALU.subtract)
            ev.inc(ts, 1)
        blk.vector(v3body)
        evl = ev.n

        def s8body(se):
            se.wait_ge(ev.h, evl)
            a = se.activation(lg[0:1, 0:3], lg[0:1, 0:3], AF.Exp)
            a.then_inc(sc2.h, 1)
        sc2.n += 1
        blk.scalar(s8body)
        scl = sc2.n

        def v4body(v):
            v.wait_ge(sc2.h, scl)
            v.tensor_reduce(lg[0:1, 9:10], lg[0:1, 0:3], AX.X, ALU.add)
            v.drain()
            v.reciprocal(lg[0:1, 10:11], lg[0:1, 9:10])
            v.drain()
            v.tensor_scalar(lg[0:1, 0:3], lg[0:1, 0:3], lg[0:1, 10:11], None, ALU.mult)
            v.drain()
            v.tensor_reduce(lg[0:1, 11:12], lg[0:1, 0:3], AX.X, ALU.max)
            v.tensor_reduce(lg[0:1, 12:13], lg[0:1, 0:3], AX.X, ALU.min)
            v.drain()
            v.tensor_scalar(lg[0:1, 4:7], lg[0:1, 0:3], lg[0:1, 11:12], None, ALU.is_equal)
            v.tensor_scalar(lg[0:1, 13:16], lg[0:1, 0:3], lg[0:1, 12:13], None, ALU.is_equal)
            v.drain()
            v.tensor_tensor(lg[0:1, 13:16], lg[0:1, 4:7], lg[0:1, 13:16], ALU.add)
            v.drain()
            i1 = v.tensor_scalar(lg[0:1, 13:16], lg[0:1, 13:16], -1.0, 1.0, ALU.mult, ALU.add)
            ev.inc(i1, 1)
        blk.vector(v4body)
        evi = ev.n
        mmi0 = mm.n

        def t2body(t):
            t.wait_ge(ev.h, evi)
            m1 = t.matmul(ps[:, 8:11], c.k_onesrf[0:1, 0:128], lg[0:1, 4:7], start=True, stop=True,
                          skip_group_check=True)
            mm.inc(m1, 1)
            m2 = t.matmul(ps[:, 12:15], c.k_onesrf[0:1, 0:128], lg[0:1, 13:16], start=True, stop=True,
                          skip_group_check=True)
            mm.inc(m2, 1)
        blk.tensor(t2body)
        mmi = mm.n

        def v5body(v):
            v.wait_ge(mm.h, mmi)
            v.tensor_copy(c.i0bc[:, :], ps[:, 8:11])
            cp = v.tensor_copy(c.i1bc[:, :], ps[:, 12:15])
            ev.inc(cp, 1)
        blk.vector(v5body)

        if c.debug:
            evn = ev.n

            def dbody(s):
                s.wait_ge(ev.h, evn)
                sd.dma(s.dma_start(out=c.dbg["scal"][0:1, 0:32], in_=lg[0:1, :]))
                sd.dma(s.dma_start(out=c.dbg["scal"][0:1, 32:33], in_=c.ratio[0:1, :]))
            blk.sync(dbody)
        if end_barrier:
            barrier(c)


def _p_gate(c, xnT):
    nc, blk = c.nc, c.blk
    mm, ev, sc2, fr = c.mm, c.ev, c.sc2, c.fr
    wsem = [c.DS(f"g_w{i}") for i in range(3)]
    gsem = [c.DS(f"g_s{i}") for i in range(8)]

    with contextlib.ExitStack() as es:
        wq = [es.enter_context(nc.sbuf_tensor(f"g_w{i}", [128, 4 * 512], BF16)) for i in range(3)]
        psg = [es.enter_context(nc.psum_tensor(f"g_ps{i}", [128, 512], F32)) for i in range(8)]
        sgt = [es.enter_context(nc.sbuf_tensor(f"g_s{i}", [128, 512], BF16)) for i in range(8)]

        fr0 = fr.n
        mm0 = mm.n
        sc20 = sc2.n
        wm = {}
        sgm = {i: gsem[i % 8].n + 16 * (i // 8 + 1) for i in range(128)}

        def gbody(g):
            i = 0
            for fs in range(16):
                for kq in range(8):
                    if i >= 3:
                        g.wait_ge(fr.h, fr0 + i - 2)
                    src = c.wgateT[kq * 512:(kq + 1) * 512, fs * 512:(fs + 1) * 512]
                    wm[i] = dslot(g, wsem[i % 3], wq[i % 3][:, :].rearrange("p (a b) -> p a b", b=512),
                                  src.rearrange("(a p) b -> p a b", p=128),
                                  max_dma_last_dim=2048)
                    i += 1
        blk.gpsimd(gbody)

        def tbody(t):
            i = 0
            for fs in range(16):
                for kq in range(8):
                    t.wait_ge(wsem[i % 3].h, wm[i])
                    for kj in range(4):
                        k = kq * 4 + kj
                        for tt in range(8):
                            if k == 0 and fs > 0:
                                t.wait_ge(sc2.h, sc20 + (fs - 1) * 8 + tt + 1)
                            last = t.matmul(psg[tt][:, :],
                                            xnT[:, :].rearrange("p (a b) -> p a b", b=TOK)[:, k, tt * 128:(tt + 1) * 128],
                                            wq[i % 3][:, kj * 512:(kj + 1) * 512],
                                            start=(k == 0), stop=(k == KD - 1))
                            if kj == 0 and tt == 0 and i >= 1:
                                fr.inc(last, 1)
                            if k == KD - 1:
                                mm.inc(last, 1)
                    i += 1
            t.drain()
            fr.inc(t.nop(), 1)
        blk.tensor(tbody)
        assert fr.n == fr0 + 128

        def scbody(se):
            for fs in range(16):
                for tt in range(8):
                    i = fs * 8 + tt
                    se.wait_ge(mm.h, mm0 + fs * 8 + tt + 1)
                    if i >= 8:
                        se.wait_ge(gsem[i % 8].h, sgm[i - 8])
                    a = se.activation(sgt[i % 8][:, :], psg[tt][:, :], AF.Silu)
                    a.then_inc(sc2.h, 1)
        sc2.n += 128
        blk.scalar(scbody)

        def sbody(s):
            for fs in range(16):
                for tt in range(8):
                    i = fs * 8 + tt
                    s.wait_ge(sc2.h, sc20 + i + 1)
                    m = dslot(s, gsem[i % 8], c.sc_sg[tt * 128:(tt + 1) * 128, fs * 512:(fs + 1) * 512],
                              sgt[i % 8][:, :])
                    assert m == sgm[i]
        blk.sync(sbody)
        barrier(c)


def _p_router(c, xnT):
    nc, blk = c.nc, c.blk
    mm, ev, sc2 = c.mm, c.ev, c.sc2
    sd = c.DS("dbg", serial=False)

    with contextlib.ExitStack() as es:
        rgb = es.enter_context(nc.sbuf_tensor("r_rgb", [1, 3], BF16))
        gbb = es.enter_context(nc.sbuf_tensor("r_gbb", [1, 3], BF16))
        sc3 = es.enter_context(nc.sbuf_tensor("r_sc", [128, 24 + 8 * 8], F32))
        rbuf = es.enter_context(nc.sbuf_tensor("r_rb", [24, 128], BF16))
        wcb = es.enter_context(nc.sbuf_tensor("r_wcb", [128, 24], BF16))
        psr = es.enter_context(nc.psum_tensor("r_psr", [128, 24], F32))
        pst = es.enter_context(nc.psum_tensor("r_pst", [128, 128], BF16))

        E3 = sc3[:, 0:24].rearrange("p (a b) -> p a b", b=3)
        col8 = lambda i: sc3[:, 24 + i * 8:24 + (i + 1) * 8]

        def vb0(v):
            v.tensor_scalar(rgb[0:1, :], c.k_gwr[0:1, :], c.ratio[0:1, 0:1], None, ALU.mult)
            cp = v.tensor_copy(gbb[0:1, :], c.k_gbr[0:1, :])
            ev.inc(cp, 1)
        blk.vector(vb0)
        evr = ev.n

        def tbody(t):
            t.wait_ge(ev.h, evr)
            for tt in range(8):
                for k in range(KD):
                    t.matmul(psr[:, tt * 3:tt * 3 + 3],
                             xnT[:, :].rearrange("p (a b) -> p a b", b=TOK)[:, k, tt * 128:(tt + 1) * 128],
                             c.k_gwm[:, k * 3:(k + 1) * 3],
                             start=(tt == 0 and k == 0), stop=False)
                t.matmul(psr[:, tt * 3:tt * 3 + 3], c.k_onesrb[0:1, 0:128], rgb[0:1, :],
                         start=False, stop=False)
                last = t.matmul(psr[:, tt * 3:tt * 3 + 3], c.k_onesrb[0:1, 0:128], gbb[0:1, :],
                                start=False, stop=(tt == 7))
            mm.inc(last, 1)
        blk.tensor(tbody)
        mmt = mm.n

        def v2body(v):
            v.wait_ge(mm.h, mmt)
            v.tensor_copy(sc3[:, 0:24], psr[:, 0:24])
            v.drain()
            v.tensor_reduce(col8(0).unsqueeze(2), E3, AX.X, ALU.max)
            v.drain()
            ts = v.tensor_tensor(E3, E3, col8(0).unsqueeze(2).broadcast_to([128, 8, 3]), ALU.subtract)
            ev.inc(ts, 1)
        blk.vector(v2body)
        evs = ev.n

        def sbody(se):
            se.wait_ge(ev.h, evs)
            a = se.activation(sc3[:, 0:24], sc3[:, 0:24], AF.Exp)
            a.then_inc(sc2.h, 1)
        sc2.n += 1
        blk.scalar(sbody)
        scs = sc2.n

        def v3body(v):
            v.wait_ge(sc2.h, scs)
            v.tensor_reduce(col8(1).unsqueeze(2), E3, AX.X, ALU.add)
            v.drain()
            v.reciprocal(col8(2), col8(1))
            v.drain()
            v.tensor_tensor(E3, E3, col8(2).unsqueeze(2).broadcast_to([128, 8, 3]), ALU.mult)
            v.drain()
            v.tensor_reduce(col8(3).unsqueeze(2), E3, AX.X, ALU.max)
            v.tensor_reduce(col8(4).unsqueeze(2), E3, AX.X, ALU.min)
            v.tensor_reduce(col8(5).unsqueeze(2), E3, AX.X, ALU.add)
            v.drain()
            v.tensor_tensor(col8(5), col8(5), col8(3), ALU.subtract)
            v.drain()
            v.tensor_tensor(col8(5), col8(5), col8(4), ALU.subtract)
            v.drain()
            v.tensor_tensor(col8(6), col8(3), col8(5), ALU.add)
            v.drain()
            v.tensor_scalar(col8(6), col8(6), 1e-8, None, ALU.add)
            v.drain()
            v.reciprocal(col8(7), col8(6))
            v.drain()
            v.tensor_tensor(col8(3), col8(3), col8(7), ALU.mult)
            v.tensor_tensor(col8(5), col8(5), col8(7), ALU.mult)
            v.drain()
            W3 = c.wcols[:, :].rearrange("p (a b) -> p a b", b=3)
            v.tensor_tensor(W3, col8(3).unsqueeze(2).broadcast_to([128, 8, 3]),
                            c.i0bc[:, :].unsqueeze(1).broadcast_to([128, 8, 3]), ALU.mult)
            v.tensor_tensor(E3, col8(5).unsqueeze(2).broadcast_to([128, 8, 3]),
                            c.i1bc[:, :].unsqueeze(1).broadcast_to([128, 8, 3]), ALU.mult)
            v.drain()
            v.tensor_tensor(c.wcols[:, :], c.wcols[:, :], sc3[:, 0:24], ALU.add)
            v.drain()
            cp = v.tensor_copy(wcb[:, :], c.wcols[:, :])
            ev.inc(cp, 1)
        blk.vector(v3body)
        evw = ev.n

        def tc(t):
            t.wait_ge(ev.h, evw)
            tr = t.transpose(pst[0:24, 0:128], wcb[:, :], c.k_identb[:, :])
            mm.inc(tr, 1)
        blk.tensor(tc)
        mmc = mm.n

        def vc2(v):
            v.wait_ge(mm.h, mmc)
            cp = v.tensor_copy(rbuf[0:24, :], pst[0:24, 0:128])
            ev.inc(cp, 1)
        blk.vector(vc2)
        evc = ev.n
        bw = c.DS("psc_bw")
        br = c.DS("psc_br")

        def sc_b(s):
            s.wait_ge(ev.h, evc)
            mwr = dslot(s, bw, c.sc_wr, rbuf[0:24, :])
            s.wait_ge(bw.h, mwr)
            src = c.sc_wr.rearrange("(t e) f -> e t f", e=3)
            for e in range(3):
                dslot(s, br, c.wrow_e[e][0:1, :].rearrange("a (t f) -> a t f", f=128),
                      src[e].unsqueeze(0))
        blk.sync(sc_b)

        if c.debug:
            evn = ev.n

            def dbody(s):
                s.wait_ge(ev.h, evn)
                sd.dma(s.dma_start(out=c.dbg["wcols"][:, :], in_=c.wcols[:, :]))
            blk.sync(dbody)
        barrier(c)


def _p_ffn(c, xnT):
    nc, blk = c.nc, c.blk
    mm, ev, sc2, fr, fr2 = c.mm, c.ev, c.sc2, c.fr, c.fr2
    w1s = [c.DS(f"f_w1{i}") for i in range(2)]
    w2s = [c.DS(f"f_w2{i}") for i in range(3)]
    bes = c.DS("f_be")
    prs = [c.DS(f"f_pv{i}") for i in range(4)]
    acs = [c.DS(f"f_ac{i}") for i in range(4)]
    sd = c.DS("dbg", serial=False)
    funcs = [AF.Tanh, AF.Gelu, AF.Silu]

    with contextlib.ExitStack() as es:
        w1b = [es.enter_context(nc.sbuf_tensor(f"f_w1{i}", [128, KD * 128], BF16)) for i in range(2)]
        w2b = [es.enter_context(nc.sbuf_tensor(f"f_w2{i}", [128, 4 * 512], BF16)) for i in range(3)]
        h1 = es.enter_context(nc.sbuf_tensor("f_h1", [128, KD * TOK], BF16))
        pso = [es.enter_context(nc.psum_tensor(f"f_ps{i}", [128, 512], F32)) for i in range(8)]
        acc = [es.enter_context(nc.sbuf_tensor(f"f_ac{i}", [128, 512], F32)) for i in range(4)]
        prev = [es.enter_context(nc.sbuf_tensor(f"f_pv{i}", [128, 512], F32)) for i in range(4)]
        be = es.enter_context(nc.sbuf_tensor("f_be", [1, DM], BF16))

        for e in range(3):
            bem = {}
            mmprev = mm.n
            evprev = ev.n

            def gb0(g, e=e):
                if e > 0:
                    g.wait_ge(mm.h, mmprev)
                bem["b1"] = dslot(g, bes, be[:, :], c.b1r[0:1, e * DM:(e + 1) * DM], max_dma_last_dim=8192)
            blk.gpsimd(gb0)
            fr0 = fr.n
            mm0 = mm.n
            sc20 = sc2.n
            w1m = c.__dict__.pop("_ffn_pf", {})

            def gbody(g, e=e):
                for ft in range(KD):
                    if ft in w1m:
                        continue
                    if ft >= 2:
                        g.wait_ge(fr.h, fr0 + ft - 1)
                    w1m[ft] = dslot(g, w1s[ft % 2], w1b[ft % 2][:, :].rearrange("p (a b) -> p a b", b=128),
                                    c.w1t[e, ft], max_dma_last_dim=8192)
            blk.gpsimd(gbody)

            def tbody(t, e=e):
                if e > 0:
                    t.wait_ge(ev.h, evprev - 7)
                for ft in range(KD):
                    t.wait_ge(w1s[ft % 2].h, w1m[ft])
                    for tb in range(2):
                        i = 2 * ft + tb
                        if i >= 2:
                            t.wait_ge(sc2.h, sc20 + i - 1)
                        elif i == 1 and e > 0:
                            t.wait_ge(ev.h, evprev - 6)
                        for k in range(KD):
                            fm = t.matmul(pso[i % 2][:, :], w1b[ft % 2][:, k * 128:(k + 1) * 128],
                                          xnT[:, :].rearrange("p (a b) -> p a b", b=TOK)[:, k, tb * 512:(tb + 1) * 512],
                                          start=(k == 0), stop=False)
                            if k == 0 and tb == 0 and ft >= 1:
                                fr.inc(fm, 1)
                        if ft == 0 and tb == 0:
                            t.wait_ge(bes.h, bem["b1"])
                        last = t.matmul(pso[i % 2][:, :],
                                        be[0:1, ft * 128:(ft + 1) * 128],
                                        c.k_onesrb[0:1, 0:512], start=False, stop=True)
                        mm.inc(last, 1)
                t.drain()
                fr.inc(t.nop(), 1)
            blk.tensor(tbody)
            assert fr.n == fr0 + KD

            def scbody(se, e=e):
                if e > 0:
                    se.wait_ge(mm.h, mmprev)
                for ft in range(KD):
                    for tb in range(2):
                        i = 2 * ft + tb
                        se.wait_ge(mm.h, mm0 + i + 1)
                        a = se.activation(h1[:, :].rearrange("p (a b) -> p a b", b=TOK)[:, ft, tb * 512:(tb + 1) * 512],
                                          pso[i % 2][:, :], funcs[e])
                        a.then_inc(sc2.h, 1)
            sc2.n += 2 * KD
            blk.scalar(scbody)
            sc2h = sc2.n

            if e < 2:
                pf = {}

                def spf(g, e=e):
                    g.wait_ge(fr.h, fr0 + KD)
                    pf[0] = dslot(g, w1s[0], w1b[0][:, :].rearrange("p (a b) -> p a b", b=128),
                                  c.w1t[e + 1, 0], max_dma_last_dim=8192)
                    pf[1] = dslot(g, w1s[1], w1b[1][:, :].rearrange("p (a b) -> p a b", b=128),
                                  c.w1t[e + 1, 1], max_dma_last_dim=8192)
                blk.gpsimd(spf)
                c._ffn_pf = pf

            def gb2(g, e=e):
                g.wait_ge(sc2.h, sc2h)
                bem["b2"] = dslot(g, bes, be[:, :], c.b2r[0:1, e * DM:(e + 1) * DM], max_dma_last_dim=8192)
            blk.gpsimd(gb2)

            fr20 = fr2.n
            mm20 = mm.n
            ev20 = ev.n
            w2m = {}

            def g2body(g, e=e):
                i = 0
                for ds in range(8):
                    for kq in range(8):
                        if i >= 3:
                            g.wait_ge(fr2.h, fr20 + i - 2)
                        src = c.w2[e, kq * 512:(kq + 1) * 512, ds * 512:(ds + 1) * 512]
                        w2m[i] = dslot(g, w2s[i % 3], w2b[i % 3][:, :].rearrange("p (a b) -> p a b", b=512),
                                       src.rearrange("(a p) b -> p a b", p=128),
                                       max_dma_last_dim=2048)
                        i += 1
            blk.gpsimd(g2body)

            def t2body(t, e=e):
                t.wait_ge(sc2.h, sc2h)
                if e > 0:
                    t.wait_ge(ev.h, evprev)
                i = 0
                for ds in range(8):
                    for kq in range(8):
                        t.wait_ge(w2s[i % 3].h, w2m[i])
                        for kj in range(4):
                            kf = kq * 4 + kj
                            for tt in range(8):
                                if kf == 0 and ds > 0:
                                    t.wait_ge(ev.h, ev20 + (ds - 1) * 8 + tt + 1)
                                fm = t.matmul(pso[tt][:, :],
                                              h1[:, :].rearrange("p (a b) -> p a b", b=TOK)[:, kf, tt * 128:(tt + 1) * 128],
                                              w2b[i % 3][:, kj * 512:(kj + 1) * 512],
                                              start=(kf == 0), stop=False)
                                if kj == 0 and tt == 0 and i >= 1:
                                    fr2.inc(fm, 1)
                        i += 1
                    if ds == 0:
                        t.wait_ge(bes.h, bem["b2"])
                    for tt in range(8):
                        last = t.matmul(pso[tt][:, :], c.wrow_e[e][0:1, tt * 128:(tt + 1) * 128],
                                        be[0:1, ds * 512:(ds + 1) * 512],
                                        start=False, stop=True)
                        mm.inc(last, 1)
                t.drain()
                fr2.inc(t.nop(), 1)
            blk.tensor(t2body)
            assert fr2.n == fr20 + 64

            sc2g0 = sc2.n
            sp_prev = c.__dict__.get("_ffn_sp_prev")
            rel_mark = {i: prs[i % 4].n + 16 * (i // 4 + 1) for i in range(64)} if e > 0 else {}
            sp_mark = {j: acs[j % 4].n + 16 * (j // 4 + 1) for j in range(64)}

            def v2body(v, e=e):
                for ds in range(8):
                    for tt in range(8):
                        i = ds * 8 + tt
                        v.wait_ge(mm.h, mm20 + ds * 8 + tt + 1)
                        if i >= 4:
                            v.wait_ge(acs[i % 4].h, sp_mark[i - 4])
                        elif e > 0:
                            v.wait_ge(acs[i % 4].h, sp_prev[60 + i])
                        wcol = c.wcols[:, tt * 3 + e:tt * 3 + e + 1]
                        if e == 0:
                            ts = v.tensor_scalar(acc[i % 4][:, :], pso[tt][:, :], wcol, None, ALU.mult)
                            ev.inc(ts, 1)
                        else:
                            v.wait_ge(prs[i % 4].h, rel_mark[i])
                            v.tensor_scalar(acc[i % 4][:, :], pso[tt][:, :], wcol, None, ALU.mult)
                            v.drain()
                            tt2 = v.tensor_tensor(acc[i % 4][:, :], acc[i % 4][:, :], prev[i % 4][:, :], ALU.add)
                            ev.inc(tt2, 1)
            blk.vector(v2body)

            if e == 2:
                def s3body(se, e=e):
                    for i in range(64):
                        se.wait_ge(ev.h, ev20 + i + 1)
                        a = se.activation(acc[i % 4][:, :], acc[i % 4][:, :], AF.Silu)
                        a.then_inc(sc2.h, 1)
                sc2.n += 64
                blk.scalar(s3body)

            def s2body(s, e=e):
                for i in range(66):
                    if i < 64 and e > 0:
                        ds, tt = i // 8, i % 8
                        if i >= 4:
                            s.wait_ge(ev.h, ev20 + i - 3)
                        else:
                            s.wait_ge(ev.h, evprev)
                        s.wait_ge(acs[i % 4].h, sp_prev[i])
                        m = dslot(s, prs[i % 4], prev[i % 4][:, :],
                                  c.sc_moe[tt * 128:(tt + 1) * 128, ds * 512:(ds + 1) * 512])
                        assert m == rel_mark[i]
                    j = i - 2
                    if 0 <= j < 64:
                        ds, tt = j // 8, j % 8
                        if e == 2:
                            s.wait_ge(sc2.h, sc2g0 + j + 1)
                            dst = c.sc_g[tt * 128:(tt + 1) * 128, ds * 512:(ds + 1) * 512]
                        else:
                            s.wait_ge(ev.h, ev20 + j + 1)
                            dst = c.sc_moe[tt * 128:(tt + 1) * 128, ds * 512:(ds + 1) * 512]
                        m = dslot(s, acs[j % 4], dst, acc[j % 4][:, :])
                        assert m == sp_mark[j]
            blk.sync(s2body)
            c._ffn_sp_prev = sp_mark
        barrier(c)

        if c.debug:
            def dbody(s):
                sd.dma(s.dma_start(out=c.dbg["g0"][:, :], in_=c.sc_g[0:128, :]))
            blk.sync(dbody)
            barrier(c)


def _p_outproj(c):
    nc, blk = c.nc, c.blk
    mm, ev, fr = c.mm, c.ev, c.fr
    zls = [c.DS(f"o_zl{i}", serial=False) for i in range(4)]
    wsem = [c.DS(f"o_w{i}") for i in range(3)]
    gls = [c.DS(f"o_g{i}") for i in range(4)]
    xls = [c.DS(f"o_x{i}") for i in range(4)]
    sts = [c.DS(f"st_o{i}") for i in range(4)]

    with contextlib.ExitStack() as es:
        zall = es.enter_context(nc.sbuf_tensor("o_z", [128, NCH * FT_X * 128], BF16))
        wb = [es.enter_context(nc.sbuf_tensor(f"o_w{i}", [128, 4 * 512], BF16)) for i in range(3)]
        pso = [es.enter_context(nc.psum_tensor(f"o_ps{i}", [128, 512], F32)) for i in range(8)]
        gld = [es.enter_context(nc.sbuf_tensor(f"o_g{i}", [128, 512], F32)) for i in range(4)]
        xld = [es.enter_context(nc.sbuf_tensor(f"o_x{i}", [128, 512], F32)) for i in range(4)]
        ob = [es.enter_context(nc.sbuf_tensor(f"o_o{i}", [128, 512], F32)) for i in range(4)]

        def sbody(s):
            for chk in range(NCH):
                zls[chk % 4].dma(s.dma_start(out=zall[:, :].rearrange("p (a b) -> p a b", b=FT_X * 128)[:, chk, :],
                                             in_=c.sc_z[chk]))
        blk.sync(sbody)
        zl_ns = [sem.n for sem in zls]

        fr0 = fr.n
        wm = {}

        def gbody(g):
            i = 0
            for ds in range(8):
                for kq in range(16):
                    if i >= 3:
                        g.wait_ge(fr.h, fr0 + i - 2)
                    src = c.woutT[kq * 512:(kq + 1) * 512, ds * 512:(ds + 1) * 512]
                    wm[i] = dslot(g, wsem[i % 3], wb[i % 3][:, :].rearrange("p (a b) -> p a b", b=512),
                                  src.rearrange("(a p) b -> p a b", p=128),
                                  max_dma_last_dim=2048)
                    i += 1
        blk.gpsimd(gbody)

        mm0 = mm.n
        ev0 = ev.n

        def tbody(t):
            for sem, n in zip(zls, zl_ns):
                t.wait_ge(sem.h, n)
            i = 0
            for ds in range(8):
                for kq in range(16):
                    t.wait_ge(wsem[i % 3].h, wm[i])
                    for kj in range(4):
                        kf = kq * 4 + kj
                        for chk in range(8):
                            if kf == 0 and ds > 0:
                                t.wait_ge(ev.h, ev0 + (ds - 1) * 8 + chk + 1)
                            last = t.matmul(pso[chk][:, :],
                                            zall[:, :].rearrange("p (a b) -> p a b", b=128)[:, chk * FT_X + kf, :],
                                            wb[i % 3][:, kj * 512:(kj + 1) * 512],
                                            start=(kf == 0), stop=(kf == FT_X - 1))
                            if kj == 0 and chk == 0 and i >= 1:
                                fr.inc(last, 1)
                            if kf == FT_X - 1:
                                mm.inc(last, 1)
                    i += 1
            t.drain()
            fr.inc(t.nop(), 1)
        blk.tensor(tbody)
        assert fr.n == fr0 + 128

        gld_mark = {i: gls[i % 4].n + 16 * (i // 4 + 1) for i in range(64)}
        xld_mark = {i: xls[i % 4].n + 16 * (i // 4 + 1) for i in range(64)}
        st_mark = {j: sts[j % 4].n + 16 * (j // 4 + 1) for j in range(64)}

        def s23body(s):
            for i in range(66):
                if i < 64:
                    ds, chk = i // 8, i % 8
                    if i >= 4:
                        s.wait_ge(ev.h, ev0 + i - 3)
                    mg = dslot(s, gls[i % 4], gld[i % 4][:, :],
                               c.sc_g[chk * 128:(chk + 1) * 128, ds * 512:(ds + 1) * 512])
                    mx = dslot(s, xls[i % 4], xld[i % 4][:, :],
                               c.x_own.rearrange("(a p) b -> a p b", p=128)[chk][:, ds * 512:(ds + 1) * 512])
                    assert (mg, mx) == (gld_mark[i], xld_mark[i])
                j = i - 2
                if 0 <= j < 64:
                    ds, chk = j // 8, j % 8
                    s.wait_ge(ev.h, ev0 + j + 1)
                    ms = dslot(s, sts[j % 4],
                               c.out.rearrange("(a p) b -> a p b", p=128)[chk][:, ds * 512:(ds + 1) * 512],
                               ob[j % 4][:, :])
                    assert ms == st_mark[j]
        blk.sync(s23body)

        def vbody(v):
            for ds in range(8):
                for chk in range(8):
                    i = ds * 8 + chk
                    v.wait_ge(mm.h, mm0 + i + 1)
                    v.wait_ge(gls[i % 4].h, gld_mark[i])
                    v.wait_ge(xls[i % 4].h, xld_mark[i])
                    if i >= 4:
                        v.wait_ge(sts[i % 4].h, st_mark[i - 4])
                    v.tensor_tensor(ob[i % 4][:, :], pso[chk][:, :], gld[i % 4][:, :], ALU.mult)
                    v.drain()
                    tt = v.tensor_tensor(ob[i % 4][:, :], ob[i % 4][:, :], xld[i % 4][:, :], ALU.add)
                    ev.inc(tt, 1)
        blk.vector(vbody)


def _prep_inputs(inputs):
    ipw = np.ascontiguousarray(inputs["L0_attn_in_proj_w"], np.float32)
    wgateT = np.ascontiguousarray(ipw[0:INTER].T)
    whbcT = np.ascontiguousarray(ipw[INTER:INTER + CONVD + 128].T)
    whbc = np.ascontiguousarray(whbcT.reshape(KD, 128, FT_ALL, 128).transpose(2, 1, 0, 3))
    woutT = np.ascontiguousarray(np.asarray(inputs["L0_attn_out_proj_w"], np.float32).T)
    w1 = np.ascontiguousarray(inputs["L0_ffn_w1"], np.float32)
    w1t = np.ascontiguousarray(w1.reshape(3, KD, 128, KD, 128).transpose(0, 3, 2, 1, 4))
    w2 = np.ascontiguousarray(inputs["L0_ffn_w2"], np.float32)
    b1r = np.ascontiguousarray(np.asarray(inputs["L0_ffn_b1"], np.float32).reshape(1, 3 * DM))
    b2r = np.ascontiguousarray(np.asarray(inputs["L0_ffn_b2"], np.float32).reshape(1, 3 * DM))
    gw = np.ascontiguousarray(inputs["L0_ffn_gate_w"], np.float32)
    gwm = np.ascontiguousarray(gw[0:DM].reshape(KD, 128, 3).transpose(1, 0, 2))
    gwr = np.ascontiguousarray(gw[DM:DM + 1])
    gbr = np.ascontiguousarray(np.asarray(inputs["L0_ffn_gate_b"], np.float32).reshape(1, 3))
    convw = np.ascontiguousarray(np.asarray(inputs["L0_attn_conv_w"], np.float32).reshape(FT_CONV, 128, 4).transpose(1, 0, 2))
    convb = np.ascontiguousarray(np.asarray(inputs["L0_attn_conv_b"], np.float32).reshape(FT_CONV, 128).T)
    dtb = np.ascontiguousarray(np.asarray(inputs["L0_attn_dt_bias"], np.float32).reshape(128, 1))
    alog = np.ascontiguousarray(np.asarray(inputs["L0_attn_A_log"], np.float32).reshape(128, 1))
    drow = np.ascontiguousarray(np.asarray(inputs["L0_attn_D"], np.float32).reshape(1, 128))
    nw = np.ascontiguousarray(np.asarray(inputs["L0_attn_norm_w"], np.float32).reshape(FT_X, 128).T)

    ident = np.eye(128, dtype=np.float32)
    i64x2 = np.concatenate([np.eye(64), np.eye(64)], axis=0).astype(np.float32)
    sl = np.arange(128)
    addmask = np.where(sl[:, None] <= sl[None, :], 0.0, NEG).astype(np.float32)
    ut = np.triu(np.ones((128, 128), np.float32))
    ones_r = np.ones((1, 512), np.float32)
    ones_c = np.ones((128, 1), np.float32)
    tokidx = (np.arange(8)[None, :] * 128 + np.arange(128)[:, None])
    vism = (tokidx < 614).astype(np.float32)

    return {
        "whbc": whbc, "wgateT": wgateT, "woutT": woutT, "w1t": w1t, "w2": w2,
        "b1r": b1r, "b2r": b2r, "gwm": gwm, "gwr": gwr, "gbr": gbr,
        "convw": convw, "convb": convb, "dtb": dtb, "alog": alog, "drow": drow,
        "nwcol": nw,
        "ident_b": ident.astype(BF), "ident_f": ident, "i64x2_b": i64x2.astype(BF),
        "ut_f": ut, "addmask": addmask.astype(BF),
        "ones_rb": ones_r.astype(BF), "ones_rf": ones_r, "ones_cb": ones_c.astype(BF),
        "ones2_b": np.ones((2, 128), BF),
        "vismask": vism,
        "epscol": np.full((128, 1), 1e-5, np.float32),
    }


def make_in_maps(inputs, shared=None):
    if shared is None:
        shared = _prep_inputs(inputs)
    x = np.ascontiguousarray(np.asarray(inputs["L0_attn_hidden_states"], np.float32))
    in_maps = []
    for core in range(8):
        b, h = core // 2, core % 2
        m = dict(shared)
        m["x_own"] = np.ascontiguousarray(x[b, h * TOK:(h + 1) * TOK])
        m["x_first"] = np.ascontiguousarray(x[b, 0:TOK])
        m["x_second"] = np.ascontiguousarray(x[b, TOK:2 * TOK])
        m["flag_col"] = np.full((128, 1), float(h), np.float32)
        in_maps.append(m)
    return in_maps


def kernel(**inputs):
    nc, _ = build_nc(stage=STAGE, debug=False)
    in_maps = make_in_maps(inputs)
    res = run_bass_kernel_spmd(nc, in_maps, core_ids=list(range(8)))
    out = np.empty((4, 2048, DM), np.float32)
    for core in range(8):
        b, h = core // 2, core % 2
        out[b, h * TOK:(h + 1) * TOK] = res.results[core]["out"]
    return out
